# version 07
# baseline (speedup 1.0000x reference)
import numpy as np
import ml_dtypes
import concourse.bass as bass
import concourse.mybir as mybir
from concourse.bass_utils import run_bass_kernel_spmd

F32 = mybir.dt.float32
BF16 = mybir.dt.bfloat16
AF = mybir.ActivationFunctionType
ALU = mybir.AluOpType
AX = mybir.AxisListType

MAGIC = float(np.float32(3 << 22))
NEG = -1.0e30
C_ATTN = float(128.0 ** -0.5)

NF4 = np.array([-1.0, -0.6962, -0.5251, -0.3949, -0.2844, -0.1848, -0.0911,
                0.0, 0.0796, 0.1609, 0.2461, 0.3379, 0.4407, 0.5626, 0.723,
                1.0], dtype=ml_dtypes.bfloat16).astype(np.float64)
NF4_MID = ((NF4[:-1] + NF4[1:]) / 2.0).astype(np.float32)
NF4_DELTA = np.diff(NF4).astype(np.float32)
NF4_BASE = float(NF4[0])
NBLK_REAL = 5632 * 2048 // 64

ENGINES = ("sync", "gpsimd", "tensor", "vector", "scalar")

SM_MEAN0, SM_RS0, SM_C1X, SM_ISX = 0, 8, 16, 24
SM_MEAN1, SM_RS1 = 48, 56
SM_G = 64
SM_THR = 72
SM_NTHR = 81
SM_NF4M, SM_NF4D = 96, 112
SM_BQ = 128
SM_ONE = 176
SM_T = 178
SM_SCG, SM_SCU, SM_C3, SM_VAR = 224, 232, 240, 248
SM_ROW = 256
SM_AT = 384
SM_SB1, SM_SB2, SM_QT = 1024, 1280, 1536



def _tl(i):
    if i == 0:
        return list(range(16))
    lo = [t for t in (i - 1, i, i + 1) if 0 <= t <= 7]
    return sorted(set([0] + lo + [8 + t for t in lo]))


TLS = [_tl(i) for i in range(8)]


def _segs(tl):
    runs = []
    for s, t in enumerate(tl):
        if runs and runs[-1][0] + runs[-1][2] == s \
                and runs[-1][1] + runs[-1][2] == t:
            runs[-1][2] += 1
        else:
            runs.append([s, t, 1])
    segs = []
    for s0, t0, L in runs:
        s = s0
        while s < s0 + L:
            e = min(s0 + L, (s // 4 + 1) * 4)
            segs.append((s, t0 + (s - s0), e - s))
            s = e
    return segs

AA_COLS = 50176
WB_COLS = 12288


class Prog:
    def __init__(self):
        self.recs = {e: [] for e in ENGINES}
        self.cnt = {}

    def emit(self, eng, fn, waits=(), inc=None, n=1):
        self.recs[eng].append((fn, tuple(waits), inc, n))
        if inc is not None:
            self.cnt[inc] = self.cnt.get(inc, 0) + n
        return self.cnt.get(inc)


class Res:
    def __init__(self):
        self.w = {}
        self.r = {}

    def rd(self, name):
        return [self.w[name]] if name in self.w else []

    def wr(self, name):
        ws = list(self.r.get(name, []))
        if name in self.w:
            ws.append(self.w[name])
        return ws

    def did_rd(self, name, sem, cnt):
        self.r.setdefault(name, []).append((sem, cnt))

    def did_wr(self, name, sem, cnt):
        self.w[name] = (sem, cnt)
        self.r[name] = []


def merge_waits(*wls):
    best = {}
    for wl in wls:
        for s, v in wl:
            if v > best.get(s, -1):
                best[s] = v
    return list(best.items())


def build_program():
    nc = bass.Bass("TRN2", target_bir_lowering=False, debug=False)
    p = Prog()
    res = Res()

    x_in = nc.dram_tensor("x_in", [1024, 2048], F32, kind="ExternalInput").ap()
    wq_d = nc.dram_tensor("wq", [2048, 2048], F32, kind="ExternalInput").ap()
    wk_d = nc.dram_tensor("wk", [2048, 2048], F32, kind="ExternalInput").ap()
    wv_d = nc.dram_tensor("wv", [2048, 2048], F32, kind="ExternalInput").ap()
    wo_d = nc.dram_tensor("wo", [2048, 2048], F32, kind="ExternalInput").ap()
    wqkv_d = nc.dram_tensor("wqkv", [2048, 6144], F32, kind="ExternalInput").ap()
    wout_d = nc.dram_tensor("wout", [2048, 2048], F32, kind="ExternalInput").ap()
    bqkvT_d = nc.dram_tensor("bqkvT", [128, 48], F32, kind="ExternalInput").ap()
    bout_d = nc.dram_tensor("bout_bc", [128, 2048], BF16, kind="ExternalInput").ap()
    wg_sh = nc.dram_tensor("wg_sh", [1024, 2048], F32, kind="ExternalInput").ap()
    wu_sh = nc.dram_tensor("wu_sh", [1024, 2048], F32, kind="ExternalInput").ap()
    wd_sh = nc.dram_tensor("wd_sh", [2048, 1024], F32, kind="ExternalInput").ap()
    lnw_d = nc.dram_tensor("lnw_bc", [128, 8192], F32, kind="ExternalInput").ap()
    w1_sh = nc.dram_tensor("w1_sh", [768, 2048], BF16, kind="ExternalInput").ap()
    w2_sh = nc.dram_tensor("w2_sh", [768, 2048], BF16, kind="ExternalInput").ap()
    w3_sh = nc.dram_tensor("w3_sh", [256, 5632], BF16, kind="ExternalInput").ap()
    mask_d = nc.dram_tensor("mask_in", [8, 128, 2048], BF16, kind="ExternalInput").ap()
    ident_d = nc.dram_tensor("ident_in", [128, 128], BF16, kind="ExternalInput").ap()
    nf4c_d = nc.dram_tensor("nf4c_in", [128, 32], F32, kind="ExternalInput").ap()
    out_d = nc.dram_tensor("out", [1024, 2048], F32, kind="ExternalOutput").ap()

    def idram(name, shape, dtype=BF16):
        return nc.dram_tensor(name, shape, dtype).ap()

    ar_in = idram("ar_in", [128, 8], F32)
    ar_out = idram("ar_out", [128, 8], F32)
    s_rt = idram("s_rt", [3, 768, 32], F32)
    r_rt = idram("r_rt", [3, 768, 32], F32)
    d_rt = idram("d_rt", [3, 768, 32], BF16)
    def sdram(name, shape, dtype=BF16):
        return nc.dram_tensor(name, shape, dtype, addr_space="Shared").ap()

    wgT_in = idram("wgT_in", [2048, 1024]); wgT_ag = sdram("wgT_ag", [16384, 1024])
    wuT_in = idram("wuT_in", [2048, 1024]); wuT_ag = sdram("wuT_ag", [16384, 1024])
    wdT_in = idram("wdT_in", [1024, 2048]); wdT_ag = sdram("wdT_ag", [8192, 2048])
    w1T_in = idram("w1T_in", [2048, 768]); w1T_ag = sdram("w1T_ag", [16384, 768])
    w2T_in = idram("w2T_in", [2048, 768]); w2T_ag = sdram("w2T_ag", [16384, 768])
    w3T_in = idram("w3T_in", [5632, 256]); w3T_ag = sdram("w3T_ag", [45056, 256])
    kT_in = [idram("kT0_in", [2048, 1024]), idram("kT1_in", [2048, 1024])]
    kT_ag = [idram("kT0_ag", [4096, 1024]), idram("kT1_ag", [4096, 1024])]
    vT_in = [idram("vT0_in", [2048, 1024]), idram("vT1_in", [2048, 1024])]
    vT_ag = [idram("vT0_ag", [4096, 1024]), idram("vT1_ag", [4096, 1024])]
    qT_dr = [idram("qT0_dr", [2048, 1024]), idram("qT1_dr", [2048, 1024])]

    from contextlib import ExitStack
    ctx = ExitStack()
    E = ctx.enter_context
    x_sb = E(nc.sbuf_tensor("x_sb", [128, 16384], F32))
    AA = E(nc.sbuf_tensor("AA", [128, AA_COLS], BF16))
    WB = E(nc.sbuf_tensor("WB", [128, WB_COLS], BF16))
    SM = E(nc.sbuf_tensor("SM", [128, 2048], F32))
    FS = E(nc.sbuf_tensor("FS", [128, 2048], F32))
    IDT = E(nc.sbuf_tensor("IDT", [128, 128], BF16))
    ps_big = E(nc.psum_tensor("ps_big", [128, 2048], F32))
    ps_a = E(nc.psum_tensor("ps_a", [128, 1024], F32))
    ps_b = E(nc.psum_tensor("ps_b", [128, 1024], F32))
    sems = {}
    for s in ("cc", "pe", "ve", "se"):
        sems[s] = E(nc.semaphore(s))

    def getsem(name):
        if name not in sems:
            sems[name] = E(nc.semaphore(name))
        return sems[name]

    ENG_SEM = {"tensor": "pe", "vector": "ve", "scalar": "se",
               "gpsimd": "sw", "sync": "hw"}
    PAIRS = [[0, 1], [2, 3], [4, 5], [6, 7]]
    ALL8 = [list(range(8))]

    def aanames(c0, n):
        return tuple(f"aa{k}" for k in range(c0 // 4096, (c0 + n - 1) // 4096 + 1))

    def wbnames(c0, n):
        return tuple(f"wb{k}" for k in range(c0 // 1024, (c0 + n - 1) // 1024 + 1))

    def dma(engname, out_ap, in_ap, reads=(), writes=(), extra=()):
        assert writes, "dma needs a write-resource for its stream semaphore"
        semname = ("dg_" if engname == "gpsimd" else "d_") + writes[0]
        getsem(semname)
        waits = merge_waits(*([res.rd(r) for r in reads]
                              + [res.wr(w) for w in writes] + [list(extra)]))
        cnt = p.emit(engname,
                     (lambda o, i: lambda e: e.dma_start(out=o, in_=i))(out_ap, in_ap),
                     waits=waits, inc=semname, n=16)
        for r in reads:
            res.did_rd(r, semname, cnt)
        for w in writes:
            res.did_wr(w, semname, cnt)
        return cnt

    def op(engname, fn, reads=(), writes=(), extra=(), sem=None):
        semname = sem or ENG_SEM[engname]
        getsem(semname)
        waits = merge_waits(*([res.rd(r) for r in reads]
                              + [res.wr(w) for w in writes] + [list(extra)]))
        cnt = p.emit(engname, fn, waits=waits, inc=semname, n=1)
        for r in reads:
            res.did_rd(r, semname, cnt)
        for w in writes:
            res.did_wr(w, semname, cnt)
        return cnt

    def mmf(o, l, r, st, sp):
        return lambda e: e.matmul(o, l, r, start=st, stop=sp)

    def mm_group(items, reads=(), writes=(), extra=()):
        waits = merge_waits(*([res.rd(r) for r in reads]
                              + [res.wr(w) for w in writes] + [list(extra)]))
        for it in items[:-1]:
            p.emit("tensor", it, waits=waits)
            waits = ()
        cnt = p.emit("tensor", items[-1], waits=waits, inc="pe", n=1)
        for r in reads:
            res.did_rd(r, "pe", cnt)
        for w in writes:
            res.did_wr(w, "pe", cnt)
        return cnt

    def aav(c0, n):
        return AA[:, c0:c0 + n]

    def wbv(c0, n):
        return WB[:, c0:c0 + n]

    def smc(col, n=1):
        return SM[:, col:col + n]

    def ag(in_ap, out_ap, rname, wname, groups):
        op("gpsimd", (lambda i_, o_, g_: lambda e: e.collective_compute(
            "AllGather", ALU.bypass, replica_groups=g_,
            ins=[i_.opt()], outs=[o_.opt()]))(in_ap, out_ap, groups),
            reads=(rname,), writes=(wname,), sem="cc")

    dma("sync", x_sb[:].rearrange("p (m f) -> p m f", f=2048),
        x_in.rearrange("(m p) f -> p m f", p=128), writes=("x",))
    dma("sync", IDT[:], ident_d, writes=("idt",))
    dma("sync", SM[:, SM_NF4M:SM_NF4M + 32], nf4c_d, writes=("st",))
    dma("sync", SM[:, SM_BQ:SM_BQ + 48], bqkvT_d, writes=("st",))
    op("vector", lambda e: e.memset(smc(SM_ONE), 1.0), writes=("st",))
    op("vector", lambda e: e.memset(SM[0:1, SM_ROW:SM_ROW + 128], 1.0),
       writes=("st",))

    SCR = 36864

    def emit_ln(layer):
        mean_c = SM_MEAN0 if layer == 0 else SM_MEAN1
        rs_c = SM_RS0 if layer == 0 else SM_RS1
        for m in range(8):
            xt = x_sb[:, 2048 * m:2048 * (m + 1)]
            op("vector", (lambda xt_, c: lambda e: e.reduce_sum(
                smc(c), xt_, axis=AX.X))(xt, mean_c + m),
                reads=("x",), writes=("st",))
            op("vector", (lambda c: lambda e: e.tensor_scalar_mul(
                smc(c), smc(c), 1.0 / 2048.0))(mean_c + m), writes=("st",))
            op("scalar", (lambda xt_: lambda e: e.activation(
                aav(SCR, 2048), xt_, AF.Square, accum_out=smc(SM_T)))(xt),
                reads=("x",), writes=aanames(SCR, 2048) + ("st",))
            op("vector", (lambda c: lambda e: e.tensor_tensor(
                smc(SM_T + 1), smc(c), smc(c), ALU.mult))(mean_c + m),
                writes=("st",))
            op("vector", lambda e: e.tensor_scalar(
                smc(SM_T), smc(SM_T), 1.0 / 2048.0, 1e-5, ALU.mult, ALU.add),
                writes=("st",))
            op("vector", lambda e: e.tensor_sub(
                smc(SM_T), smc(SM_T), smc(SM_T + 1)), writes=("st",))
            op("scalar", lambda e: e.activation(smc(SM_T), smc(SM_T), AF.Ln),
               reads=("st",), writes=("st",))
            op("scalar", (lambda c: lambda e: e.activation(
                smc(c), smc(SM_T), AF.Exp, scale=-0.5))(rs_c + m),
                reads=("st",), writes=("st",))
            op("vector", (lambda xt_, mc, rc: lambda e: e.scalar_tensor_tensor(
                out=aav(SCR, 2048), in0=xt_, scalar=smc(mc),
                in1=smc(rc).broadcast_to([128, 2048]), op0=ALU.subtract,
                op1=ALU.mult))(xt, mean_c + m, rs_c + m),
                reads=("x", "st"), writes=aanames(SCR, 2048))
            if layer == 0:
                op("vector", lambda e: e.reduce_max(
                    smc(SM_T), aav(SCR, 2048), axis=AX.X,
                    apply_absolute_value=True),
                    reads=aanames(SCR, 2048), writes=("st",))
                op("vector", lambda e: e.tensor_scalar_max(
                    smc(SM_T), smc(SM_T), 1e-5), writes=("st",))
                op("vector", (lambda mi: lambda e: e.tensor_scalar_mul(
                    smc(SM_ISX + mi), smc(SM_T), 1.0 / 127.0))(m),
                    writes=("st",))
                op("vector", lambda e: e.reciprocal(
                    smc(SM_T + 1), smc(SM_T)), writes=("st",))
                op("vector", (lambda mi: lambda e: e.scalar_tensor_tensor(
                    out=smc(SM_C1X + mi), in0=smc(SM_T + 1), scalar=127.0,
                    in1=smc(SM_RS0 + mi), op0=ALU.mult, op1=ALU.mult))(m),
                    writes=("st",))
            items = [mmf(ps_big[:, 128 * ft:128 * (ft + 1)],
                         aav(SCR + 128 * ft, 128), IDT[:], True, True)
                     for ft in range(16)]
            mm_group(items, reads=aanames(SCR, 2048) + ("idt",),
                     writes=("ps_big",))
            op("vector", (lambda mi: lambda e: e.tensor_copy(
                AA[:, 0:16384].rearrange("p (t q) -> p t q", q=1024)
                [:, :, 128 * mi:128 * (mi + 1)],
                ps_big[:].rearrange("p (t q) -> p t q", q=128)))(m),
                reads=("ps_big",), writes=aanames(0, 16384))

    emit_ln(0)

    for colz in range(8):
        op("vector", (lambda c: lambda e: e.memset(smc(SM_G + c), 0.0))(colz),
           writes=("st",))

    def tern_pass1(dram, gcol, tiles, width):
        for t in range(tiles):
            dma("sync", FS[:, 0:width], dram[128 * t:128 * (t + 1), :],
                writes=("fs",))
            op("vector", (lambda w: lambda e: e.reduce_sum(
                smc(SM_T + 2), FS[:, 0:w], axis=AX.X,
                apply_absolute_value=True))(width),
                reads=("fs",), writes=("st",))
            op("vector", (lambda c: lambda e: e.tensor_add(
                smc(SM_G + c), smc(SM_G + c), smc(SM_T + 2)))(gcol),
                writes=("st",))

    tern_pass1(wg_sh, 0, 8, 2048)
    tern_pass1(wu_sh, 1, 8, 2048)
    tern_pass1(wd_sh, 2, 16, 1024)

    NF4S = [(w1_sh, 0, 1024, 6, 2048, 3), (w2_sh, 1, 1216, 6, 2048, 4),
            (w3_sh, 2, 1408, 2, 5632, 5)]
    for dram, widx, scol, tiles, width, gcol in NF4S:
        nblk = width // 64
        for t in range(tiles):
            nsub = (width + 2047) // 2048
            for s in range(nsub):
                c0 = 2048 * s
                w = min(2048, width - c0)
                dma("sync", wbv(0, w), dram[128 * t:128 * (t + 1), c0:c0 + w],
                    writes=wbnames(0, w))
                op("vector", (lambda w_, sc: lambda e: e.reduce_max(
                    SM[:, sc:sc + w_ // 64].unsqueeze(2),
                    WB[:, 0:w_].rearrange("p (b i) -> p b i", i=64),
                    axis=AX.X, apply_absolute_value=True))(
                        w, scol + t * nblk + c0 // 64),
                    reads=wbnames(0, w), writes=("smsb",))
        op("vector", (lambda sc, n: lambda e: e.reduce_sum(
            smc(SM_T + 2), SM[:, sc:sc + n], axis=AX.X))(scol, tiles * nblk),
            reads=("smsb",), writes=("st",))
        op("vector", (lambda c: lambda e: e.tensor_add(
            smc(SM_G + c), smc(SM_G + c), smc(SM_T + 2)))(gcol),
            writes=("st",))

    dma("sync", ar_in, SM[:, SM_G:SM_G + 8], reads=("st",), writes=("ar_in",))
    op("gpsimd", lambda e: e.collective_compute(
        "AllReduce", ALU.add, replica_groups=ALL8,
        ins=[ar_in.opt()], outs=[ar_out.opt()]),
        reads=("ar_in",), writes=("ar_out",), sem="cc")
    dma("sync", SM[:, SM_G:SM_G + 8], ar_out, reads=("ar_out",), writes=("st",))
    op("tensor", lambda e: e.matmul(ps_a[0:1, 0:8], smc(SM_ONE),
                                    SM[:, SM_G:SM_G + 8], start=True, stop=True),
       reads=("st",), writes=("ps_a",))
    op("vector", lambda e: e.tensor_copy(SM[0:1, SM_T + 4:SM_T + 12],
                                         ps_a[0:1, 0:8]),
       reads=("ps_a",), writes=("st",))
    op("tensor", lambda e: e.matmul(ps_a[:, 0:8], SM[0:1, SM_ROW:SM_ROW + 128],
                                    SM[0:1, SM_T + 4:SM_T + 12],
                                    start=True, stop=True),
       reads=("st",), writes=("ps_a",))
    op("vector", lambda e: e.tensor_copy(SM[:, SM_G:SM_G + 8], ps_a[:, 0:8]),
       reads=("ps_a",), writes=("st",))
    NW = float(8192 * 2048)
    for j in range(3):
        op("vector", (lambda jj: lambda e: e.tensor_scalar(
            smc(SM_THR + 3 + jj), smc(SM_G + jj), 1.0 / NW, 1e-5,
            ALU.mult, ALU.max))(j), writes=("st",))
        op("vector", (lambda jj: lambda e: e.tensor_scalar_mul(
            smc(SM_THR + jj), smc(SM_THR + 3 + jj), 0.5))(j), writes=("st",))
        op("vector", (lambda jj: lambda e: e.tensor_scalar_mul(
            smc(SM_NTHR + jj), smc(SM_THR + jj), -1.0))(j), writes=("st",))
        op("vector", (lambda jj: lambda e: e.tensor_scalar_mul(
            smc(SM_THR + 6 + jj), smc(SM_G + 3 + jj), 1.0 / NBLK_REAL))(j),
            writes=("st",))
        op("scalar", (lambda jj: lambda e: e.copy(
            wbv(11264 + jj, 1), smc(SM_THR + 6 + jj)))(j),
            reads=("st",), writes=("wbm",))
        op("scalar", (lambda jj: lambda e: e.copy(
            smc(SM_NTHR + 3 + jj), wbv(11264 + jj, 1)))(j),
            reads=("wbm",), writes=("st",))

    def emit_proj(layer):
        srcs = {"q": (wq_d, 0), "k": (wk_d, 0), "v": (wv_d, 0)} if layer == 0 \
            else {"q": (wqkv_d, 0), "k": (wqkv_d, 2048), "v": (wqkv_d, 4096)}
        dsts = {"k": kT_in[layer], "v": vT_in[layer], "q": qT_dr[layer]}
        for which in ("k", "v", "q"):
            wsrc, coff = srcs[which]
            for t in range(16):
                slot = 18432 + 2048 * (t % 2)
                dma("gpsimd",
                    aav(slot, 2048).rearrange("p (ft c) -> p ft c", c=128),
                    wsrc[:, coff + 128 * t:coff + 128 * (t + 1)]
                    .rearrange("(ft p) c -> p ft c", p=128),
                    writes=aanames(slot, 2048))
                items = []
                for n2 in range(2):
                    for ft in range(16):
                        items.append(mmf(
                            ps_a[:, 512 * n2:512 * (n2 + 1)],
                            aav(slot + 128 * ft, 128),
                            aav(1024 * ft + 512 * n2, 512),
                            ft == 0, ft == 15))
                mm_group(items, reads=aanames(slot, 2048) + aanames(0, 16384),
                         writes=("ps_a",))
                stg = 22528 + 1024 * (t % 2)
                if layer == 1:
                    boff = {"q": 0, "k": 16, "v": 32}[which] + t
                    op("scalar", (lambda st_, bo: lambda e: e.activation(
                        aav(st_, 1024), ps_a[:], AF.Identity,
                        bias=smc(SM_BQ + bo)))(stg, boff),
                        reads=("ps_a", "st"), writes=aanames(stg, 1024))
                else:
                    op("scalar", (lambda st_: lambda e: e.copy(
                        aav(st_, 1024), ps_a[:]))(stg),
                        reads=("ps_a",), writes=aanames(stg, 1024))
                dma("sync", dsts[which][128 * t:128 * (t + 1), :],
                    aav(stg, 1024), reads=aanames(stg, 1024),
                    writes=(f"{which}T{layer}_in",))
            if which in ("k", "v"):
                agi = kT_in[layer] if which == "k" else vT_in[layer]
                ago = kT_ag[layer] if which == "k" else vT_ag[layer]
                ag(agi, ago, f"{which}T{layer}_in", f"{which}T{layer}_ag",
                   PAIRS)

    emit_proj(0)

    def transpose_tile_out(src_c0, width, stage_c0, reads):
        nt = width // 128
        items = [mmf(ps_big[:, 128 * j:128 * (j + 1)],
                     wbv(src_c0 + 128 * j, 128), IDT[:], True, True)
                 for j in range(nt)]
        mm_group(items, reads=tuple(reads) + ("idt",), writes=("ps_big",))
        op("vector", (lambda w: lambda e: e.tensor_copy(
            wbv(stage_c0, w), ps_big[:, 0:w]))(width),
            reads=("ps_big",), writes=wbnames(stage_c0, width))

    def tern_pass2(dram, gidx, tiles, width, dst, dstname):
        for t in range(tiles):
            dma("sync", FS[:, 0:width], dram[128 * t:128 * (t + 1), :],
                writes=("fs",))
            op("vector", (lambda w, g: lambda e: e.tensor_single_scalar(
                wbv(0, w), FS[:, 0:w], smc(SM_THR + g), ALU.is_gt))(width, gidx),
                reads=("fs", "st"), writes=wbnames(0, width))
            op("vector", (lambda w, g: lambda e: e.tensor_scalar(
                wbv(2048, w), FS[:, 0:w], smc(SM_NTHR + g), -1.0,
                ALU.is_gt, ALU.add))(width, gidx),
                reads=("fs", "st"), writes=wbnames(2048, width))
            op("vector", (lambda w: lambda e: e.tensor_add(
                wbv(4096, w), wbv(0, w), wbv(2048, w)))(width),
                reads=wbnames(0, width) + wbnames(2048, width),
                writes=wbnames(4096, width))
            transpose_tile_out(4096, width, 8192, wbnames(4096, width))
            dma("sync",
                dst[:, 128 * t:128 * (t + 1)]
                .rearrange("(f p) c -> p f c", p=128),
                wbv(8192, width).rearrange("p (f c) -> p f c", c=128),
                reads=wbnames(8192, width), writes=(dstname,))

    tern_pass2(wg_sh, 0, 8, 2048, wgT_in, "wgT_in")
    ag(wgT_in, wgT_ag, "wgT_in", "wgT_ag", ALL8)
    tern_pass2(wu_sh, 1, 8, 2048, wuT_in, "wuT_in")
    ag(wuT_in, wuT_ag, "wuT_in", "wuT_ag", ALL8)
    tern_pass2(wd_sh, 2, 16, 1024, wdT_in, "wdT_in")
    ag(wdT_in, wdT_ag, "wdT_in", "wdT_ag", ALL8)

    def nf4_pass2(dram, widx, tiles, width, scol, dst, dstname):
        nblk = width // 64
        nsc = tiles * nblk
        nsb = nsc * 128 // 256
        P = nsb
        flat = nsc * 128
        rowc = nblk if width == 5632 else 32
        dma("sync", SM[0:P, SM_SB1:SM_SB1 + 256],
            s_rt[widx].flatten()[0:flat].rearrange("(s c) -> s c", c=256),
            reads=("s_rt",), writes=("smsb1",))
        op("vector", (lambda P_, w_: lambda e: e.tensor_single_scalar(
            SM[0:P_, SM_SB2:SM_SB2 + 256], SM[0:P_, SM_SB1:SM_SB1 + 256],
            SM[0:P_, SM_THR + 6 + w_:SM_THR + 7 + w_], ALU.subtract))(P, widx),
            reads=("smsb1", "st"), writes=("smsb2",))
        op("vector", (lambda P_: lambda e: e.reduce_max(
            SM[0:P_, SM_T + 4:SM_T + 5], SM[0:P_, SM_SB2:SM_SB2 + 256],
            axis=AX.X, apply_absolute_value=True))(P),
            reads=("smsb2",), writes=("st",))
        op("vector", (lambda P_: lambda e: e.reciprocal(
            SM[0:P_, SM_T + 5:SM_T + 6], SM[0:P_, SM_T + 4:SM_T + 5]))(P),
            writes=("st",))
        op("vector", (lambda P_: lambda e: e.tensor_scalar_mul(
            SM[0:P_, SM_T + 5:SM_T + 6], SM[0:P_, SM_T + 5:SM_T + 6],
            128.0))(P), writes=("st",))
        op("vector", (lambda P_: lambda e: e.tensor_scalar(
            SM[0:P_, SM_SB2:SM_SB2 + 256], SM[0:P_, SM_SB2:SM_SB2 + 256],
            SM[0:P_, SM_T + 5:SM_T + 6], MAGIC, ALU.mult, ALU.add))(P),
            reads=("st",), writes=("smsb2",))
        op("vector", (lambda P_: lambda e: e.tensor_scalar(
            SM[0:P_, SM_SB2:SM_SB2 + 256], SM[0:P_, SM_SB2:SM_SB2 + 256],
            -MAGIC, -128.0, ALU.add, ALU.max))(P), writes=("smsb2",))
        op("vector", (lambda P_: lambda e: e.tensor_scalar_min(
            SM[0:P_, SM_SB2:SM_SB2 + 256], SM[0:P_, SM_SB2:SM_SB2 + 256],
            127.0))(P), writes=("smsb2",))
        op("vector", (lambda P_: lambda e: e.tensor_scalar_mul(
            SM[0:P_, SM_T + 4:SM_T + 5], SM[0:P_, SM_T + 4:SM_T + 5],
            1.0 / 128.0))(P), writes=("st",))
        op("vector", (lambda P_: lambda e: e.tensor_single_scalar(
            WB[0:P_, 8192:8448], SM[0:P_, SM_SB2:SM_SB2 + 256],
            SM[0:P_, SM_T + 4:SM_T + 5], ALU.mult))(P),
            reads=("smsb2", "st"), writes=wbnames(8192, 256))
        op("vector", (lambda P_, w_: lambda e: e.tensor_single_scalar(
            WB[0:P_, 8192:8448], WB[0:P_, 8192:8448],
            SM[0:P_, SM_NTHR + 3 + w_:SM_NTHR + 4 + w_], ALU.add))(P, widx),
            reads=("st",), writes=wbnames(8192, 256))
        op("vector", (lambda P_: lambda e: e.tensor_scalar_max(
            SM[0:P_, SM_SB1:SM_SB1 + 256], SM[0:P_, SM_SB1:SM_SB1 + 256],
            1e-30))(P), writes=("smsb1",))
        op("vector", (lambda P_: lambda e: e.reciprocal(
            SM[0:P_, SM_SB2:SM_SB2 + 256], SM[0:P_, SM_SB1:SM_SB1 + 256]))(P),
            reads=("smsb1",), writes=("smsb2",))
        dma("sync", d_rt[widx].flatten()[0:flat].rearrange(
            "(s c) -> s c", c=256)[0:P, :], WB[0:P, 8192:8448],
            reads=wbnames(8192, 256), writes=("d_rt",))
        dma("sync", r_rt[widx].flatten()[0:flat].rearrange(
            "(s c) -> s c", c=256)[0:P, :], SM[0:P, SM_SB2:SM_SB2 + 256],
            reads=("smsb2",), writes=("r_rt",))
        dma("sync", WB[:, 8448:8448 + nsc].rearrange(
            "p (t c) -> p t c", c=rowc),
            d_rt[widx].flatten()[0:flat].rearrange(
                "(t p c) -> p t c", p=128, c=rowc),
            reads=("d_rt",), writes=wbnames(8448, nsc))
        dma("sync", FS[:, 0:nsc].rearrange("p (t c) -> p t c", c=rowc),
            r_rt[widx].flatten()[0:flat].rearrange(
                "(t p c) -> p t c", p=128, c=rowc),
            reads=("r_rt",), writes=("fs",))
        for t in range(tiles):
            nsub = (width + 2047) // 2048
            for s in range(nsub):
                c0 = 2048 * s
                w = min(2048, width - c0)
                bc = t * nblk + c0 // 64
                dma("sync", wbv(0, w), dram[128 * t:128 * (t + 1), c0:c0 + w],
                    writes=wbnames(0, w))
                op("vector", (lambda w_, b_: lambda e: e.tensor_tensor(
                    wbv(2048, w_).rearrange("p (b i) -> p b i", i=64),
                    wbv(0, w_).rearrange("p (b i) -> p b i", i=64),
                    FS[:, b_:b_ + w_ // 64].unsqueeze(2)
                    .broadcast_to([128, w_ // 64, 64]), ALU.mult))(w, bc),
                    reads=wbnames(0, w) + ("fs",), writes=wbnames(2048, w))
                op("vector", (lambda w_: lambda e: e.tensor_scalar(
                    wbv(4096, w_), wbv(2048, w_), smc(SM_NF4M),
                    float(NF4_DELTA[0]), ALU.is_gt, ALU.mult))(w),
                    reads=wbnames(2048, w) + ("st",), writes=wbnames(4096, w))
                op("vector", (lambda w_: lambda e: e.tensor_scalar_add(
                    wbv(4096, w_), wbv(4096, w_), NF4_BASE))(w),
                    writes=wbnames(4096, w))
                for kk in range(1, 15):
                    op("vector", (lambda w_, k_: lambda e: e.tensor_scalar(
                        wbv(6144, w_), wbv(2048, w_), smc(SM_NF4M + k_),
                        float(NF4_DELTA[k_]), ALU.is_gt, ALU.mult))(w, kk),
                        reads=wbnames(2048, w) + ("st",),
                        writes=wbnames(6144, w))
                    op("vector", (lambda w_: lambda e: e.tensor_add(
                        wbv(4096, w_), wbv(4096, w_), wbv(6144, w_)))(w),
                        reads=wbnames(6144, w), writes=wbnames(4096, w))
                op("vector", (lambda w_, b_: lambda e: e.tensor_tensor(
                    wbv(0, w_).rearrange("p (b i) -> p b i", i=64),
                    wbv(4096, w_).rearrange("p (b i) -> p b i", i=64),
                    WB[:, 8448 + b_:8448 + b_ + w_ // 64].unsqueeze(2)
                    .broadcast_to([128, w_ // 64, 64]), ALU.mult))(w, bc),
                    reads=wbnames(4096, w) + wbnames(8448, nsc),
                    writes=wbnames(0, w))
                transpose_tile_out(0, w, 9472, wbnames(0, w))
                dma("sync",
                    dst[c0:c0 + w, 128 * t:128 * (t + 1)]
                    .rearrange("(f p) c -> p f c", p=128),
                    wbv(9472, w).rearrange("p (f c) -> p f c", c=128),
                    reads=wbnames(9472, w), writes=(dstname,))

    for widx, scol, tiles, width in ((0, 1024, 6, 2048), (1, 1216, 6, 2048),
                                     (2, 1408, 2, 5632)):
        nblk = width // 64
        nsc = tiles * nblk
        flat = nsc * 128
        rowc = nblk if width == 5632 else 32
        dma("sync",
            s_rt[widx].flatten()[0:flat].rearrange(
                "(t p c) -> p t c", p=128, c=rowc),
            SM[:, scol:scol + nsc].rearrange("p (t c) -> p t c", c=rowc),
            reads=("smsb",), writes=("s_rt",))

    nf4_pass2(w1_sh, 0, 6, 2048, 1024, w1T_in, "w1T_in")
    ag(w1T_in, w1T_ag, "w1T_in", "w1T_ag", ALL8)
    nf4_pass2(w2_sh, 1, 6, 2048, 1216, w2T_in, "w2T_in")
    ag(w2T_in, w2T_ag, "w2T_in", "w2T_ag", ALL8)
    nf4_pass2(w3_sh, 2, 2, 5632, 1408, w3T_in, "w3T_in")
    ag(w3T_in, w3T_ag, "w3T_in", "w3T_ag", ALL8)

    def emit_ffn0():
        for m in range(8):
            op("vector", (lambda mi: lambda e: e.tensor_tensor(
                smc(SM_SCG + mi), smc(SM_ISX + mi), smc(SM_THR + 3),
                ALU.mult))(m), writes=("st",))
            op("vector", (lambda mi: lambda e: e.tensor_tensor(
                smc(SM_SCU + mi), smc(SM_ISX + mi), smc(SM_THR + 4),
                ALU.mult))(m), writes=("st",))
        for th in range(2):
            ms = [4 * th + i for i in range(4)]
            hslot = [8192, 16384, 24576, 32768]
            for i, m in enumerate(ms):
                xt = x_sb[:, 2048 * m:2048 * (m + 1)]
                for cchunk in range(4):
                    c0 = 512 * cchunk
                    op("vector", (lambda xt_, mi, c_: lambda e:
                        e.scalar_tensor_tensor(
                            out=SM[:, SM_QT:SM_QT + 512],
                            in0=xt_[:, c_:c_ + 512],
                            scalar=smc(SM_MEAN0 + mi),
                            in1=smc(SM_C1X + mi).broadcast_to([128, 512]),
                            op0=ALU.subtract, op1=ALU.mult))(xt, m, c0),
                        reads=("x", "st"), writes=("smqt",))
                    op("vector", lambda e: e.tensor_scalar(
                        SM[:, SM_QT:SM_QT + 512], SM[:, SM_QT:SM_QT + 512],
                        MAGIC, -MAGIC, ALU.add, ALU.add), writes=("smqt",))
                    op("vector", (lambda c_: lambda e: e.tensor_copy(
                        aav(SCR + c_, 512), SM[:, SM_QT:SM_QT + 512]))(c0),
                        reads=("smqt",), writes=aanames(SCR, 2048))
                items = [mmf(ps_big[:, 128 * ft:128 * (ft + 1)],
                             aav(SCR + 128 * ft, 128), IDT[:], True, True)
                         for ft in range(16)]
                mm_group(items, reads=aanames(SCR, 2048) + ("idt",),
                         writes=("ps_big",))
                op("vector", (lambda ii: lambda e: e.tensor_copy(
                    AA[:, 0:8192].rearrange("p (t q) -> p t q", q=512)
                    [:, :, 128 * ii:128 * (ii + 1)],
                    ps_big[:].rearrange("p (t q) -> p t q", q=128)))(i),
                    reads=("ps_big",), writes=aanames(0, 8192))
            for m in ms:
                op("vector", (lambda mi: lambda e: e.memset(
                    smc(SM_VAR + mi), 0.0))(m), writes=("st",))
            for i in range(4):
                op("vector", (lambda ii: lambda e: e.memset(
                    smc(SM_T + 12 + ii), 0.0))(i), writes=("st",))
            for n in range(16):
                r, off = (512 * n) // 1024, (512 * n) % 1024
                dma("sync", WB[:, 0:8192].rearrange("p (f c) -> p f c", c=512),
                    wgT_ag[2048 * r:2048 * (r + 1), off:off + 512]
                    .rearrange("(f p) c -> p f c", p=128),
                    writes=wbnames(0, 8192), extra=res.rd("wgT_ag"))
                dma("sync", AA[:, 40960:49152].rearrange(
                    "p (f c) -> p f c", c=512),
                    wuT_ag[2048 * r:2048 * (r + 1), off:off + 512]
                    .rearrange("(f p) c -> p f c", p=128),
                    writes=aanames(40960, 8192), extra=res.rd("wuT_ag"))
                dma("gpsimd", wbv(8960, 512),
                    lnw_d[:, 512 * n:512 * (n + 1)],
                    writes=wbnames(8960, 512))
                for i, m in enumerate(ms):
                    items = [mmf(ps_a[:, 0:512], aav(512 * ft + 128 * i, 128),
                                 wbv(512 * ft, 512), ft == 0, ft == 15)
                             for ft in range(16)]
                    mm_group(items, reads=aanames(0, 8192) + wbnames(0, 8192),
                             writes=("ps_a",))
                    items = [mmf(ps_b[:, 0:512], aav(512 * ft + 128 * i, 128),
                                 aav(40960 + 512 * ft, 512), ft == 0, ft == 15)
                             for ft in range(16)]
                    mm_group(items,
                             reads=aanames(0, 8192) + aanames(40960, 8192),
                             writes=("ps_b",))
                    op("scalar", (lambda mi: lambda e: e.activation(
                        wbv(8192, 512), ps_a[:, 0:512], AF.Silu,
                        scale=smc(SM_SCG + mi)))(m),
                        reads=("ps_a", "st"), writes=wbnames(8192, 512))
                    hc = hslot[i] + 512 * n
                    op("vector", (lambda mi, hc_: lambda e:
                        e.scalar_tensor_tensor(
                            out=aav(hc_, 512), in0=ps_b[:, 0:512],
                            scalar=smc(SM_SCU + mi), in1=wbv(8192, 512),
                            op0=ALU.mult, op1=ALU.mult))(m, hc),
                        reads=("ps_b", "st") + wbnames(8192, 512),
                        writes=aanames(hc, 512))
                    op("scalar", (lambda hc_: lambda e: e.activation(
                        wbv(9984, 512), aav(hc_, 512), AF.Square,
                        accum_out=smc(SM_T + 2)))(hc),
                        reads=aanames(hc, 512),
                        writes=wbnames(9984, 512) + ("st",))
                    op("vector", (lambda mi: lambda e: e.tensor_add(
                        smc(SM_VAR + mi), smc(SM_VAR + mi),
                        smc(SM_T + 2)))(m), writes=("st",))
                    op("vector", (lambda hc_: lambda e: e.tensor_tensor(
                        SM[:, 1024:1536], aav(hc_, 512),
                        wbv(8960, 512), ALU.mult))(hc),
                        reads=aanames(hc, 512) + wbnames(8960, 512),
                        writes=("smsb1", "smsb2"))
                    op("vector", lambda e: e.reduce_max(
                        smc(392), SM[:, 1024:1536], axis=AX.X,
                        apply_absolute_value=True),
                        reads=("smsb1", "smsb2"), writes=("sfg",))
                    op("vector", (lambda ii: lambda e: e.tensor_max(
                        smc(SM_T + 12 + ii), smc(SM_T + 12 + ii),
                        smc(392)))(i), reads=("sfg",), writes=("st",))
            for i, m in enumerate(ms):
                op("vector", (lambda mi: lambda e: e.tensor_scalar(
                    smc(SM_T + 2), smc(SM_VAR + mi), 1.0 / 8192.0, 1e-6,
                    ALU.mult, ALU.add))(m), writes=("st",))
                op("scalar", lambda e: e.activation(
                    smc(SM_T + 2), smc(SM_T + 2), AF.Ln),
                    reads=("st",), writes=("st",))
                op("scalar", lambda e: e.activation(
                    smc(SM_T + 3), smc(SM_T + 2), AF.Exp, scale=-0.5),
                    reads=("st",), writes=("st",))
                op("vector", (lambda ii: lambda e: e.tensor_tensor(
                    smc(SM_T + 2), smc(SM_T + 3), smc(SM_T + 12 + ii),
                    ALU.mult))(i), writes=("st",))
                op("vector", lambda e: e.tensor_scalar_max(
                    smc(SM_T + 2), smc(SM_T + 2), 1e-5), writes=("st",))
                op("vector", (lambda mi: lambda e: e.tensor_scalar_mul(
                    smc(SM_C3 + mi), smc(SM_T + 2), 1.0 / 127.0))(m),
                    writes=("st",))
                op("vector", lambda e: e.reciprocal(
                    smc(SM_T + 2), smc(SM_T + 2)), writes=("st",))
                op("vector", lambda e: e.scalar_tensor_tensor(
                    out=smc(SM_T + 2), in0=smc(SM_T + 2), scalar=127.0,
                    in1=smc(SM_T + 3), op0=ALU.mult, op1=ALU.mult),
                    writes=("st",))
                op("vector", (lambda mi: lambda e: e.tensor_tensor(
                    smc(SM_C3 + mi), smc(SM_C3 + mi), smc(SM_THR + 5),
                    ALU.mult))(m), writes=("st",))
                for n in range(16):
                    hc = hslot[i] + 512 * n
                    eng = "vector" if n % 2 == 0 else "gpsimd"
                    qc = SM_QT if n % 2 == 0 else 1024
                    qn = ("smqt",) if n % 2 == 0 else ("smsb1", "smsb2")
                    lc = 8960 if n % 2 == 0 else 9984
                    dma("gpsimd", wbv(lc, 512),
                        lnw_d[:, 512 * n:512 * (n + 1)],
                        writes=wbnames(lc, 512))
                    op("vector", (lambda hc_, qc_, lc_: lambda e: e.tensor_tensor(
                        SM[:, qc_:qc_ + 512], aav(hc_, 512),
                        wbv(lc_, 512), ALU.mult))(hc, qc, lc),
                        reads=aanames(hc, 512) + wbnames(lc, 512),
                        writes=qn)
                    op(eng, (lambda qc_: lambda e: e.tensor_scalar(
                        SM[:, qc_:qc_ + 512], SM[:, qc_:qc_ + 512],
                        smc(SM_T + 2), MAGIC, ALU.mult, ALU.add))(qc),
                        reads=("st",), writes=qn)
                    op(eng, (lambda hc_, qc_: lambda e: e.tensor_scalar_add(
                        aav(hc_, 512), SM[:, qc_:qc_ + 512], -MAGIC))(hc, qc),
                        reads=qn, writes=aanames(hc, 512))
                for batch in range(4):
                    c0 = hslot[i] + 2048 * batch
                    items = [mmf(ps_big[:, 128 * j:128 * (j + 1)],
                                 aav(c0 + 128 * j, 128), IDT[:], True, True)
                             for j in range(16)]
                    mm_group(items, reads=aanames(c0, 2048) + ("idt",),
                             writes=("ps_big",))
                    op("vector", (lambda c_: lambda e: e.tensor_copy(
                        aav(c_, 2048), ps_big[:]))(c0),
                        reads=("ps_big",), writes=aanames(c0, 2048))
            for dg in range(8):
                wdc = []
                for jj in range(8):
                    d = 8 * dg + jj
                    dst_c = (2048 * jj) if jj < 4 else (40960 + 2048 * (jj - 4))
                    wdc.append(dst_c)
                    dma("sync", aav(dst_c, 2048),
                        wdT_ag[128 * d:128 * (d + 1), :],
                        writes=aanames(dst_c, 2048),
                        extra=res.rd("wdT_ag"))
                for i, m in enumerate(ms):
                    items = []
                    for jj in range(8):
                        d = 8 * dg + jj
                        for n4 in range(4):
                            items.append(mmf(
                                ps_big[:, 512 * n4:512 * (n4 + 1)],
                                aav(hslot[i] + 128 * d, 128),
                                aav(wdc[jj] + 512 * n4, 512),
                                jj == 0, jj == 7))
                    rdn = aanames(hslot[i], 8192) + aanames(0, 8192) \
                        + aanames(40960, 8192)
                    mm_group(items, reads=rdn, writes=("ps_big",))
                    op("vector", (lambda mi: lambda e: e.scalar_tensor_tensor(
                        out=x_sb[:, 2048 * mi:2048 * (mi + 1)],
                        in0=ps_big[:], scalar=smc(SM_C3 + mi),
                        in1=x_sb[:, 2048 * mi:2048 * (mi + 1)],
                        op0=ALU.mult, op1=ALU.add))(m),
                        reads=("ps_big", "st", "x"), writes=("x",))

    emit_ffn0()

    def emit_attn(layer):
        use_mask = layer == 0
        wo_src = wo_d if layer == 0 else wout_d
        kag, vag, qdr = kT_ag[layer], vT_ag[layer], qT_dr[layer]
        kname, vname = f"kT{layer}_ag", f"vT{layer}_ag"
        qname = f"qT{layer}_in"
        tls = TLS if layer == 0 else [list(range(16))] * 8
        offs = [0]
        for i in range(8):
            offs.append(offs[-1] + 128 * len(tls[i]))
        if layer == 1:
            dma("sync", wbv(0, 2048), bout_d, writes=wbnames(0, 2048))
            for m in range(8):
                op("vector", (lambda mi: lambda e: e.tensor_add(
                    x_sb[:, 2048 * mi:2048 * (mi + 1)],
                    x_sb[:, 2048 * mi:2048 * (mi + 1)], wbv(0, 2048)))(m),
                    reads=wbnames(0, 2048) + ("x",), writes=("x",))

        def psb_names(qt):
            if layer == 1 or len(tls[qt]) > 8:
                return ("ps_big", "psbg0", "psbg1")
            return (f"psbg{qt % 2}",)

        def softmax_stage(qt):
            tl = tls[qt]
            nt = len(tl)
            W = 128 * nt
            par = qt % 2
            dense = nt > 8
            st0 = SM_AT + 4 * par
            if layer == 1:
                pc = 6144 + 2048 * par
                po = 0
            else:
                pc = 6144 if dense else 6144 + 1024 * par
                po = 0 if dense else 1024 * par
            mc = 8192 if dense else 8192 + 1024 * par
            psn = psb_names(qt)
            san = (f"sa{par}",)
            items = []
            for s0, t0, L in _segs(tl):
                items.append(mmf(ps_big[:, po + 128 * s0:po + 128 * (s0 + L)],
                                 wbv(10240 + 128 * qt, 128),
                                 wbv(128 * t0, 128 * L), True, True))
            mm_group(items, reads=wbnames(10240, 1024) + wbnames(0, 2048),
                     writes=psn)
            if use_mask:
                dma("sync", wbv(mc, W), mask_d[qt][:, 0:W],
                    writes=wbnames(mc, W))
                op("vector", (lambda W_, mc_, po_: lambda e: e.tensor_add(
                    ps_big[:, po_:po_ + W_], ps_big[:, po_:po_ + W_],
                    wbv(mc_, W_)))(W, mc, po),
                    reads=wbnames(mc, W), writes=psn)
            op("vector", (lambda W_, c, po_: lambda e: e.reduce_max(
                smc(c), ps_big[:, po_:po_ + W_], axis=AX.X))(W, st0, po),
                reads=psn, writes=san)
            op("vector", (lambda c: lambda e: e.tensor_scalar_mul(
                smc(c + 1), smc(c), -C_ATTN))(st0), writes=san)
            op("scalar", (lambda W_, c, pc_, po_: lambda e: e.activation(
                wbv(pc_, W_), ps_big[:, po_:po_ + W_], AF.Exp,
                bias=smc(c + 1), scale=C_ATTN,
                accum_out=smc(c + 2)))(W, st0, pc, po),
                reads=psn + san, writes=wbnames(pc, W) + san)
            op("vector", (lambda c: lambda e: e.reciprocal(
                smc(c + 3), smc(c + 2)))(st0), writes=san)
            op("gpsimd", (lambda W_, c, pc_: lambda e: e.tensor_single_scalar(
                wbv(pc_, W_), wbv(pc_, W_), smc(c + 3), ALU.mult))(W, st0, pc),
                reads=san, writes=wbnames(pc, W))

        def transpose_stage(qt):
            tl = tls[qt]
            nt = len(tl)
            par = qt % 2
            dense = nt > 8
            if layer == 1:
                pc = 6144 + 2048 * par
            else:
                pc = 6144 if dense else 6144 + 1024 * par
            if layer == 0:
                for w0 in range(0, nt, 8):
                    nw = min(8, nt - w0)
                    items = [mmf(ps_b[:, 128 * j:128 * (j + 1)],
                                 wbv(pc + 128 * (w0 + j), 128), IDT[:],
                                 True, True) for j in range(nw)]
                    mm_group(items, reads=wbnames(pc, 128 * nt) + ("idt",),
                             writes=("ps_b",))
                    dst = offs[qt] + 128 * w0
                    if (qt + w0) % 2 == 0:
                        op("scalar", (lambda d_, n_: lambda e: e.copy(
                            AA[:, d_:d_ + 128 * n_],
                            ps_b[:, 0:128 * n_]))(dst, nw),
                            reads=("ps_b",), writes=aanames(dst, 128 * nw))
                    else:
                        op("vector", (lambda d_, n_: lambda e: e.tensor_copy(
                            AA[:, d_:d_ + 128 * n_],
                            ps_b[:, 0:128 * n_]))(dst, nw),
                            reads=("ps_b",), writes=aanames(dst, 128 * nw))
            else:
                items = [mmf(ps_a[:, 128 * j:128 * (j + 1)],
                             wbv(pc + 128 * j, 128), IDT[:], True, True)
                         for j in range(8)]
                mm_group(items, reads=wbnames(pc, 2048) + ("idt",),
                         writes=("ps_a",))
                items = [mmf(ps_b[:, 128 * j:128 * (j + 1)],
                             wbv(pc + 1024 + 128 * j, 128), IDT[:],
                             True, True) for j in range(8)]
                mm_group(items, reads=wbnames(pc, 2048) + ("idt",),
                         writes=("ps_b",))
                op("scalar", (lambda q_: lambda e: e.copy(
                    AA[:, 0:16384].rearrange("p (j q) -> p j q", q=1024)
                    [:, 0:8, 128 * q_:128 * (q_ + 1)],
                    ps_a[:].rearrange("p (j q) -> p j q", q=128)))(qt),
                    reads=("ps_a",), writes=aanames(0, 8192))
                op("vector", (lambda q_: lambda e: e.tensor_copy(
                    AA[:, 0:16384].rearrange("p (j q) -> p j q", q=1024)
                    [:, 8:16, 128 * q_:128 * (q_ + 1)],
                    ps_b[:].rearrange("p (j q) -> p j q", q=128)))(qt),
                    reads=("ps_b",), writes=aanames(8192, 8192))

        for g in range(4):
            for hh in range(4):
                dma("gpsimd", aav(20480 + 2048 * hh, 2048),
                    wo_src[512 * g + 128 * hh:512 * g + 128 * (hh + 1), :],
                    writes=aanames(20480 + 2048 * hh, 2048))
            for hh in range(4):
                h = 4 * g + hh
                for half in range(2):
                    dma("sync", wbv(1024 * half, 1024),
                        kag[2048 * half + 128 * h:2048 * half + 128 * (h + 1), :],
                        writes=wbnames(1024 * half, 1024),
                        extra=res.rd(kname))
                    dma("sync", wbv(2048 + 1024 * half, 1024),
                        vag[2048 * half + 128 * h:2048 * half + 128 * (h + 1), :],
                        writes=wbnames(2048 + 1024 * half, 1024),
                        extra=res.rd(vname))
                dma("sync", wbv(10240, 1024), qdr[128 * h:128 * (h + 1), :],
                    writes=wbnames(10240, 1024), extra=res.rd(qname))
                items = [mmf(ps_a[:, 128 * j:128 * (j + 1)],
                             wbv(2048 + 128 * j, 128), IDT[:], True, True)
                         for j in range(8)]
                mm_group(items, reads=wbnames(2048, 2048) + ("idt",),
                         writes=("ps_a",))
                items = [mmf(ps_b[:, 128 * j:128 * (j + 1)],
                             wbv(2048 + 1024 + 128 * j, 128), IDT[:],
                             True, True) for j in range(8)]
                mm_group(items, reads=wbnames(2048, 2048) + ("idt",),
                         writes=("ps_b",))
                op("scalar", lambda e: e.copy(wbv(4096, 1024),
                                              ps_a[:]),
                   reads=("ps_a",), writes=wbnames(4096, 1024))
                op("vector", lambda e: e.tensor_copy(wbv(4096 + 1024, 1024),
                                                     ps_b[:]),
                   reads=("ps_b",), writes=wbnames(4096 + 1024, 1024))
                first_def = 1 if layer == 0 else 0
                for qt in range(8):
                    softmax_stage(qt)
                    if qt == 0 and first_def == 1:
                        transpose_stage(0)
                    elif qt > first_def:
                        transpose_stage(qt - 1)
                transpose_stage(7)
                if layer == 0:
                    for qt in range(8):
                        tl = tls[qt]
                        items = []
                        for s, t in enumerate(tl):
                            items.append(mmf(
                                ps_a[:, 128 * qt:128 * (qt + 1)],
                                wbv(4096 + 128 * t, 128),
                                AA[:, offs[qt] + 128 * s:
                                   offs[qt] + 128 * (s + 1)],
                                s == 0, s == len(tl) - 1))
                        mm_group(items,
                                 reads=aanames(offs[qt], 128 * len(tl))
                                 + wbnames(4096, 2048), writes=("ps_a",))
                else:
                    items = []
                    for j in range(16):
                        for n2 in range(2):
                            items.append(mmf(
                                ps_a[:, 512 * n2:512 * (n2 + 1)],
                                wbv(4096 + 128 * j, 128),
                                aav(1024 * j + 512 * n2, 512),
                                j == 0, j == 15))
                    mm_group(items,
                             reads=aanames(0, 16384) + wbnames(4096, 2048),
                             writes=("ps_a",))
                op("scalar", (lambda hh_: lambda e: e.copy(
                    aav(16384 + 1024 * hh_, 1024), ps_a[:]))(hh),
                    reads=("ps_a",), writes=aanames(16384 + 1024 * hh, 1024))
            for m in range(8):
                items = []
                for hh in range(4):
                    for n4 in range(4):
                        items.append(mmf(
                            ps_big[:, 512 * n4:512 * (n4 + 1)],
                            aav(16384 + 1024 * hh + 128 * m, 128),
                            aav(20480 + 2048 * hh + 512 * n4, 512),
                            hh == 0, hh == 3))
                mm_group(items, reads=aanames(16384, 4096) + aanames(20480, 8192),
                         writes=("ps_big", "psbg0", "psbg1"))
                op("vector", (lambda mi: lambda e: e.tensor_add(
                    x_sb[:, 2048 * mi:2048 * (mi + 1)],
                    x_sb[:, 2048 * mi:2048 * (mi + 1)], ps_big[:]))(m),
                    reads=("ps_big", "x"), writes=("x",))

    emit_attn(0)

    emit_ln(1)
    emit_proj(1)

    def emit_ffn1():
        for qd in range(4):
            dls = list(range(11 * qd, 11 * qd + 11))
            for idx, dl in enumerate(dls):
                r, off = (128 * dl) // 768, (128 * dl) % 768
                dma("sync", wbv(0, 2048).rearrange("p (f c) -> p f c", c=128),
                    w1T_ag[2048 * r:2048 * (r + 1), off:off + 128]
                    .rearrange("(f p) c -> p f c", p=128),
                    writes=wbnames(0, 2048), extra=res.rd("w1T_ag"))
                dma("sync", wbv(2048, 2048).rearrange("p (f c) -> p f c", c=128),
                    w2T_ag[2048 * r:2048 * (r + 1), off:off + 128]
                    .rearrange("(f p) c -> p f c", p=128),
                    writes=wbnames(2048, 2048), extra=res.rd("w2T_ag"))
                items = []
                for n2 in range(2):
                    for ft in range(16):
                        items.append(mmf(
                            ps_a[:, 512 * n2:512 * (n2 + 1)],
                            wbv(128 * ft, 128),
                            aav(1024 * ft + 512 * n2, 512),
                            ft == 0, ft == 15))
                mm_group(items, reads=wbnames(0, 2048) + aanames(0, 16384),
                         writes=("ps_a",))
                items = []
                for n2 in range(2):
                    for ft in range(16):
                        items.append(mmf(
                            ps_b[:, 512 * n2:512 * (n2 + 1)],
                            wbv(2048 + 128 * ft, 128),
                            aav(1024 * ft + 512 * n2, 512),
                            ft == 0, ft == 15))
                mm_group(items, reads=wbnames(2048, 2048) + aanames(0, 16384),
                         writes=("ps_b",))
                op("scalar", lambda e: e.activation(
                    wbv(4096, 1024), ps_a[:], AF.Silu),
                    reads=("ps_a",), writes=wbnames(4096, 1024))
                hc = 16384 + 1024 * idx
                op("vector", (lambda hc_: lambda e: e.tensor_tensor(
                    aav(hc_, 1024), wbv(4096, 1024), ps_b[:], ALU.mult))(hc),
                    reads=wbnames(4096, 1024) + ("ps_b",),
                    writes=aanames(hc, 1024))
            for idx, dl in enumerate(dls):
                wc = 27648 + 2048 * idx
                dma("sync", aav(wc, 2048).rearrange("p (r c) -> p r c", c=256),
                    w3T_ag.rearrange("(r d) c -> d r c", d=5632)
                    [128 * dl:128 * (dl + 1)],
                    writes=aanames(wc, 2048), extra=res.rd("w3T_ag"))
            for m in range(8):
                items = []
                for idx in range(11):
                    for n4 in range(4):
                        items.append(mmf(
                            ps_big[:, 512 * n4:512 * (n4 + 1)],
                            aav(16384 + 1024 * idx + 128 * m, 128),
                            aav(27648 + 2048 * idx + 512 * n4, 512),
                            idx == 0, idx == 10))
                mm_group(items,
                         reads=aanames(16384, 11264) + aanames(27648, 22528),
                         writes=("ps_big",))
                op("vector", (lambda mi: lambda e: e.tensor_add(
                    x_sb[:, 2048 * mi:2048 * (mi + 1)],
                    x_sb[:, 2048 * mi:2048 * (mi + 1)], ps_big[:]))(m),
                    reads=("ps_big", "x"), writes=("x",))

    emit_ffn1()
    emit_attn(1)

    fin = dma("sync", out_d.rearrange("(m p) f -> p m f", p=128),
              x_sb[:].rearrange("p (m f) -> p m f", f=2048), reads=("x",),
              writes=("outd",))
    p.emit("sync", lambda e: None, waits=[("d_outd", fin)])

    with nc.Block() as block:
        def mk(engname):
            recs = p.recs[engname]

            def body(e, recs=recs):
                for fn, waits, inc, n in recs:
                    for s, v in waits:
                        e.wait_ge(sems[s], v)
                    instr = fn(e)
                    if inc is not None:
                        instr.then_inc(sems[inc], n)
            return body

        block.sync(mk("sync"))
        block.gpsimd(mk("gpsimd"))
        block.tensor(mk("tensor"))
        block.vector(mk("vector"))
        block.scalar(mk("scalar"))

    ctx.close()
    return nc


def _blk_idx(r):
    return np.concatenate([np.arange(128 * (2 * i + r), 128 * (2 * i + r) + 128)
                           for i in range(8)])


def _mask_for_core(r):
    out = np.zeros((8, 128, 2048), np.float32)
    for i in range(8):
        tl = TLS[i]
        q = (128 * (2 * i + r) + np.arange(128))[:, None]
        k = np.concatenate([128 * (2 * (t % 8) + t // 8) + np.arange(128)
                            for t in tl])[None, :]
        allow = (np.abs(q - k) < 256) | (q < 16) | (k < 16)
        out[i, :, 0:128 * len(tl)] = np.where(allow, np.float32(0.0),
                                              np.float32(NEG))
    return out.astype(ml_dtypes.bfloat16)


_NC_CACHE = [None]


def kernel(L0_attn_hidden_states, L0_attn_Wq, L0_attn_Wk, L0_attn_Wv,
           L0_attn_Wo, L0_ffn_w_gate, L0_ffn_w_up, L0_ffn_w_down,
           L0_ffn_ln_weight, L1_attn_w_qkv, L1_attn_b_qkv, L1_attn_w_out,
           L1_attn_b_out, L1_ffn_weight1, L1_ffn_weight2, L1_ffn_weight3):
    xs = np.asarray(L0_attn_hidden_states)
    f32 = np.float32
    bq = np.ascontiguousarray(np.asarray(L1_attn_b_qkv, f32)
                              .reshape(48, 128).T)
    bout = np.broadcast_to(np.asarray(L1_attn_b_out).astype(ml_dtypes.bfloat16),
                           (128, 2048)).copy()
    lnw = np.broadcast_to(np.asarray(L0_ffn_ln_weight, f32), (128, 8192)).copy()
    ident = np.eye(128, dtype=ml_dtypes.bfloat16)
    nf4c = np.zeros((32,), f32)
    nf4c[0:15] = NF4_MID
    nf4c[16:31] = NF4_DELTA
    nf4c = np.broadcast_to(nf4c, (128, 32)).copy()
    w1p = np.zeros((6144, 2048), ml_dtypes.bfloat16)
    w1p[0:5632] = np.asarray(L1_ffn_weight1)
    w2p = np.zeros((6144, 2048), ml_dtypes.bfloat16)
    w2p[0:5632] = np.asarray(L1_ffn_weight2)
    w3 = np.asarray(L1_ffn_weight3)
    wg = np.asarray(L0_ffn_w_gate, f32)
    wu = np.asarray(L0_ffn_w_up, f32)
    wd = np.asarray(L0_ffn_w_down, f32)

    bidx = [_blk_idx(0), _blk_idx(1)]
    in_maps = []
    for c in range(8):
        b, hf = c // 2, c % 2
        in_maps.append({
            "x_in": np.ascontiguousarray(xs[b][bidx[hf]], f32),
            "wq": np.asarray(L0_attn_Wq, f32),
            "wk": np.asarray(L0_attn_Wk, f32),
            "wv": np.asarray(L0_attn_Wv, f32),
            "wo": np.asarray(L0_attn_Wo, f32),
            "wqkv": np.asarray(L1_attn_w_qkv, f32),
            "wout": np.asarray(L1_attn_w_out, f32),
            "bqkvT": bq,
            "bout_bc": bout,
            "wg_sh": np.ascontiguousarray(wg[1024 * c:1024 * (c + 1), :]),
            "wu_sh": np.ascontiguousarray(wu[1024 * c:1024 * (c + 1), :]),
            "wd_sh": np.ascontiguousarray(wd[:, 1024 * c:1024 * (c + 1)]),
            "lnw_bc": lnw,
            "w1_sh": np.ascontiguousarray(w1p[768 * c:768 * (c + 1), :]),
            "w2_sh": np.ascontiguousarray(w2p[768 * c:768 * (c + 1), :]),
            "w3_sh": np.ascontiguousarray(w3[256 * c:256 * (c + 1), :]),
            "mask_in": _mask_for_core(hf),
            "ident_in": ident,
            "nf4c_in": nf4c,
        })
    if _NC_CACHE[0] is None:
        _NC_CACHE[0] = build_program()
    nc = _NC_CACHE[0]
    r = run_bass_kernel_spmd(nc, in_maps, core_ids=list(range(8)))
    out = np.empty((4, 2048, 2048), np.float32)
    for c in range(8):
        b, hf = c // 2, c % 2
        out[b][bidx[hf]] = r.results[c]["out"]
    return out


if __name__ == "__main__":
    nc = build_program()
    print("build ok")


# version 08
# speedup vs baseline: 1.3548x; 1.3548x over previous
import numpy as np
import ml_dtypes
import concourse.bass as bass
import concourse.mybir as mybir
from concourse.bass_utils import run_bass_kernel_spmd

F32 = mybir.dt.float32
BF16 = mybir.dt.bfloat16
AF = mybir.ActivationFunctionType
ALU = mybir.AluOpType
AX = mybir.AxisListType

MAGIC = float(np.float32(3 << 22))
NEG = -1.0e30
C_ATTN = float(128.0 ** -0.5)

NF4 = np.array([-1.0, -0.6962, -0.5251, -0.3949, -0.2844, -0.1848, -0.0911,
                0.0, 0.0796, 0.1609, 0.2461, 0.3379, 0.4407, 0.5626, 0.723,
                1.0], dtype=ml_dtypes.bfloat16).astype(np.float64)
NF4_MID = ((NF4[:-1] + NF4[1:]) / 2.0).astype(np.float32)
NF4_DELTA = np.diff(NF4).astype(np.float32)
NF4_BASE = float(NF4[0])
NBLK_REAL = 5632 * 2048 // 64

ENGINES = ("sync", "gpsimd", "tensor", "vector", "scalar")

SM_MEAN0, SM_RS0, SM_C1X, SM_ISX = 0, 8, 16, 24
SM_MEAN1, SM_RS1 = 48, 56
SM_G = 64
SM_THR = 72
SM_NTHR = 81
SM_NF4M, SM_NF4D = 96, 112
SM_BQ = 128
SM_ONE = 176
SM_T = 178
SM_SCG, SM_SCU, SM_C3, SM_VAR = 224, 232, 240, 248
SM_ROW = 256
SM_AT = 384
SM_SB1, SM_SB2, SM_QT = 1024, 1280, 1536



def _tl(i):
    if i == 0:
        return list(range(16))
    lo = [t for t in (i - 1, i, i + 1) if 0 <= t <= 7]
    return sorted(set([0] + lo + [8 + t for t in lo]))


TLS = [_tl(i) for i in range(8)]


def _segs(tl):
    runs = []
    for s, t in enumerate(tl):
        if runs and runs[-1][0] + runs[-1][2] == s \
                and runs[-1][1] + runs[-1][2] == t:
            runs[-1][2] += 1
        else:
            runs.append([s, t, 1])
    segs = []
    for s0, t0, L in runs:
        s = s0
        while s < s0 + L:
            e = min(s0 + L, (s // 4 + 1) * 4)
            segs.append((s, t0 + (s - s0), e - s))
            s = e
    return segs

AA_COLS = 50176
WB_COLS = 12288


class Prog:
    def __init__(self):
        self.recs = {e: [] for e in ENGINES}
        self.cnt = {}

    def emit(self, eng, fn, waits=(), inc=None, n=1):
        self.recs[eng].append((fn, tuple(waits), inc, n))
        if inc is not None:
            self.cnt[inc] = self.cnt.get(inc, 0) + n
        return self.cnt.get(inc)


class Res:
    def __init__(self):
        self.w = {}
        self.r = {}

    def rd(self, name):
        return [self.w[name]] if name in self.w else []

    def wr(self, name):
        ws = list(self.r.get(name, []))
        if name in self.w:
            ws.append(self.w[name])
        return ws

    def did_rd(self, name, sem, cnt):
        self.r.setdefault(name, []).append((sem, cnt))

    def did_wr(self, name, sem, cnt):
        self.w[name] = (sem, cnt)
        self.r[name] = []


def merge_waits(*wls):
    best = {}
    for wl in wls:
        for s, v in wl:
            if v > best.get(s, -1):
                best[s] = v
    return list(best.items())


def build_program():
    nc = bass.Bass("TRN2", target_bir_lowering=False, debug=False)
    p = Prog()
    res = Res()

    x_in = nc.dram_tensor("x_in", [1024, 2048], F32, kind="ExternalInput").ap()
    wq_d = nc.dram_tensor("wq", [2048, 2048], F32, kind="ExternalInput").ap()
    wk_d = nc.dram_tensor("wk", [2048, 2048], F32, kind="ExternalInput").ap()
    wv_d = nc.dram_tensor("wv", [2048, 2048], F32, kind="ExternalInput").ap()
    wo_d = nc.dram_tensor("wo", [2048, 2048], F32, kind="ExternalInput").ap()
    wqkv_d = nc.dram_tensor("wqkv", [2048, 6144], F32, kind="ExternalInput").ap()
    wout_d = nc.dram_tensor("wout", [2048, 2048], F32, kind="ExternalInput").ap()
    bqkvT_d = nc.dram_tensor("bqkvT", [128, 48], F32, kind="ExternalInput").ap()
    bout_d = nc.dram_tensor("bout_bc", [128, 2048], BF16, kind="ExternalInput").ap()
    wg_sh = nc.dram_tensor("wg_sh", [1024, 2048], F32, kind="ExternalInput").ap()
    wu_sh = nc.dram_tensor("wu_sh", [1024, 2048], F32, kind="ExternalInput").ap()
    wd_sh = nc.dram_tensor("wd_sh", [2048, 1024], F32, kind="ExternalInput").ap()
    lnw_d = nc.dram_tensor("lnw_bc", [128, 8192], F32, kind="ExternalInput").ap()
    w1_sh = nc.dram_tensor("w1_sh", [768, 2048], BF16, kind="ExternalInput").ap()
    w2_sh = nc.dram_tensor("w2_sh", [768, 2048], BF16, kind="ExternalInput").ap()
    w3_sh = nc.dram_tensor("w3_sh", [256, 5632], BF16, kind="ExternalInput").ap()
    mask_d = nc.dram_tensor("mask_in", [8, 128, 2048], BF16, kind="ExternalInput").ap()
    ident_d = nc.dram_tensor("ident_in", [128, 128], BF16, kind="ExternalInput").ap()
    nf4c_d = nc.dram_tensor("nf4c_in", [128, 32], F32, kind="ExternalInput").ap()
    out_d = nc.dram_tensor("out", [1024, 2048], F32, kind="ExternalOutput").ap()

    def idram(name, shape, dtype=BF16):
        return nc.dram_tensor(name, shape, dtype).ap()

    ar_in = idram("ar_in", [128, 8], F32)
    ar_out = idram("ar_out", [128, 8], F32)
    s_rt = idram("s_rt", [3, 768, 32], F32)
    r_rt = idram("r_rt", [3, 768, 32], F32)
    d_rt = idram("d_rt", [3, 768, 32], BF16)
    def sdram(name, shape, dtype=BF16):
        return nc.dram_tensor(name, shape, dtype, addr_space="Shared").ap()

    wgT_in = idram("wgT_in", [2048, 1024]); wgT_ag = sdram("wgT_ag", [16384, 1024])
    wuT_in = idram("wuT_in", [2048, 1024]); wuT_ag = sdram("wuT_ag", [16384, 1024])
    wdT_in = idram("wdT_in", [1024, 2048]); wdT_ag = sdram("wdT_ag", [8192, 2048])
    w1T_in = idram("w1T_in", [2048, 768]); w1T_ag = sdram("w1T_ag", [16384, 768])
    w2T_in = idram("w2T_in", [2048, 768]); w2T_ag = sdram("w2T_ag", [16384, 768])
    w3T_in = idram("w3T_in", [5632, 256]); w3T_ag = sdram("w3T_ag", [45056, 256])
    kT_in = [idram("kT0_in", [2048, 1024]), idram("kT1_in", [2048, 1024])]
    kT_ag = [idram("kT0_ag", [4096, 1024]), idram("kT1_ag", [4096, 1024])]
    vT_in = [idram("vT0_in", [2048, 1024]), idram("vT1_in", [2048, 1024])]
    vT_ag = [idram("vT0_ag", [4096, 1024]), idram("vT1_ag", [4096, 1024])]
    qT_dr = [idram("qT0_dr", [2048, 1024]), idram("qT1_dr", [2048, 1024])]

    from contextlib import ExitStack
    ctx = ExitStack()
    E = ctx.enter_context
    x_sb = E(nc.sbuf_tensor("x_sb", [128, 16384], F32))
    AA = E(nc.sbuf_tensor("AA", [128, AA_COLS], BF16))
    WB = E(nc.sbuf_tensor("WB", [128, WB_COLS], BF16))
    SM = E(nc.sbuf_tensor("SM", [128, 2048], F32))
    FS = E(nc.sbuf_tensor("FS", [128, 2048], F32))
    IDT = E(nc.sbuf_tensor("IDT", [128, 128], BF16))
    ps_big = E(nc.psum_tensor("ps_big", [128, 2048], F32))
    ps_a = E(nc.psum_tensor("ps_a", [128, 1024], F32))
    ps_b = E(nc.psum_tensor("ps_b", [128, 1024], F32))
    sems = {}
    for s in ("cc", "pe", "ve", "se"):
        sems[s] = E(nc.semaphore(s))

    def getsem(name):
        if name not in sems:
            sems[name] = E(nc.semaphore(name))
        return sems[name]

    ENG_SEM = {"tensor": "pe", "vector": "ve", "scalar": "se",
               "gpsimd": "sw", "sync": "hw"}
    PAIRS = [[0, 1], [2, 3], [4, 5], [6, 7]]
    ALL8 = [list(range(8))]

    def aanames(c0, n):
        return tuple(f"aa{k}" for k in range(c0 // 4096, (c0 + n - 1) // 4096 + 1))

    def wbnames(c0, n):
        return tuple(f"wb{k}" for k in range(c0 // 1024, (c0 + n - 1) // 1024 + 1))

    def dma(engname, out_ap, in_ap, reads=(), writes=(), extra=()):
        assert writes, "dma needs a write-resource for its stream semaphore"
        semname = ("dg_" if engname == "gpsimd" else "d_") + writes[0]
        getsem(semname)
        waits = merge_waits(*([res.rd(r) for r in reads]
                              + [res.wr(w) for w in writes] + [list(extra)]))
        cnt = p.emit(engname,
                     (lambda o, i: lambda e: e.dma_start(out=o, in_=i))(out_ap, in_ap),
                     waits=waits, inc=semname, n=16)
        for r in reads:
            res.did_rd(r, semname, cnt)
        for w in writes:
            res.did_wr(w, semname, cnt)
        return cnt

    def op(engname, fn, reads=(), writes=(), extra=(), sem=None):
        semname = sem or ENG_SEM[engname]
        getsem(semname)
        waits = merge_waits(*([res.rd(r) for r in reads]
                              + [res.wr(w) for w in writes] + [list(extra)]))
        cnt = p.emit(engname, fn, waits=waits, inc=semname, n=1)
        for r in reads:
            res.did_rd(r, semname, cnt)
        for w in writes:
            res.did_wr(w, semname, cnt)
        return cnt

    def mmf(o, l, r, st, sp):
        return lambda e: e.matmul(o, l, r, start=st, stop=sp)

    def mm_group(items, reads=(), writes=(), extra=()):
        waits = merge_waits(*([res.rd(r) for r in reads]
                              + [res.wr(w) for w in writes] + [list(extra)]))
        for it in items[:-1]:
            p.emit("tensor", it, waits=waits)
            waits = ()
        cnt = p.emit("tensor", items[-1], waits=waits, inc="pe", n=1)
        for r in reads:
            res.did_rd(r, "pe", cnt)
        for w in writes:
            res.did_wr(w, "pe", cnt)
        return cnt

    def aav(c0, n):
        return AA[:, c0:c0 + n]

    def wbv(c0, n):
        return WB[:, c0:c0 + n]

    def smc(col, n=1):
        return SM[:, col:col + n]

    def ag(in_ap, out_ap, rname, wname, groups):
        op("gpsimd", (lambda i_, o_, g_: lambda e: e.collective_compute(
            "AllGather", ALU.bypass, replica_groups=g_,
            ins=[i_.opt()], outs=[o_.opt()]))(in_ap, out_ap, groups),
            reads=(rname,), writes=(wname,), sem="cc")

    dma("sync", x_sb[:].rearrange("p (m f) -> p m f", f=2048),
        x_in.rearrange("(m p) f -> p m f", p=128), writes=("x",))
    dma("sync", IDT[:], ident_d, writes=("idt",))
    dma("sync", SM[:, SM_NF4M:SM_NF4M + 32], nf4c_d, writes=("st",))
    dma("sync", SM[:, SM_BQ:SM_BQ + 48], bqkvT_d, writes=("st",))
    op("vector", lambda e: e.memset(smc(SM_ONE), 1.0), writes=("st",))
    op("vector", lambda e: e.memset(SM[0:1, SM_ROW:SM_ROW + 128], 1.0),
       writes=("st",))

    SCR = 36864

    def emit_ln(layer):
        mean_c = SM_MEAN0 if layer == 0 else SM_MEAN1
        rs_c = SM_RS0 if layer == 0 else SM_RS1
        for m in range(8):
            xt = x_sb[:, 2048 * m:2048 * (m + 1)]
            op("vector", (lambda xt_, c: lambda e: e.reduce_sum(
                smc(c), xt_, axis=AX.X))(xt, mean_c + m),
                reads=("x",), writes=("st",))
            op("vector", (lambda c: lambda e: e.tensor_scalar_mul(
                smc(c), smc(c), 1.0 / 2048.0))(mean_c + m), writes=("st",))
            op("scalar", (lambda xt_: lambda e: e.activation(
                aav(SCR, 2048), xt_, AF.Square, accum_out=smc(SM_T)))(xt),
                reads=("x",), writes=aanames(SCR, 2048) + ("st",))
            op("vector", (lambda c: lambda e: e.tensor_tensor(
                smc(SM_T + 1), smc(c), smc(c), ALU.mult))(mean_c + m),
                writes=("st",))
            op("vector", lambda e: e.tensor_scalar(
                smc(SM_T), smc(SM_T), 1.0 / 2048.0, 1e-5, ALU.mult, ALU.add),
                writes=("st",))
            op("vector", lambda e: e.tensor_sub(
                smc(SM_T), smc(SM_T), smc(SM_T + 1)), writes=("st",))
            op("scalar", lambda e: e.activation(smc(SM_T), smc(SM_T), AF.Ln),
               reads=("st",), writes=("st",))
            op("scalar", (lambda c: lambda e: e.activation(
                smc(c), smc(SM_T), AF.Exp, scale=-0.5))(rs_c + m),
                reads=("st",), writes=("st",))
            op("vector", (lambda xt_, mc, rc: lambda e: e.scalar_tensor_tensor(
                out=aav(SCR, 2048), in0=xt_, scalar=smc(mc),
                in1=smc(rc).broadcast_to([128, 2048]), op0=ALU.subtract,
                op1=ALU.mult))(xt, mean_c + m, rs_c + m),
                reads=("x", "st"), writes=aanames(SCR, 2048))
            if layer == 0:
                op("vector", lambda e: e.reduce_max(
                    smc(SM_T), aav(SCR, 2048), axis=AX.X,
                    apply_absolute_value=True),
                    reads=aanames(SCR, 2048), writes=("st",))
                op("vector", lambda e: e.tensor_scalar_max(
                    smc(SM_T), smc(SM_T), 1e-5), writes=("st",))
                op("vector", (lambda mi: lambda e: e.tensor_scalar_mul(
                    smc(SM_ISX + mi), smc(SM_T), 1.0 / 127.0))(m),
                    writes=("st",))
                op("vector", lambda e: e.reciprocal(
                    smc(SM_T + 1), smc(SM_T)), writes=("st",))
                op("vector", (lambda mi: lambda e: e.scalar_tensor_tensor(
                    out=smc(SM_C1X + mi), in0=smc(SM_T + 1), scalar=127.0,
                    in1=smc(SM_RS0 + mi), op0=ALU.mult, op1=ALU.mult))(m),
                    writes=("st",))
            items = [mmf(ps_big[:, 128 * ft:128 * (ft + 1)],
                         aav(SCR + 128 * ft, 128), IDT[:], True, True)
                     for ft in range(16)]
            mm_group(items, reads=aanames(SCR, 2048) + ("idt",),
                     writes=("ps_big",))
            op("vector", (lambda mi: lambda e: e.tensor_copy(
                AA[:, 0:16384].rearrange("p (t q) -> p t q", q=1024)
                [:, :, 128 * mi:128 * (mi + 1)],
                ps_big[:].rearrange("p (t q) -> p t q", q=128)))(m),
                reads=("ps_big",), writes=aanames(0, 16384))

    emit_ln(0)

    for colz in range(8):
        op("vector", (lambda c: lambda e: e.memset(smc(SM_G + c), 0.0))(colz),
           writes=("st",))

    def tern_pass1(dram, gcol, tiles, width):
        for t in range(tiles):
            dma("sync", FS[:, 0:width], dram[128 * t:128 * (t + 1), :],
                writes=("fs",))
            op("vector", (lambda w: lambda e: e.reduce_sum(
                smc(SM_T + 2), FS[:, 0:w], axis=AX.X,
                apply_absolute_value=True))(width),
                reads=("fs",), writes=("st",))
            op("vector", (lambda c: lambda e: e.tensor_add(
                smc(SM_G + c), smc(SM_G + c), smc(SM_T + 2)))(gcol),
                writes=("st",))

    tern_pass1(wg_sh, 0, 8, 2048)
    tern_pass1(wu_sh, 1, 8, 2048)
    tern_pass1(wd_sh, 2, 16, 1024)

    NF4S = [(w1_sh, 0, 1024, 6, 2048, 3), (w2_sh, 1, 1216, 6, 2048, 4),
            (w3_sh, 2, 1408, 2, 5632, 5)]
    for dram, widx, scol, tiles, width, gcol in NF4S:
        nblk = width // 64
        for t in range(tiles):
            nsub = (width + 2047) // 2048
            for s in range(nsub):
                c0 = 2048 * s
                w = min(2048, width - c0)
                dma("sync", wbv(0, w), dram[128 * t:128 * (t + 1), c0:c0 + w],
                    writes=wbnames(0, w))
                op("vector", (lambda w_, sc: lambda e: e.reduce_max(
                    SM[:, sc:sc + w_ // 64].unsqueeze(2),
                    WB[:, 0:w_].rearrange("p (b i) -> p b i", i=64),
                    axis=AX.X, apply_absolute_value=True))(
                        w, scol + t * nblk + c0 // 64),
                    reads=wbnames(0, w), writes=("smsb",))
        op("vector", (lambda sc, n: lambda e: e.reduce_sum(
            smc(SM_T + 2), SM[:, sc:sc + n], axis=AX.X))(scol, tiles * nblk),
            reads=("smsb",), writes=("st",))
        op("vector", (lambda c: lambda e: e.tensor_add(
            smc(SM_G + c), smc(SM_G + c), smc(SM_T + 2)))(gcol),
            writes=("st",))

    dma("sync", ar_in, SM[:, SM_G:SM_G + 8], reads=("st",), writes=("ar_in",))
    op("gpsimd", lambda e: e.collective_compute(
        "AllReduce", ALU.add, replica_groups=ALL8,
        ins=[ar_in.opt()], outs=[ar_out.opt()]),
        reads=("ar_in",), writes=("ar_out",), sem="cc")
    dma("sync", SM[:, SM_G:SM_G + 8], ar_out, reads=("ar_out",), writes=("st",))
    op("tensor", lambda e: e.matmul(ps_a[0:1, 0:8], smc(SM_ONE),
                                    SM[:, SM_G:SM_G + 8], start=True, stop=True),
       reads=("st",), writes=("ps_a",))
    op("vector", lambda e: e.tensor_copy(SM[0:1, SM_T + 4:SM_T + 12],
                                         ps_a[0:1, 0:8]),
       reads=("ps_a",), writes=("st",))
    op("tensor", lambda e: e.matmul(ps_a[:, 0:8], SM[0:1, SM_ROW:SM_ROW + 128],
                                    SM[0:1, SM_T + 4:SM_T + 12],
                                    start=True, stop=True),
       reads=("st",), writes=("ps_a",))
    op("vector", lambda e: e.tensor_copy(SM[:, SM_G:SM_G + 8], ps_a[:, 0:8]),
       reads=("ps_a",), writes=("st",))
    NW = float(8192 * 2048)
    for j in range(3):
        op("vector", (lambda jj: lambda e: e.tensor_scalar(
            smc(SM_THR + 3 + jj), smc(SM_G + jj), 1.0 / NW, 1e-5,
            ALU.mult, ALU.max))(j), writes=("st",))
        op("vector", (lambda jj: lambda e: e.tensor_scalar_mul(
            smc(SM_THR + jj), smc(SM_THR + 3 + jj), 0.5))(j), writes=("st",))
        op("vector", (lambda jj: lambda e: e.tensor_scalar_mul(
            smc(SM_NTHR + jj), smc(SM_THR + jj), -1.0))(j), writes=("st",))
        op("vector", (lambda jj: lambda e: e.tensor_scalar_mul(
            smc(SM_THR + 6 + jj), smc(SM_G + 3 + jj), 1.0 / NBLK_REAL))(j),
            writes=("st",))
        op("scalar", (lambda jj: lambda e: e.copy(
            wbv(11264 + jj, 1), smc(SM_THR + 6 + jj)))(j),
            reads=("st",), writes=("wbm",))
        op("scalar", (lambda jj: lambda e: e.copy(
            smc(SM_NTHR + 3 + jj), wbv(11264 + jj, 1)))(j),
            reads=("wbm",), writes=("st",))

    def emit_proj(layer):
        srcs = {"q": (wq_d, 0), "k": (wk_d, 0), "v": (wv_d, 0)} if layer == 0 \
            else {"q": (wqkv_d, 0), "k": (wqkv_d, 2048), "v": (wqkv_d, 4096)}
        dsts = {"k": kT_in[layer], "v": vT_in[layer], "q": qT_dr[layer]}
        for which in ("k", "v", "q"):
            wsrc, coff = srcs[which]
            for t in range(16):
                slot = 18432 + 2048 * (t % 2)
                dma("gpsimd",
                    aav(slot, 2048).rearrange("p (ft c) -> p ft c", c=128),
                    wsrc[:, coff + 128 * t:coff + 128 * (t + 1)]
                    .rearrange("(ft p) c -> p ft c", p=128),
                    writes=aanames(slot, 2048))
                items = []
                for n2 in range(2):
                    for ft in range(16):
                        items.append(mmf(
                            ps_a[:, 512 * n2:512 * (n2 + 1)],
                            aav(slot + 128 * ft, 128),
                            aav(1024 * ft + 512 * n2, 512),
                            ft == 0, ft == 15))
                mm_group(items, reads=aanames(slot, 2048) + aanames(0, 16384),
                         writes=("ps_a",))
                stg = 22528 + 1024 * (t % 2)
                if layer == 1:
                    boff = {"q": 0, "k": 16, "v": 32}[which] + t
                    op("scalar", (lambda st_, bo: lambda e: e.activation(
                        aav(st_, 1024), ps_a[:], AF.Identity,
                        bias=smc(SM_BQ + bo)))(stg, boff),
                        reads=("ps_a", "st"), writes=aanames(stg, 1024))
                else:
                    op("scalar", (lambda st_: lambda e: e.copy(
                        aav(st_, 1024), ps_a[:]))(stg),
                        reads=("ps_a",), writes=aanames(stg, 1024))
                dma("sync", dsts[which][128 * t:128 * (t + 1), :],
                    aav(stg, 1024), reads=aanames(stg, 1024),
                    writes=(f"{which}T{layer}_in",))
            if which in ("k", "v"):
                agi = kT_in[layer] if which == "k" else vT_in[layer]
                ago = kT_ag[layer] if which == "k" else vT_ag[layer]
                ag(agi, ago, f"{which}T{layer}_in", f"{which}T{layer}_ag",
                   PAIRS)

    emit_proj(0)

    def transpose_tile_out(src_c0, width, stage_c0, reads):
        nt = width // 128
        items = [mmf(ps_big[:, 128 * j:128 * (j + 1)],
                     wbv(src_c0 + 128 * j, 128), IDT[:], True, True)
                 for j in range(nt)]
        mm_group(items, reads=tuple(reads) + ("idt",), writes=("ps_big",))
        op("vector", (lambda w: lambda e: e.tensor_copy(
            wbv(stage_c0, w), ps_big[:, 0:w]))(width),
            reads=("ps_big",), writes=wbnames(stage_c0, width))

    def tern_pass2(dram, gidx, tiles, width, dst, dstname):
        for t in range(tiles):
            dma("sync", FS[:, 0:width], dram[128 * t:128 * (t + 1), :],
                writes=("fs",))
            op("vector", (lambda w, g: lambda e: e.tensor_single_scalar(
                wbv(0, w), FS[:, 0:w], smc(SM_THR + g), ALU.is_gt))(width, gidx),
                reads=("fs", "st"), writes=wbnames(0, width))
            op("vector", (lambda w, g: lambda e: e.tensor_scalar(
                wbv(2048, w), FS[:, 0:w], smc(SM_NTHR + g), -1.0,
                ALU.is_gt, ALU.add))(width, gidx),
                reads=("fs", "st"), writes=wbnames(2048, width))
            op("vector", (lambda w: lambda e: e.tensor_add(
                wbv(4096, w), wbv(0, w), wbv(2048, w)))(width),
                reads=wbnames(0, width) + wbnames(2048, width),
                writes=wbnames(4096, width))
            transpose_tile_out(4096, width, 8192, wbnames(4096, width))
            dma("sync",
                dst[:, 128 * t:128 * (t + 1)]
                .rearrange("(f p) c -> p f c", p=128),
                wbv(8192, width).rearrange("p (f c) -> p f c", c=128),
                reads=wbnames(8192, width), writes=(dstname,))

    tern_pass2(wg_sh, 0, 8, 2048, wgT_in, "wgT_in")
    ag(wgT_in, wgT_ag, "wgT_in", "wgT_ag", ALL8)
    tern_pass2(wu_sh, 1, 8, 2048, wuT_in, "wuT_in")
    ag(wuT_in, wuT_ag, "wuT_in", "wuT_ag", ALL8)
    tern_pass2(wd_sh, 2, 16, 1024, wdT_in, "wdT_in")
    ag(wdT_in, wdT_ag, "wdT_in", "wdT_ag", ALL8)

    def nf4_pass2(dram, widx, tiles, width, scol, dst, dstname):
        nblk = width // 64
        nsc = tiles * nblk
        nsb = nsc * 128 // 256
        P = nsb
        flat = nsc * 128
        rowc = nblk if width == 5632 else 32
        dma("sync", SM[0:P, SM_SB1:SM_SB1 + 256],
            s_rt[widx].flatten()[0:flat].rearrange("(s c) -> s c", c=256),
            reads=("s_rt",), writes=("smsb1",))
        op("vector", (lambda P_, w_: lambda e: e.tensor_single_scalar(
            SM[0:P_, SM_SB2:SM_SB2 + 256], SM[0:P_, SM_SB1:SM_SB1 + 256],
            SM[0:P_, SM_THR + 6 + w_:SM_THR + 7 + w_], ALU.subtract))(P, widx),
            reads=("smsb1", "st"), writes=("smsb2",))
        op("vector", (lambda P_: lambda e: e.reduce_max(
            SM[0:P_, SM_T + 4:SM_T + 5], SM[0:P_, SM_SB2:SM_SB2 + 256],
            axis=AX.X, apply_absolute_value=True))(P),
            reads=("smsb2",), writes=("st",))
        op("vector", (lambda P_: lambda e: e.reciprocal(
            SM[0:P_, SM_T + 5:SM_T + 6], SM[0:P_, SM_T + 4:SM_T + 5]))(P),
            writes=("st",))
        op("vector", (lambda P_: lambda e: e.tensor_scalar_mul(
            SM[0:P_, SM_T + 5:SM_T + 6], SM[0:P_, SM_T + 5:SM_T + 6],
            128.0))(P), writes=("st",))
        op("vector", (lambda P_: lambda e: e.tensor_scalar(
            SM[0:P_, SM_SB2:SM_SB2 + 256], SM[0:P_, SM_SB2:SM_SB2 + 256],
            SM[0:P_, SM_T + 5:SM_T + 6], MAGIC, ALU.mult, ALU.add))(P),
            reads=("st",), writes=("smsb2",))
        op("vector", (lambda P_: lambda e: e.tensor_scalar(
            SM[0:P_, SM_SB2:SM_SB2 + 256], SM[0:P_, SM_SB2:SM_SB2 + 256],
            -MAGIC, -128.0, ALU.add, ALU.max))(P), writes=("smsb2",))
        op("vector", (lambda P_: lambda e: e.tensor_scalar_min(
            SM[0:P_, SM_SB2:SM_SB2 + 256], SM[0:P_, SM_SB2:SM_SB2 + 256],
            127.0))(P), writes=("smsb2",))
        op("vector", (lambda P_: lambda e: e.tensor_scalar_mul(
            SM[0:P_, SM_T + 4:SM_T + 5], SM[0:P_, SM_T + 4:SM_T + 5],
            1.0 / 128.0))(P), writes=("st",))
        op("vector", (lambda P_: lambda e: e.tensor_single_scalar(
            WB[0:P_, 8192:8448], SM[0:P_, SM_SB2:SM_SB2 + 256],
            SM[0:P_, SM_T + 4:SM_T + 5], ALU.mult))(P),
            reads=("smsb2", "st"), writes=wbnames(8192, 256))
        op("vector", (lambda P_, w_: lambda e: e.tensor_single_scalar(
            WB[0:P_, 8192:8448], WB[0:P_, 8192:8448],
            SM[0:P_, SM_NTHR + 3 + w_:SM_NTHR + 4 + w_], ALU.add))(P, widx),
            reads=("st",), writes=wbnames(8192, 256))
        op("vector", (lambda P_: lambda e: e.tensor_scalar_max(
            SM[0:P_, SM_SB1:SM_SB1 + 256], SM[0:P_, SM_SB1:SM_SB1 + 256],
            1e-30))(P), writes=("smsb1",))
        op("vector", (lambda P_: lambda e: e.reciprocal(
            SM[0:P_, SM_SB2:SM_SB2 + 256], SM[0:P_, SM_SB1:SM_SB1 + 256]))(P),
            reads=("smsb1",), writes=("smsb2",))
        dma("sync", d_rt[widx].flatten()[0:flat].rearrange(
            "(s c) -> s c", c=256)[0:P, :], WB[0:P, 8192:8448],
            reads=wbnames(8192, 256), writes=("d_rt",))
        dma("sync", r_rt[widx].flatten()[0:flat].rearrange(
            "(s c) -> s c", c=256)[0:P, :], SM[0:P, SM_SB2:SM_SB2 + 256],
            reads=("smsb2",), writes=("r_rt",))
        dma("sync", WB[:, 8448:8448 + nsc].rearrange(
            "p (t c) -> p t c", c=rowc),
            d_rt[widx].flatten()[0:flat].rearrange(
                "(t p c) -> p t c", p=128, c=rowc),
            reads=("d_rt",), writes=wbnames(8448, nsc))
        dma("sync", FS[:, 0:nsc].rearrange("p (t c) -> p t c", c=rowc),
            r_rt[widx].flatten()[0:flat].rearrange(
                "(t p c) -> p t c", p=128, c=rowc),
            reads=("r_rt",), writes=("fs",))
        for t in range(tiles):
            nsub = (width + 2047) // 2048
            for s in range(nsub):
                c0 = 2048 * s
                w = min(2048, width - c0)
                bc = t * nblk + c0 // 64
                dma("sync", wbv(0, w), dram[128 * t:128 * (t + 1), c0:c0 + w],
                    writes=wbnames(0, w))
                op("vector", (lambda w_, b_: lambda e: e.tensor_tensor(
                    wbv(2048, w_).rearrange("p (b i) -> p b i", i=64),
                    wbv(0, w_).rearrange("p (b i) -> p b i", i=64),
                    FS[:, b_:b_ + w_ // 64].unsqueeze(2)
                    .broadcast_to([128, w_ // 64, 64]), ALU.mult))(w, bc),
                    reads=wbnames(0, w) + ("fs",), writes=wbnames(2048, w))
                op("vector", (lambda w_: lambda e: e.tensor_scalar(
                    wbv(4096, w_), wbv(2048, w_), smc(SM_NF4M),
                    float(NF4_DELTA[0]), ALU.is_gt, ALU.mult))(w),
                    reads=wbnames(2048, w) + ("st",), writes=wbnames(4096, w))
                op("vector", (lambda w_: lambda e: e.tensor_scalar_add(
                    wbv(4096, w_), wbv(4096, w_), NF4_BASE))(w),
                    writes=wbnames(4096, w))
                for kk in range(1, 15):
                    op("vector", (lambda w_, k_: lambda e: e.tensor_scalar(
                        wbv(6144, w_), wbv(2048, w_), smc(SM_NF4M + k_),
                        float(NF4_DELTA[k_]), ALU.is_gt, ALU.mult))(w, kk),
                        reads=wbnames(2048, w) + ("st",),
                        writes=wbnames(6144, w))
                    op("vector", (lambda w_: lambda e: e.tensor_add(
                        wbv(4096, w_), wbv(4096, w_), wbv(6144, w_)))(w),
                        reads=wbnames(6144, w), writes=wbnames(4096, w))
                op("vector", (lambda w_, b_: lambda e: e.tensor_tensor(
                    wbv(0, w_).rearrange("p (b i) -> p b i", i=64),
                    wbv(4096, w_).rearrange("p (b i) -> p b i", i=64),
                    WB[:, 8448 + b_:8448 + b_ + w_ // 64].unsqueeze(2)
                    .broadcast_to([128, w_ // 64, 64]), ALU.mult))(w, bc),
                    reads=wbnames(4096, w) + wbnames(8448, nsc),
                    writes=wbnames(0, w))
                transpose_tile_out(0, w, 9472, wbnames(0, w))
                dma("sync",
                    dst[c0:c0 + w, 128 * t:128 * (t + 1)]
                    .rearrange("(f p) c -> p f c", p=128),
                    wbv(9472, w).rearrange("p (f c) -> p f c", c=128),
                    reads=wbnames(9472, w), writes=(dstname,))

    for widx, scol, tiles, width in ((0, 1024, 6, 2048), (1, 1216, 6, 2048),
                                     (2, 1408, 2, 5632)):
        nblk = width // 64
        nsc = tiles * nblk
        flat = nsc * 128
        rowc = nblk if width == 5632 else 32
        dma("sync",
            s_rt[widx].flatten()[0:flat].rearrange(
                "(t p c) -> p t c", p=128, c=rowc),
            SM[:, scol:scol + nsc].rearrange("p (t c) -> p t c", c=rowc),
            reads=("smsb",), writes=("s_rt",))

    nf4_pass2(w1_sh, 0, 6, 2048, 1024, w1T_in, "w1T_in")
    ag(w1T_in, w1T_ag, "w1T_in", "w1T_ag", ALL8)
    nf4_pass2(w2_sh, 1, 6, 2048, 1216, w2T_in, "w2T_in")
    ag(w2T_in, w2T_ag, "w2T_in", "w2T_ag", ALL8)
    nf4_pass2(w3_sh, 2, 2, 5632, 1408, w3T_in, "w3T_in")
    ag(w3T_in, w3T_ag, "w3T_in", "w3T_ag", ALL8)

    def emit_ffn0():
        for m in range(8):
            op("vector", (lambda mi: lambda e: e.tensor_tensor(
                smc(SM_SCG + mi), smc(SM_ISX + mi), smc(SM_THR + 3),
                ALU.mult))(m), writes=("st",))
            op("vector", (lambda mi: lambda e: e.tensor_tensor(
                smc(SM_SCU + mi), smc(SM_ISX + mi), smc(SM_THR + 4),
                ALU.mult))(m), writes=("st",))
        for th in range(2):
            ms = [4 * th + i for i in range(4)]
            hslot = [8192, 16384, 24576, 32768]
            for i, m in enumerate(ms):
                xt = x_sb[:, 2048 * m:2048 * (m + 1)]
                for cchunk in range(4):
                    c0 = 512 * cchunk
                    op("vector", (lambda xt_, mi, c_: lambda e:
                        e.scalar_tensor_tensor(
                            out=SM[:, SM_QT:SM_QT + 512],
                            in0=xt_[:, c_:c_ + 512],
                            scalar=smc(SM_MEAN0 + mi),
                            in1=smc(SM_C1X + mi).broadcast_to([128, 512]),
                            op0=ALU.subtract, op1=ALU.mult))(xt, m, c0),
                        reads=("x", "st"), writes=("smqt",))
                    op("vector", lambda e: e.tensor_scalar(
                        SM[:, SM_QT:SM_QT + 512], SM[:, SM_QT:SM_QT + 512],
                        MAGIC, -MAGIC, ALU.add, ALU.add), writes=("smqt",))
                    op("vector", (lambda c_: lambda e: e.tensor_copy(
                        aav(SCR + c_, 512), SM[:, SM_QT:SM_QT + 512]))(c0),
                        reads=("smqt",), writes=aanames(SCR, 2048))
                items = [mmf(ps_big[:, 128 * ft:128 * (ft + 1)],
                             aav(SCR + 128 * ft, 128), IDT[:], True, True)
                         for ft in range(16)]
                mm_group(items, reads=aanames(SCR, 2048) + ("idt",),
                         writes=("ps_big",))
                op("vector", (lambda ii: lambda e: e.tensor_copy(
                    AA[:, 0:8192].rearrange("p (t q) -> p t q", q=512)
                    [:, :, 128 * ii:128 * (ii + 1)],
                    ps_big[:].rearrange("p (t q) -> p t q", q=128)))(i),
                    reads=("ps_big",), writes=aanames(0, 8192))
            for m in ms:
                op("vector", (lambda mi: lambda e: e.memset(
                    smc(SM_VAR + mi), 0.0))(m), writes=("st",))
            for i in range(4):
                op("vector", (lambda ii: lambda e: e.memset(
                    smc(SM_T + 12 + ii), 0.0))(i), writes=("st",))
            for n in range(16):
                r, off = (512 * n) // 1024, (512 * n) % 1024
                dma("sync", WB[:, 0:8192].rearrange("p (f c) -> p f c", c=512),
                    wgT_ag[2048 * r:2048 * (r + 1), off:off + 512]
                    .rearrange("(f p) c -> p f c", p=128),
                    writes=wbnames(0, 8192), extra=res.rd("wgT_ag"))
                dma("sync", AA[:, 40960:49152].rearrange(
                    "p (f c) -> p f c", c=512),
                    wuT_ag[2048 * r:2048 * (r + 1), off:off + 512]
                    .rearrange("(f p) c -> p f c", p=128),
                    writes=aanames(40960, 8192), extra=res.rd("wuT_ag"))
                dma("gpsimd", wbv(8960, 512),
                    lnw_d[:, 512 * n:512 * (n + 1)],
                    writes=wbnames(8960, 512))
                for i, m in enumerate(ms):
                    items = [mmf(ps_a[:, 0:512], aav(512 * ft + 128 * i, 128),
                                 wbv(512 * ft, 512), ft == 0, ft == 15)
                             for ft in range(16)]
                    mm_group(items, reads=aanames(0, 8192) + wbnames(0, 8192),
                             writes=("ps_a",))
                    items = [mmf(ps_b[:, 0:512], aav(512 * ft + 128 * i, 128),
                                 aav(40960 + 512 * ft, 512), ft == 0, ft == 15)
                             for ft in range(16)]
                    mm_group(items,
                             reads=aanames(0, 8192) + aanames(40960, 8192),
                             writes=("ps_b",))
                    op("scalar", (lambda mi: lambda e: e.activation(
                        wbv(8192, 512), ps_a[:, 0:512], AF.Silu,
                        scale=smc(SM_SCG + mi)))(m),
                        reads=("ps_a", "st"), writes=wbnames(8192, 512))
                    hc = hslot[i] + 512 * n
                    op("vector", (lambda mi, hc_: lambda e:
                        e.scalar_tensor_tensor(
                            out=aav(hc_, 512), in0=ps_b[:, 0:512],
                            scalar=smc(SM_SCU + mi), in1=wbv(8192, 512),
                            op0=ALU.mult, op1=ALU.mult))(m, hc),
                        reads=("ps_b", "st") + wbnames(8192, 512),
                        writes=aanames(hc, 512))
                    op("scalar", (lambda hc_: lambda e: e.activation(
                        wbv(9984, 512), aav(hc_, 512), AF.Square,
                        accum_out=smc(SM_T + 2)))(hc),
                        reads=aanames(hc, 512),
                        writes=wbnames(9984, 512) + ("st",))
                    op("vector", (lambda mi: lambda e: e.tensor_add(
                        smc(SM_VAR + mi), smc(SM_VAR + mi),
                        smc(SM_T + 2)))(m), writes=("st",))
                    op("vector", (lambda hc_: lambda e: e.tensor_tensor(
                        SM[:, 1024:1536], aav(hc_, 512),
                        wbv(8960, 512), ALU.mult))(hc),
                        reads=aanames(hc, 512) + wbnames(8960, 512),
                        writes=("smsb1", "smsb2"))
                    op("vector", lambda e: e.reduce_max(
                        smc(392), SM[:, 1024:1536], axis=AX.X,
                        apply_absolute_value=True),
                        reads=("smsb1", "smsb2"), writes=("sfg",))
                    op("vector", (lambda ii: lambda e: e.tensor_max(
                        smc(SM_T + 12 + ii), smc(SM_T + 12 + ii),
                        smc(392)))(i), reads=("sfg",), writes=("st",))
            for i, m in enumerate(ms):
                op("vector", (lambda mi: lambda e: e.tensor_scalar(
                    smc(SM_T + 2), smc(SM_VAR + mi), 1.0 / 8192.0, 1e-6,
                    ALU.mult, ALU.add))(m), writes=("st",))
                op("scalar", lambda e: e.activation(
                    smc(SM_T + 2), smc(SM_T + 2), AF.Ln),
                    reads=("st",), writes=("st",))
                op("scalar", lambda e: e.activation(
                    smc(SM_T + 3), smc(SM_T + 2), AF.Exp, scale=-0.5),
                    reads=("st",), writes=("st",))
                op("vector", (lambda ii: lambda e: e.tensor_tensor(
                    smc(SM_T + 2), smc(SM_T + 3), smc(SM_T + 12 + ii),
                    ALU.mult))(i), writes=("st",))
                op("vector", lambda e: e.tensor_scalar_max(
                    smc(SM_T + 2), smc(SM_T + 2), 1e-5), writes=("st",))
                op("vector", (lambda mi: lambda e: e.tensor_scalar_mul(
                    smc(SM_C3 + mi), smc(SM_T + 2), 1.0 / 127.0))(m),
                    writes=("st",))
                op("vector", lambda e: e.reciprocal(
                    smc(SM_T + 2), smc(SM_T + 2)), writes=("st",))
                op("vector", lambda e: e.scalar_tensor_tensor(
                    out=smc(SM_T + 2), in0=smc(SM_T + 2), scalar=127.0,
                    in1=smc(SM_T + 3), op0=ALU.mult, op1=ALU.mult),
                    writes=("st",))
                op("vector", (lambda mi: lambda e: e.tensor_tensor(
                    smc(SM_C3 + mi), smc(SM_C3 + mi), smc(SM_THR + 5),
                    ALU.mult))(m), writes=("st",))
                for n in range(16):
                    hc = hslot[i] + 512 * n
                    eng = "vector" if n % 2 == 0 else "gpsimd"
                    qc = SM_QT if n % 2 == 0 else 1024
                    qn = ("smqt",) if n % 2 == 0 else ("smsb1", "smsb2")
                    lc = 8960 if n % 2 == 0 else 9984
                    dma("gpsimd", wbv(lc, 512),
                        lnw_d[:, 512 * n:512 * (n + 1)],
                        writes=wbnames(lc, 512))
                    op("vector", (lambda hc_, qc_, lc_: lambda e: e.tensor_tensor(
                        SM[:, qc_:qc_ + 512], aav(hc_, 512),
                        wbv(lc_, 512), ALU.mult))(hc, qc, lc),
                        reads=aanames(hc, 512) + wbnames(lc, 512),
                        writes=qn)
                    op(eng, (lambda qc_: lambda e: e.tensor_scalar(
                        SM[:, qc_:qc_ + 512], SM[:, qc_:qc_ + 512],
                        smc(SM_T + 2), MAGIC, ALU.mult, ALU.add))(qc),
                        reads=("st",), writes=qn)
                    op(eng, (lambda hc_, qc_: lambda e: e.tensor_scalar_add(
                        aav(hc_, 512), SM[:, qc_:qc_ + 512], -MAGIC))(hc, qc),
                        reads=qn, writes=aanames(hc, 512))
                for batch in range(4):
                    c0 = hslot[i] + 2048 * batch
                    items = [mmf(ps_big[:, 128 * j:128 * (j + 1)],
                                 aav(c0 + 128 * j, 128), IDT[:], True, True)
                             for j in range(16)]
                    mm_group(items, reads=aanames(c0, 2048) + ("idt",),
                             writes=("ps_big",))
                    op("vector", (lambda c_: lambda e: e.tensor_copy(
                        aav(c_, 2048), ps_big[:]))(c0),
                        reads=("ps_big",), writes=aanames(c0, 2048))
            for dg in range(8):
                wdc = []
                for jj in range(8):
                    d = 8 * dg + jj
                    dst_c = (2048 * jj) if jj < 4 else (40960 + 2048 * (jj - 4))
                    wdc.append(dst_c)
                    dma("sync", aav(dst_c, 2048),
                        wdT_ag[128 * d:128 * (d + 1), :],
                        writes=aanames(dst_c, 2048),
                        extra=res.rd("wdT_ag"))
                for i, m in enumerate(ms):
                    items = []
                    for jj in range(8):
                        d = 8 * dg + jj
                        for n4 in range(4):
                            items.append(mmf(
                                ps_big[:, 512 * n4:512 * (n4 + 1)],
                                aav(hslot[i] + 128 * d, 128),
                                aav(wdc[jj] + 512 * n4, 512),
                                jj == 0, jj == 7))
                    rdn = aanames(hslot[i], 8192) + aanames(0, 8192) \
                        + aanames(40960, 8192)
                    mm_group(items, reads=rdn, writes=("ps_big",))
                    op("vector", (lambda mi: lambda e: e.scalar_tensor_tensor(
                        out=x_sb[:, 2048 * mi:2048 * (mi + 1)],
                        in0=ps_big[:], scalar=smc(SM_C3 + mi),
                        in1=x_sb[:, 2048 * mi:2048 * (mi + 1)],
                        op0=ALU.mult, op1=ALU.add))(m),
                        reads=("ps_big", "st", "x"), writes=("x",))

    emit_ffn0()

    def emit_attn(layer):
        use_mask = layer == 0
        wo_src = wo_d if layer == 0 else wout_d
        kag, vag, qdr = kT_ag[layer], vT_ag[layer], qT_dr[layer]
        kname, vname = f"kT{layer}_ag", f"vT{layer}_ag"
        qname = f"qT{layer}_in"
        tls = TLS if layer == 0 else [list(range(16))] * 8
        offs = [0]
        for i in range(8):
            offs.append(offs[-1] + 128 * len(tls[i]))
        if layer == 1:
            dma("sync", wbv(0, 2048), bout_d, writes=wbnames(0, 2048))
            for m in range(8):
                op("vector", (lambda mi: lambda e: e.tensor_add(
                    x_sb[:, 2048 * mi:2048 * (mi + 1)],
                    x_sb[:, 2048 * mi:2048 * (mi + 1)], wbv(0, 2048)))(m),
                    reads=wbnames(0, 2048) + ("x",), writes=("x",))

        def psb_names(qt):
            if layer == 1 or len(tls[qt]) > 8:
                return ("ps_big", "psbg0", "psbg1")
            return (f"psbg{qt % 2}",)

        def softmax_stage(qt):
            tl = tls[qt]
            nt = len(tl)
            W = 128 * nt
            par = qt % 2
            dense = nt > 8
            st0 = SM_AT + 4 * par
            if layer == 1:
                pc = 6144 + 2048 * par
                po = 0
            else:
                pc = 6144 if dense else 6144 + 1024 * par
                po = 0 if dense else 1024 * par
            mc = 8192 if dense else 8192 + 1024 * par
            psn = psb_names(qt)
            san = (f"sa{par}",)
            items = []
            for s0, t0, L in _segs(tl):
                items.append(mmf(ps_big[:, po + 128 * s0:po + 128 * (s0 + L)],
                                 wbv(10240 + 128 * qt, 128),
                                 wbv(128 * t0, 128 * L), True, True))
            mm_group(items, reads=wbnames(10240, 1024) + wbnames(0, 2048),
                     writes=psn)
            if use_mask:
                dma("sync", wbv(mc, W), mask_d[qt][:, 0:W],
                    writes=wbnames(mc, W))
                op("vector", (lambda W_, mc_, po_: lambda e: e.tensor_add(
                    ps_big[:, po_:po_ + W_], ps_big[:, po_:po_ + W_],
                    wbv(mc_, W_)))(W, mc, po),
                    reads=wbnames(mc, W), writes=psn)
            op("vector", (lambda W_, c, po_: lambda e: e.reduce_max(
                smc(c), ps_big[:, po_:po_ + W_], axis=AX.X))(W, st0, po),
                reads=psn, writes=san)
            op("vector", (lambda c: lambda e: e.tensor_scalar_mul(
                smc(c + 1), smc(c), -C_ATTN))(st0), writes=san)
            op("scalar", (lambda W_, c, pc_, po_: lambda e: e.activation(
                wbv(pc_, W_), ps_big[:, po_:po_ + W_], AF.Exp,
                bias=smc(c + 1), scale=C_ATTN,
                accum_out=smc(c + 2)))(W, st0, pc, po),
                reads=psn + san, writes=wbnames(pc, W) + san)
            op("vector", (lambda c: lambda e: e.reciprocal(
                smc(c + 3), smc(c + 2)))(st0), writes=san)
            op("scalar", (lambda W_, c, pc_: lambda e: e.activation(
                wbv(pc_, W_), wbv(pc_, W_), AF.Identity,
                scale=smc(c + 3)))(W, st0, pc),
                reads=san, writes=wbnames(pc, W))

        def transpose_stage(qt):
            tl = tls[qt]
            nt = len(tl)
            par = qt % 2
            dense = nt > 8
            if layer == 1:
                pc = 6144 + 2048 * par
            else:
                pc = 6144 if dense else 6144 + 1024 * par
            if layer == 0:
                for w0 in range(0, nt, 8):
                    nw = min(8, nt - w0)
                    items = [mmf(ps_b[:, 128 * j:128 * (j + 1)],
                                 wbv(pc + 128 * (w0 + j), 128), IDT[:],
                                 True, True) for j in range(nw)]
                    mm_group(items, reads=wbnames(pc, 128 * nt) + ("idt",),
                             writes=("ps_b",))
                    dst = offs[qt] + 128 * w0
                    if (qt + w0) % 2 == 0:
                        op("scalar", (lambda d_, n_: lambda e: e.copy(
                            AA[:, d_:d_ + 128 * n_],
                            ps_b[:, 0:128 * n_]))(dst, nw),
                            reads=("ps_b",), writes=aanames(dst, 128 * nw))
                    else:
                        op("vector", (lambda d_, n_: lambda e: e.tensor_copy(
                            AA[:, d_:d_ + 128 * n_],
                            ps_b[:, 0:128 * n_]))(dst, nw),
                            reads=("ps_b",), writes=aanames(dst, 128 * nw))
            else:
                items = [mmf(ps_a[:, 128 * j:128 * (j + 1)],
                             wbv(pc + 128 * j, 128), IDT[:], True, True)
                         for j in range(8)]
                mm_group(items, reads=wbnames(pc, 2048) + ("idt",),
                         writes=("ps_a",))
                items = [mmf(ps_b[:, 128 * j:128 * (j + 1)],
                             wbv(pc + 1024 + 128 * j, 128), IDT[:],
                             True, True) for j in range(8)]
                mm_group(items, reads=wbnames(pc, 2048) + ("idt",),
                         writes=("ps_b",))
                op("scalar", (lambda q_: lambda e: e.copy(
                    AA[:, 0:16384].rearrange("p (j q) -> p j q", q=1024)
                    [:, 0:8, 128 * q_:128 * (q_ + 1)],
                    ps_a[:].rearrange("p (j q) -> p j q", q=128)))(qt),
                    reads=("ps_a",), writes=aanames(0, 8192))
                op("vector", (lambda q_: lambda e: e.tensor_copy(
                    AA[:, 0:16384].rearrange("p (j q) -> p j q", q=1024)
                    [:, 8:16, 128 * q_:128 * (q_ + 1)],
                    ps_b[:].rearrange("p (j q) -> p j q", q=128)))(qt),
                    reads=("ps_b",), writes=aanames(8192, 8192))

        for g in range(4):
            for hh in range(4):
                dma("gpsimd", aav(20480 + 2048 * hh, 2048),
                    wo_src[512 * g + 128 * hh:512 * g + 128 * (hh + 1), :],
                    writes=aanames(20480 + 2048 * hh, 2048))
            for hh in range(4):
                h = 4 * g + hh
                for half in range(2):
                    dma("sync", wbv(1024 * half, 1024),
                        kag[2048 * half + 128 * h:2048 * half + 128 * (h + 1), :],
                        writes=wbnames(1024 * half, 1024),
                        extra=res.rd(kname))
                    dma("sync", wbv(2048 + 1024 * half, 1024),
                        vag[2048 * half + 128 * h:2048 * half + 128 * (h + 1), :],
                        writes=wbnames(2048 + 1024 * half, 1024),
                        extra=res.rd(vname))
                dma("sync", wbv(10240, 1024), qdr[128 * h:128 * (h + 1), :],
                    writes=wbnames(10240, 1024), extra=res.rd(qname))
                items = [mmf(ps_a[:, 128 * j:128 * (j + 1)],
                             wbv(2048 + 128 * j, 128), IDT[:], True, True)
                         for j in range(8)]
                mm_group(items, reads=wbnames(2048, 2048) + ("idt",),
                         writes=("ps_a",))
                items = [mmf(ps_b[:, 128 * j:128 * (j + 1)],
                             wbv(2048 + 1024 + 128 * j, 128), IDT[:],
                             True, True) for j in range(8)]
                mm_group(items, reads=wbnames(2048, 2048) + ("idt",),
                         writes=("ps_b",))
                op("scalar", lambda e: e.copy(wbv(4096, 1024),
                                              ps_a[:]),
                   reads=("ps_a",), writes=wbnames(4096, 1024))
                op("vector", lambda e: e.tensor_copy(wbv(4096 + 1024, 1024),
                                                     ps_b[:]),
                   reads=("ps_b",), writes=wbnames(4096 + 1024, 1024))
                first_def = 1 if layer == 0 else 0
                for qt in range(8):
                    softmax_stage(qt)
                    if qt == 0 and first_def == 1:
                        transpose_stage(0)
                    elif qt > first_def:
                        transpose_stage(qt - 1)
                transpose_stage(7)
                if layer == 0:
                    for qt in range(8):
                        tl = tls[qt]
                        items = []
                        for s, t in enumerate(tl):
                            items.append(mmf(
                                ps_a[:, 128 * qt:128 * (qt + 1)],
                                wbv(4096 + 128 * t, 128),
                                AA[:, offs[qt] + 128 * s:
                                   offs[qt] + 128 * (s + 1)],
                                s == 0, s == len(tl) - 1))
                        mm_group(items,
                                 reads=aanames(offs[qt], 128 * len(tl))
                                 + wbnames(4096, 2048), writes=("ps_a",))
                else:
                    items = []
                    for j in range(16):
                        for n2 in range(2):
                            items.append(mmf(
                                ps_a[:, 512 * n2:512 * (n2 + 1)],
                                wbv(4096 + 128 * j, 128),
                                aav(1024 * j + 512 * n2, 512),
                                j == 0, j == 15))
                    mm_group(items,
                             reads=aanames(0, 16384) + wbnames(4096, 2048),
                             writes=("ps_a",))
                op("scalar", (lambda hh_: lambda e: e.copy(
                    aav(16384 + 1024 * hh_, 1024), ps_a[:]))(hh),
                    reads=("ps_a",), writes=aanames(16384 + 1024 * hh, 1024))
            for m in range(8):
                items = []
                for hh in range(4):
                    for n4 in range(4):
                        items.append(mmf(
                            ps_big[:, 512 * n4:512 * (n4 + 1)],
                            aav(16384 + 1024 * hh + 128 * m, 128),
                            aav(20480 + 2048 * hh + 512 * n4, 512),
                            hh == 0, hh == 3))
                mm_group(items, reads=aanames(16384, 4096) + aanames(20480, 8192),
                         writes=("ps_big", "psbg0", "psbg1"))
                op("vector", (lambda mi: lambda e: e.tensor_add(
                    x_sb[:, 2048 * mi:2048 * (mi + 1)],
                    x_sb[:, 2048 * mi:2048 * (mi + 1)], ps_big[:]))(m),
                    reads=("ps_big", "x"), writes=("x",))

    emit_attn(0)

    emit_ln(1)
    emit_proj(1)

    def emit_ffn1():
        for qd in range(4):
            dls = list(range(11 * qd, 11 * qd + 11))
            for idx, dl in enumerate(dls):
                r, off = (128 * dl) // 768, (128 * dl) % 768
                dma("sync", wbv(0, 2048).rearrange("p (f c) -> p f c", c=128),
                    w1T_ag[2048 * r:2048 * (r + 1), off:off + 128]
                    .rearrange("(f p) c -> p f c", p=128),
                    writes=wbnames(0, 2048), extra=res.rd("w1T_ag"))
                dma("sync", wbv(2048, 2048).rearrange("p (f c) -> p f c", c=128),
                    w2T_ag[2048 * r:2048 * (r + 1), off:off + 128]
                    .rearrange("(f p) c -> p f c", p=128),
                    writes=wbnames(2048, 2048), extra=res.rd("w2T_ag"))
                items = []
                for n2 in range(2):
                    for ft in range(16):
                        items.append(mmf(
                            ps_a[:, 512 * n2:512 * (n2 + 1)],
                            wbv(128 * ft, 128),
                            aav(1024 * ft + 512 * n2, 512),
                            ft == 0, ft == 15))
                mm_group(items, reads=wbnames(0, 2048) + aanames(0, 16384),
                         writes=("ps_a",))
                items = []
                for n2 in range(2):
                    for ft in range(16):
                        items.append(mmf(
                            ps_b[:, 512 * n2:512 * (n2 + 1)],
                            wbv(2048 + 128 * ft, 128),
                            aav(1024 * ft + 512 * n2, 512),
                            ft == 0, ft == 15))
                mm_group(items, reads=wbnames(2048, 2048) + aanames(0, 16384),
                         writes=("ps_b",))
                op("scalar", lambda e: e.activation(
                    wbv(4096, 1024), ps_a[:], AF.Silu),
                    reads=("ps_a",), writes=wbnames(4096, 1024))
                hc = 16384 + 1024 * idx
                op("vector", (lambda hc_: lambda e: e.tensor_tensor(
                    aav(hc_, 1024), wbv(4096, 1024), ps_b[:], ALU.mult))(hc),
                    reads=wbnames(4096, 1024) + ("ps_b",),
                    writes=aanames(hc, 1024))
            for idx, dl in enumerate(dls):
                wc = 27648 + 2048 * idx
                dma("sync", aav(wc, 2048).rearrange("p (r c) -> p r c", c=256),
                    w3T_ag.rearrange("(r d) c -> d r c", d=5632)
                    [128 * dl:128 * (dl + 1)],
                    writes=aanames(wc, 2048), extra=res.rd("w3T_ag"))
            for m in range(8):
                items = []
                for idx in range(11):
                    for n4 in range(4):
                        items.append(mmf(
                            ps_big[:, 512 * n4:512 * (n4 + 1)],
                            aav(16384 + 1024 * idx + 128 * m, 128),
                            aav(27648 + 2048 * idx + 512 * n4, 512),
                            idx == 0, idx == 10))
                mm_group(items,
                         reads=aanames(16384, 11264) + aanames(27648, 22528),
                         writes=("ps_big",))
                op("vector", (lambda mi: lambda e: e.tensor_add(
                    x_sb[:, 2048 * mi:2048 * (mi + 1)],
                    x_sb[:, 2048 * mi:2048 * (mi + 1)], ps_big[:]))(m),
                    reads=("ps_big", "x"), writes=("x",))

    emit_ffn1()
    emit_attn(1)

    fin = dma("sync", out_d.rearrange("(m p) f -> p m f", p=128),
              x_sb[:].rearrange("p (m f) -> p m f", f=2048), reads=("x",),
              writes=("outd",))
    p.emit("sync", lambda e: None, waits=[("d_outd", fin)])

    with nc.Block() as block:
        def mk(engname):
            recs = p.recs[engname]

            def body(e, recs=recs):
                for fn, waits, inc, n in recs:
                    for s, v in waits:
                        e.wait_ge(sems[s], v)
                    instr = fn(e)
                    if inc is not None:
                        instr.then_inc(sems[inc], n)
            return body

        block.sync(mk("sync"))
        block.gpsimd(mk("gpsimd"))
        block.tensor(mk("tensor"))
        block.vector(mk("vector"))
        block.scalar(mk("scalar"))

    ctx.close()
    return nc


def _blk_idx(r):
    return np.concatenate([np.arange(128 * (2 * i + r), 128 * (2 * i + r) + 128)
                           for i in range(8)])


def _mask_for_core(r):
    out = np.zeros((8, 128, 2048), np.float32)
    for i in range(8):
        tl = TLS[i]
        q = (128 * (2 * i + r) + np.arange(128))[:, None]
        k = np.concatenate([128 * (2 * (t % 8) + t // 8) + np.arange(128)
                            for t in tl])[None, :]
        allow = (np.abs(q - k) < 256) | (q < 16) | (k < 16)
        out[i, :, 0:128 * len(tl)] = np.where(allow, np.float32(0.0),
                                              np.float32(NEG))
    return out.astype(ml_dtypes.bfloat16)


_NC_CACHE = [None]


def kernel(L0_attn_hidden_states, L0_attn_Wq, L0_attn_Wk, L0_attn_Wv,
           L0_attn_Wo, L0_ffn_w_gate, L0_ffn_w_up, L0_ffn_w_down,
           L0_ffn_ln_weight, L1_attn_w_qkv, L1_attn_b_qkv, L1_attn_w_out,
           L1_attn_b_out, L1_ffn_weight1, L1_ffn_weight2, L1_ffn_weight3):
    xs = np.asarray(L0_attn_hidden_states)
    f32 = np.float32
    bq = np.ascontiguousarray(np.asarray(L1_attn_b_qkv, f32)
                              .reshape(48, 128).T)
    bout = np.broadcast_to(np.asarray(L1_attn_b_out).astype(ml_dtypes.bfloat16),
                           (128, 2048)).copy()
    lnw = np.broadcast_to(np.asarray(L0_ffn_ln_weight, f32), (128, 8192)).copy()
    ident = np.eye(128, dtype=ml_dtypes.bfloat16)
    nf4c = np.zeros((32,), f32)
    nf4c[0:15] = NF4_MID
    nf4c[16:31] = NF4_DELTA
    nf4c = np.broadcast_to(nf4c, (128, 32)).copy()
    w1p = np.zeros((6144, 2048), ml_dtypes.bfloat16)
    w1p[0:5632] = np.asarray(L1_ffn_weight1)
    w2p = np.zeros((6144, 2048), ml_dtypes.bfloat16)
    w2p[0:5632] = np.asarray(L1_ffn_weight2)
    w3 = np.asarray(L1_ffn_weight3)
    wg = np.asarray(L0_ffn_w_gate, f32)
    wu = np.asarray(L0_ffn_w_up, f32)
    wd = np.asarray(L0_ffn_w_down, f32)

    bidx = [_blk_idx(0), _blk_idx(1)]
    in_maps = []
    for c in range(8):
        b, hf = c // 2, c % 2
        in_maps.append({
            "x_in": np.ascontiguousarray(xs[b][bidx[hf]], f32),
            "wq": np.asarray(L0_attn_Wq, f32),
            "wk": np.asarray(L0_attn_Wk, f32),
            "wv": np.asarray(L0_attn_Wv, f32),
            "wo": np.asarray(L0_attn_Wo, f32),
            "wqkv": np.asarray(L1_attn_w_qkv, f32),
            "wout": np.asarray(L1_attn_w_out, f32),
            "bqkvT": bq,
            "bout_bc": bout,
            "wg_sh": np.ascontiguousarray(wg[1024 * c:1024 * (c + 1), :]),
            "wu_sh": np.ascontiguousarray(wu[1024 * c:1024 * (c + 1), :]),
            "wd_sh": np.ascontiguousarray(wd[:, 1024 * c:1024 * (c + 1)]),
            "lnw_bc": lnw,
            "w1_sh": np.ascontiguousarray(w1p[768 * c:768 * (c + 1), :]),
            "w2_sh": np.ascontiguousarray(w2p[768 * c:768 * (c + 1), :]),
            "w3_sh": np.ascontiguousarray(w3[256 * c:256 * (c + 1), :]),
            "mask_in": _mask_for_core(hf),
            "ident_in": ident,
            "nf4c_in": nf4c,
        })
    if _NC_CACHE[0] is None:
        _NC_CACHE[0] = build_program()
    nc = _NC_CACHE[0]
    r = run_bass_kernel_spmd(nc, in_maps, core_ids=list(range(8)))
    out = np.empty((4, 2048, 2048), np.float32)
    for c in range(8):
        b, hf = c // 2, c % 2
        out[b][bidx[hf]] = r.results[c]["out"]
    return out


if __name__ == "__main__":
    nc = build_program()
    print("build ok")


# version 10
# speedup vs baseline: 1.4037x; 1.0361x over previous
import numpy as np
import ml_dtypes
import concourse.bass as bass
import concourse.mybir as mybir
from concourse.bass_utils import run_bass_kernel_spmd

F32 = mybir.dt.float32
BF16 = mybir.dt.bfloat16
AF = mybir.ActivationFunctionType
ALU = mybir.AluOpType
AX = mybir.AxisListType

MAGIC = float(np.float32(3 << 22))
NEG = -1.0e30
C_ATTN = float(128.0 ** -0.5)

NF4 = np.array([-1.0, -0.6962, -0.5251, -0.3949, -0.2844, -0.1848, -0.0911,
                0.0, 0.0796, 0.1609, 0.2461, 0.3379, 0.4407, 0.5626, 0.723,
                1.0], dtype=ml_dtypes.bfloat16).astype(np.float64)
NF4_MID = ((NF4[:-1] + NF4[1:]) / 2.0).astype(np.float32)
NF4_DELTA = np.diff(NF4).astype(np.float32)
NF4_BASE = float(NF4[0])
NBLK_REAL = 5632 * 2048 // 64

ENGINES = ("sync", "gpsimd", "tensor", "vector", "scalar")

SM_MEAN0, SM_RS0, SM_C1X, SM_ISX = 0, 8, 16, 24
SM_MEAN1, SM_RS1 = 48, 56
SM_G = 64
SM_THR = 72
SM_NTHR = 81
SM_NF4M, SM_NF4D = 96, 112
SM_BQ = 128
SM_ONE = 176
SM_T = 178
SM_SCG, SM_SCU, SM_C3, SM_VAR = 224, 232, 240, 248
SM_ROW = 256
SM_AT = 384
SM_SB1, SM_SB2, SM_QT = 1024, 1280, 1536



def _tl(i):
    if i == 0:
        return list(range(16))
    lo = [t for t in (i - 1, i, i + 1) if 0 <= t <= 7]
    return sorted(set([0] + lo + [8 + t for t in lo]))


TLS = [_tl(i) for i in range(8)]


def _segs(tl):
    runs = []
    for s, t in enumerate(tl):
        if runs and runs[-1][0] + runs[-1][2] == s \
                and runs[-1][1] + runs[-1][2] == t:
            runs[-1][2] += 1
        else:
            runs.append([s, t, 1])
    segs = []
    for s0, t0, L in runs:
        s = s0
        while s < s0 + L:
            e = min(s0 + L, (s // 4 + 1) * 4)
            segs.append((s, t0 + (s - s0), e - s))
            s = e
    return segs

AA_COLS = 50176
WB_COLS = 12288


class Prog:
    def __init__(self):
        self.recs = {e: [] for e in ENGINES}
        self.cnt = {}

    def emit(self, eng, fn, waits=(), inc=None, n=1):
        self.recs[eng].append((fn, tuple(waits), inc, n))
        if inc is not None:
            self.cnt[inc] = self.cnt.get(inc, 0) + n
        return self.cnt.get(inc)


class Res:
    def __init__(self):
        self.w = {}
        self.r = {}

    def rd(self, name):
        return [self.w[name]] if name in self.w else []

    def wr(self, name):
        ws = list(self.r.get(name, []))
        if name in self.w:
            ws.append(self.w[name])
        return ws

    def did_rd(self, name, sem, cnt):
        self.r.setdefault(name, []).append((sem, cnt))

    def did_wr(self, name, sem, cnt):
        self.w[name] = (sem, cnt)
        self.r[name] = []


def merge_waits(*wls):
    best = {}
    for wl in wls:
        for s, v in wl:
            if v > best.get(s, -1):
                best[s] = v
    return list(best.items())


def build_program():
    nc = bass.Bass("TRN2", target_bir_lowering=False, debug=False)
    p = Prog()
    res = Res()

    x_in = nc.dram_tensor("x_in", [1024, 2048], F32, kind="ExternalInput").ap()
    wq_d = nc.dram_tensor("wq", [2048, 2048], F32, kind="ExternalInput").ap()
    wk_d = nc.dram_tensor("wk", [2048, 2048], F32, kind="ExternalInput").ap()
    wv_d = nc.dram_tensor("wv", [2048, 2048], F32, kind="ExternalInput").ap()
    wo_d = nc.dram_tensor("wo", [2048, 2048], F32, kind="ExternalInput").ap()
    wqkv_d = nc.dram_tensor("wqkv", [2048, 6144], F32, kind="ExternalInput").ap()
    wout_d = nc.dram_tensor("wout", [2048, 2048], F32, kind="ExternalInput").ap()
    bqkvT_d = nc.dram_tensor("bqkvT", [128, 48], F32, kind="ExternalInput").ap()
    bout_d = nc.dram_tensor("bout_bc", [128, 2048], BF16, kind="ExternalInput").ap()
    wg_sh = nc.dram_tensor("wg_sh", [1024, 2048], F32, kind="ExternalInput").ap()
    wu_sh = nc.dram_tensor("wu_sh", [1024, 2048], F32, kind="ExternalInput").ap()
    wd_sh = nc.dram_tensor("wd_sh", [2048, 1024], F32, kind="ExternalInput").ap()
    lnw_d = nc.dram_tensor("lnw_bc", [128, 8192], F32, kind="ExternalInput").ap()
    w1_sh = nc.dram_tensor("w1_sh", [768, 2048], BF16, kind="ExternalInput").ap()
    w2_sh = nc.dram_tensor("w2_sh", [768, 2048], BF16, kind="ExternalInput").ap()
    w3_sh = nc.dram_tensor("w3_sh", [256, 5632], BF16, kind="ExternalInput").ap()
    mask_d = nc.dram_tensor("mask_in", [8, 128, 2048], BF16, kind="ExternalInput").ap()
    ident_d = nc.dram_tensor("ident_in", [128, 128], BF16, kind="ExternalInput").ap()
    nf4c_d = nc.dram_tensor("nf4c_in", [128, 32], F32, kind="ExternalInput").ap()
    out_d = nc.dram_tensor("out", [1024, 2048], F32, kind="ExternalOutput").ap()

    def idram(name, shape, dtype=BF16):
        return nc.dram_tensor(name, shape, dtype).ap()

    ar_in = idram("ar_in", [128, 8], F32)
    ar_out = idram("ar_out", [128, 8], F32)
    s_rt = idram("s_rt", [3, 768, 32], F32)
    r_rt = idram("r_rt", [3, 768, 32], F32)
    d_rt = idram("d_rt", [3, 768, 32], BF16)
    def sdram(name, shape, dtype=BF16):
        return nc.dram_tensor(name, shape, dtype, addr_space="Shared").ap()

    wgT_in = idram("wgT_in", [2048, 1024]); wgT_ag = sdram("wgT_ag", [16384, 1024])
    wuT_in = idram("wuT_in", [2048, 1024]); wuT_ag = sdram("wuT_ag", [16384, 1024])
    wdT_in = idram("wdT_in", [1024, 2048]); wdT_ag = sdram("wdT_ag", [8192, 2048])
    w1T_in = idram("w1T_in", [2048, 768]); w1T_ag = sdram("w1T_ag", [16384, 768])
    w2T_in = idram("w2T_in", [2048, 768]); w2T_ag = sdram("w2T_ag", [16384, 768])
    w3T_in = idram("w3T_in", [5632, 256]); w3T_ag = sdram("w3T_ag", [45056, 256])
    kT_in = [idram("kT0_in", [2048, 1024]), idram("kT1_in", [2048, 1024])]
    kT_ag = [idram("kT0_ag", [4096, 1024]), idram("kT1_ag", [4096, 1024])]
    vT_in = [idram("vT0_in", [2048, 1024]), idram("vT1_in", [2048, 1024])]
    vT_ag = [idram("vT0_ag", [4096, 1024]), idram("vT1_ag", [4096, 1024])]
    qT_dr = [idram("qT0_dr", [2048, 1024]), idram("qT1_dr", [2048, 1024])]

    from contextlib import ExitStack
    ctx = ExitStack()
    E = ctx.enter_context
    x_sb = E(nc.sbuf_tensor("x_sb", [128, 16384], F32))
    AA = E(nc.sbuf_tensor("AA", [128, AA_COLS], BF16))
    WB = E(nc.sbuf_tensor("WB", [128, WB_COLS], BF16))
    SM = E(nc.sbuf_tensor("SM", [128, 2048], F32))
    FS = E(nc.sbuf_tensor("FS", [128, 2048], F32))
    IDT = E(nc.sbuf_tensor("IDT", [128, 128], BF16))
    ps_big = E(nc.psum_tensor("ps_big", [128, 2048], F32))
    ps_a = E(nc.psum_tensor("ps_a", [128, 1024], F32))
    ps_b = E(nc.psum_tensor("ps_b", [128, 1024], F32))
    sems = {}
    for s in ("cc", "pe", "ve", "se"):
        sems[s] = E(nc.semaphore(s))

    def getsem(name):
        if name not in sems:
            sems[name] = E(nc.semaphore(name))
        return sems[name]

    ENG_SEM = {"tensor": "pe", "vector": "ve", "scalar": "se",
               "gpsimd": "sw", "sync": "hw"}
    PAIRS = [[0, 1], [2, 3], [4, 5], [6, 7]]
    ALL8 = [list(range(8))]

    def aanames(c0, n):
        return tuple(f"aa{k}" for k in range(c0 // 4096, (c0 + n - 1) // 4096 + 1))

    def wbnames(c0, n):
        return tuple(f"wb{k}" for k in range(c0 // 1024, (c0 + n - 1) // 1024 + 1))

    def dma(engname, out_ap, in_ap, reads=(), writes=(), extra=()):
        assert writes, "dma needs a write-resource for its stream semaphore"
        semname = ("dg_" if engname == "gpsimd" else "d_") + writes[0]
        getsem(semname)
        waits = merge_waits(*([res.rd(r) for r in reads]
                              + [res.wr(w) for w in writes] + [list(extra)]))
        cnt = p.emit(engname,
                     (lambda o, i: lambda e: e.dma_start(out=o, in_=i))(out_ap, in_ap),
                     waits=waits, inc=semname, n=16)
        for r in reads:
            res.did_rd(r, semname, cnt)
        for w in writes:
            res.did_wr(w, semname, cnt)
        return cnt

    def op(engname, fn, reads=(), writes=(), extra=(), sem=None):
        semname = sem or ENG_SEM[engname]
        getsem(semname)
        waits = merge_waits(*([res.rd(r) for r in reads]
                              + [res.wr(w) for w in writes] + [list(extra)]))
        cnt = p.emit(engname, fn, waits=waits, inc=semname, n=1)
        for r in reads:
            res.did_rd(r, semname, cnt)
        for w in writes:
            res.did_wr(w, semname, cnt)
        return cnt

    def mmf(o, l, r, st, sp):
        return lambda e: e.matmul(o, l, r, start=st, stop=sp)

    def mm_group(items, reads=(), writes=(), extra=()):
        waits = merge_waits(*([res.rd(r) for r in reads]
                              + [res.wr(w) for w in writes] + [list(extra)]))
        for it in items[:-1]:
            p.emit("tensor", it, waits=waits)
            waits = ()
        cnt = p.emit("tensor", items[-1], waits=waits, inc="pe", n=1)
        for r in reads:
            res.did_rd(r, "pe", cnt)
        for w in writes:
            res.did_wr(w, "pe", cnt)
        return cnt

    def aav(c0, n):
        return AA[:, c0:c0 + n]

    def wbv(c0, n):
        return WB[:, c0:c0 + n]

    def smc(col, n=1):
        return SM[:, col:col + n]

    def ag(in_ap, out_ap, rname, wname, groups):
        op("gpsimd", (lambda i_, o_, g_: lambda e: e.collective_compute(
            "AllGather", ALU.bypass, replica_groups=g_,
            ins=[i_.opt()], outs=[o_.opt()]))(in_ap, out_ap, groups),
            reads=(rname,), writes=(wname,), sem="cc")

    dma("sync", x_sb[:].rearrange("p (m f) -> p m f", f=2048),
        x_in.rearrange("(m p) f -> p m f", p=128), writes=("x",))
    dma("sync", IDT[:], ident_d, writes=("idt",))
    dma("sync", SM[:, SM_NF4M:SM_NF4M + 32], nf4c_d, writes=("st",))
    dma("sync", SM[:, SM_BQ:SM_BQ + 48], bqkvT_d, writes=("st",))
    op("vector", lambda e: e.memset(smc(SM_ONE), 1.0), writes=("st",))
    op("vector", lambda e: e.memset(SM[0:1, SM_ROW:SM_ROW + 128], 1.0),
       writes=("st",))

    SCR = 36864

    def emit_ln(layer):
        mean_c = SM_MEAN0 if layer == 0 else SM_MEAN1
        rs_c = SM_RS0 if layer == 0 else SM_RS1
        for m in range(8):
            xt = x_sb[:, 2048 * m:2048 * (m + 1)]
            op("vector", (lambda xt_, c: lambda e: e.reduce_sum(
                smc(c), xt_, axis=AX.X))(xt, mean_c + m),
                reads=("x",), writes=("st",))
            op("vector", (lambda c: lambda e: e.tensor_scalar_mul(
                smc(c), smc(c), 1.0 / 2048.0))(mean_c + m), writes=("st",))
            op("scalar", (lambda xt_: lambda e: e.activation(
                aav(SCR, 2048), xt_, AF.Square, accum_out=smc(SM_T)))(xt),
                reads=("x",), writes=aanames(SCR, 2048) + ("st",))
            op("vector", (lambda c: lambda e: e.tensor_tensor(
                smc(SM_T + 1), smc(c), smc(c), ALU.mult))(mean_c + m),
                writes=("st",))
            op("vector", lambda e: e.tensor_scalar(
                smc(SM_T), smc(SM_T), 1.0 / 2048.0, 1e-5, ALU.mult, ALU.add),
                writes=("st",))
            op("vector", lambda e: e.tensor_sub(
                smc(SM_T), smc(SM_T), smc(SM_T + 1)), writes=("st",))
            op("scalar", lambda e: e.activation(smc(SM_T), smc(SM_T), AF.Ln),
               reads=("st",), writes=("st",))
            op("scalar", (lambda c: lambda e: e.activation(
                smc(c), smc(SM_T), AF.Exp, scale=-0.5))(rs_c + m),
                reads=("st",), writes=("st",))
            op("vector", (lambda xt_, mc, rc: lambda e: e.scalar_tensor_tensor(
                out=aav(SCR, 2048), in0=xt_, scalar=smc(mc),
                in1=smc(rc).broadcast_to([128, 2048]), op0=ALU.subtract,
                op1=ALU.mult))(xt, mean_c + m, rs_c + m),
                reads=("x", "st"), writes=aanames(SCR, 2048))
            if layer == 0:
                op("vector", lambda e: e.reduce_max(
                    smc(SM_T), aav(SCR, 2048), axis=AX.X,
                    apply_absolute_value=True),
                    reads=aanames(SCR, 2048), writes=("st",))
                op("vector", lambda e: e.tensor_scalar_max(
                    smc(SM_T), smc(SM_T), 1e-5), writes=("st",))
                op("vector", (lambda mi: lambda e: e.tensor_scalar_mul(
                    smc(SM_ISX + mi), smc(SM_T), 1.0 / 127.0))(m),
                    writes=("st",))
                op("vector", lambda e: e.reciprocal(
                    smc(SM_T + 1), smc(SM_T)), writes=("st",))
                op("vector", (lambda mi: lambda e: e.scalar_tensor_tensor(
                    out=smc(SM_C1X + mi), in0=smc(SM_T + 1), scalar=127.0,
                    in1=smc(SM_RS0 + mi), op0=ALU.mult, op1=ALU.mult))(m),
                    writes=("st",))
            items = [mmf(ps_big[:, 128 * ft:128 * (ft + 1)],
                         aav(SCR + 128 * ft, 128), IDT[:], True, True)
                     for ft in range(16)]
            mm_group(items, reads=aanames(SCR, 2048) + ("idt",),
                     writes=("ps_big",))
            op("vector", (lambda mi: lambda e: e.tensor_copy(
                AA[:, 0:16384].rearrange("p (t q) -> p t q", q=1024)
                [:, :, 128 * mi:128 * (mi + 1)],
                ps_big[:].rearrange("p (t q) -> p t q", q=128)))(m),
                reads=("ps_big",), writes=aanames(0, 16384))

    emit_ln(0)

    for colz in range(8):
        op("vector", (lambda c: lambda e: e.memset(smc(SM_G + c), 0.0))(colz),
           writes=("st",))

    def tern_pass1(dram, gcol, tiles, width):
        for t in range(tiles):
            dma("sync", FS[:, 0:width], dram[128 * t:128 * (t + 1), :],
                writes=("fs",))
            op("vector", (lambda w: lambda e: e.reduce_sum(
                smc(SM_T + 2), FS[:, 0:w], axis=AX.X,
                apply_absolute_value=True))(width),
                reads=("fs",), writes=("st",))
            op("vector", (lambda c: lambda e: e.tensor_add(
                smc(SM_G + c), smc(SM_G + c), smc(SM_T + 2)))(gcol),
                writes=("st",))

    tern_pass1(wg_sh, 0, 8, 2048)
    tern_pass1(wu_sh, 1, 8, 2048)
    tern_pass1(wd_sh, 2, 16, 1024)

    NF4S = [(w1_sh, 0, 1024, 6, 2048, 3), (w2_sh, 1, 1216, 6, 2048, 4),
            (w3_sh, 2, 1408, 2, 5632, 5)]
    for dram, widx, scol, tiles, width, gcol in NF4S:
        nblk = width // 64
        for t in range(tiles):
            nsub = (width + 2047) // 2048
            for s in range(nsub):
                c0 = 2048 * s
                w = min(2048, width - c0)
                dma("sync", wbv(0, w), dram[128 * t:128 * (t + 1), c0:c0 + w],
                    writes=wbnames(0, w))
                op("vector", (lambda w_, sc: lambda e: e.reduce_max(
                    SM[:, sc:sc + w_ // 64].unsqueeze(2),
                    WB[:, 0:w_].rearrange("p (b i) -> p b i", i=64),
                    axis=AX.X, apply_absolute_value=True))(
                        w, scol + t * nblk + c0 // 64),
                    reads=wbnames(0, w), writes=("smsb",))
        op("vector", (lambda sc, n: lambda e: e.reduce_sum(
            smc(SM_T + 2), SM[:, sc:sc + n], axis=AX.X))(scol, tiles * nblk),
            reads=("smsb",), writes=("st",))
        op("vector", (lambda c: lambda e: e.tensor_add(
            smc(SM_G + c), smc(SM_G + c), smc(SM_T + 2)))(gcol),
            writes=("st",))

    dma("sync", ar_in, SM[:, SM_G:SM_G + 8], reads=("st",), writes=("ar_in",))
    op("gpsimd", lambda e: e.collective_compute(
        "AllReduce", ALU.add, replica_groups=ALL8,
        ins=[ar_in.opt()], outs=[ar_out.opt()]),
        reads=("ar_in",), writes=("ar_out",), sem="cc")
    dma("sync", SM[:, SM_G:SM_G + 8], ar_out, reads=("ar_out",), writes=("st",))
    op("tensor", lambda e: e.matmul(ps_a[0:1, 0:8], smc(SM_ONE),
                                    SM[:, SM_G:SM_G + 8], start=True, stop=True),
       reads=("st",), writes=("ps_a",))
    op("vector", lambda e: e.tensor_copy(SM[0:1, SM_T + 4:SM_T + 12],
                                         ps_a[0:1, 0:8]),
       reads=("ps_a",), writes=("st",))
    op("tensor", lambda e: e.matmul(ps_a[:, 0:8], SM[0:1, SM_ROW:SM_ROW + 128],
                                    SM[0:1, SM_T + 4:SM_T + 12],
                                    start=True, stop=True),
       reads=("st",), writes=("ps_a",))
    op("vector", lambda e: e.tensor_copy(SM[:, SM_G:SM_G + 8], ps_a[:, 0:8]),
       reads=("ps_a",), writes=("st",))
    NW = float(8192 * 2048)
    for j in range(3):
        op("vector", (lambda jj: lambda e: e.tensor_scalar(
            smc(SM_THR + 3 + jj), smc(SM_G + jj), 1.0 / NW, 1e-5,
            ALU.mult, ALU.max))(j), writes=("st",))
        op("vector", (lambda jj: lambda e: e.tensor_scalar_mul(
            smc(SM_THR + jj), smc(SM_THR + 3 + jj), 0.5))(j), writes=("st",))
        op("vector", (lambda jj: lambda e: e.tensor_scalar_mul(
            smc(SM_NTHR + jj), smc(SM_THR + jj), -1.0))(j), writes=("st",))
        op("vector", (lambda jj: lambda e: e.tensor_scalar_mul(
            smc(SM_THR + 6 + jj), smc(SM_G + 3 + jj), 1.0 / NBLK_REAL))(j),
            writes=("st",))
        op("scalar", (lambda jj: lambda e: e.copy(
            wbv(11264 + jj, 1), smc(SM_THR + 6 + jj)))(j),
            reads=("st",), writes=("wbm",))
        op("scalar", (lambda jj: lambda e: e.copy(
            smc(SM_NTHR + 3 + jj), wbv(11264 + jj, 1)))(j),
            reads=("wbm",), writes=("st",))

    def emit_proj(layer):
        srcs = {"q": (wq_d, 0), "k": (wk_d, 0), "v": (wv_d, 0)} if layer == 0 \
            else {"q": (wqkv_d, 0), "k": (wqkv_d, 2048), "v": (wqkv_d, 4096)}
        dsts = {"k": kT_in[layer], "v": vT_in[layer], "q": qT_dr[layer]}
        for which in ("k", "v", "q"):
            wsrc, coff = srcs[which]
            for t in range(16):
                slot = 18432 + 2048 * (t % 2)
                dma("gpsimd",
                    aav(slot, 2048).rearrange("p (ft c) -> p ft c", c=128),
                    wsrc[:, coff + 128 * t:coff + 128 * (t + 1)]
                    .rearrange("(ft p) c -> p ft c", p=128),
                    writes=aanames(slot, 2048))
                items = []
                for n2 in range(2):
                    for ft in range(16):
                        items.append(mmf(
                            ps_a[:, 512 * n2:512 * (n2 + 1)],
                            aav(slot + 128 * ft, 128),
                            aav(1024 * ft + 512 * n2, 512),
                            ft == 0, ft == 15))
                mm_group(items, reads=aanames(slot, 2048) + aanames(0, 16384),
                         writes=("ps_a",))
                stg = 22528 + 1024 * (t % 2)
                if layer == 1:
                    boff = {"q": 0, "k": 16, "v": 32}[which] + t
                    op("scalar", (lambda st_, bo: lambda e: e.activation(
                        aav(st_, 1024), ps_a[:], AF.Identity,
                        bias=smc(SM_BQ + bo)))(stg, boff),
                        reads=("ps_a", "st"), writes=aanames(stg, 1024))
                else:
                    op("scalar", (lambda st_: lambda e: e.copy(
                        aav(st_, 1024), ps_a[:]))(stg),
                        reads=("ps_a",), writes=aanames(stg, 1024))
                dma("sync", dsts[which][128 * t:128 * (t + 1), :],
                    aav(stg, 1024), reads=aanames(stg, 1024),
                    writes=(f"{which}T{layer}_in",))
            if which in ("k", "v"):
                agi = kT_in[layer] if which == "k" else vT_in[layer]
                ago = kT_ag[layer] if which == "k" else vT_ag[layer]
                ag(agi, ago, f"{which}T{layer}_in", f"{which}T{layer}_ag",
                   PAIRS)

    emit_proj(0)

    def transpose_tile_out(src_c0, width, stage_c0, reads):
        nt = width // 128
        items = [mmf(ps_big[:, 128 * j:128 * (j + 1)],
                     wbv(src_c0 + 128 * j, 128), IDT[:], True, True)
                 for j in range(nt)]
        mm_group(items, reads=tuple(reads) + ("idt",), writes=("ps_big",))
        op("vector", (lambda w: lambda e: e.tensor_copy(
            wbv(stage_c0, w), ps_big[:, 0:w]))(width),
            reads=("ps_big",), writes=wbnames(stage_c0, width))

    def tern_pass2(dram, gidx, tiles, width, dst, dstname):
        for t in range(tiles):
            dma("sync", FS[:, 0:width], dram[128 * t:128 * (t + 1), :],
                writes=("fs",))
            op("vector", (lambda w, g: lambda e: e.tensor_single_scalar(
                wbv(0, w), FS[:, 0:w], smc(SM_THR + g), ALU.is_gt))(width, gidx),
                reads=("fs", "st"), writes=wbnames(0, width))
            op("vector", (lambda w, g: lambda e: e.tensor_scalar(
                wbv(2048, w), FS[:, 0:w], smc(SM_NTHR + g), -1.0,
                ALU.is_gt, ALU.add))(width, gidx),
                reads=("fs", "st"), writes=wbnames(2048, width))
            op("vector", (lambda w: lambda e: e.tensor_add(
                wbv(4096, w), wbv(0, w), wbv(2048, w)))(width),
                reads=wbnames(0, width) + wbnames(2048, width),
                writes=wbnames(4096, width))
            transpose_tile_out(4096, width, 8192, wbnames(4096, width))
            dma("sync",
                dst[:, 128 * t:128 * (t + 1)]
                .rearrange("(f p) c -> p f c", p=128),
                wbv(8192, width).rearrange("p (f c) -> p f c", c=128),
                reads=wbnames(8192, width), writes=(dstname,))

    tern_pass2(wg_sh, 0, 8, 2048, wgT_in, "wgT_in")
    ag(wgT_in, wgT_ag, "wgT_in", "wgT_ag", ALL8)
    tern_pass2(wu_sh, 1, 8, 2048, wuT_in, "wuT_in")
    ag(wuT_in, wuT_ag, "wuT_in", "wuT_ag", ALL8)
    tern_pass2(wd_sh, 2, 16, 1024, wdT_in, "wdT_in")
    ag(wdT_in, wdT_ag, "wdT_in", "wdT_ag", ALL8)

    def nf4_pass2(dram, widx, tiles, width, scol, dst, dstname):
        nblk = width // 64
        nsc = tiles * nblk
        nsb = nsc * 128 // 256
        P = nsb
        flat = nsc * 128
        rowc = nblk if width == 5632 else 32
        dma("sync", SM[0:P, SM_SB1:SM_SB1 + 256],
            s_rt[widx].flatten()[0:flat].rearrange("(s c) -> s c", c=256),
            reads=("s_rt",), writes=("smsb1",))
        op("vector", (lambda P_, w_: lambda e: e.tensor_single_scalar(
            SM[0:P_, SM_SB2:SM_SB2 + 256], SM[0:P_, SM_SB1:SM_SB1 + 256],
            SM[0:P_, SM_THR + 6 + w_:SM_THR + 7 + w_], ALU.subtract))(P, widx),
            reads=("smsb1", "st"), writes=("smsb2",))
        op("vector", (lambda P_: lambda e: e.reduce_max(
            SM[0:P_, SM_T + 4:SM_T + 5], SM[0:P_, SM_SB2:SM_SB2 + 256],
            axis=AX.X, apply_absolute_value=True))(P),
            reads=("smsb2",), writes=("st",))
        op("vector", (lambda P_: lambda e: e.reciprocal(
            SM[0:P_, SM_T + 5:SM_T + 6], SM[0:P_, SM_T + 4:SM_T + 5]))(P),
            writes=("st",))
        op("vector", (lambda P_: lambda e: e.tensor_scalar_mul(
            SM[0:P_, SM_T + 5:SM_T + 6], SM[0:P_, SM_T + 5:SM_T + 6],
            128.0))(P), writes=("st",))
        op("vector", (lambda P_: lambda e: e.tensor_scalar(
            SM[0:P_, SM_SB2:SM_SB2 + 256], SM[0:P_, SM_SB2:SM_SB2 + 256],
            SM[0:P_, SM_T + 5:SM_T + 6], MAGIC, ALU.mult, ALU.add))(P),
            reads=("st",), writes=("smsb2",))
        op("vector", (lambda P_: lambda e: e.tensor_scalar(
            SM[0:P_, SM_SB2:SM_SB2 + 256], SM[0:P_, SM_SB2:SM_SB2 + 256],
            -MAGIC, -128.0, ALU.add, ALU.max))(P), writes=("smsb2",))
        op("vector", (lambda P_: lambda e: e.tensor_scalar_min(
            SM[0:P_, SM_SB2:SM_SB2 + 256], SM[0:P_, SM_SB2:SM_SB2 + 256],
            127.0))(P), writes=("smsb2",))
        op("vector", (lambda P_: lambda e: e.tensor_scalar_mul(
            SM[0:P_, SM_T + 4:SM_T + 5], SM[0:P_, SM_T + 4:SM_T + 5],
            1.0 / 128.0))(P), writes=("st",))
        op("vector", (lambda P_: lambda e: e.tensor_single_scalar(
            WB[0:P_, 8192:8448], SM[0:P_, SM_SB2:SM_SB2 + 256],
            SM[0:P_, SM_T + 4:SM_T + 5], ALU.mult))(P),
            reads=("smsb2", "st"), writes=wbnames(8192, 256))
        op("vector", (lambda P_, w_: lambda e: e.tensor_single_scalar(
            WB[0:P_, 8192:8448], WB[0:P_, 8192:8448],
            SM[0:P_, SM_NTHR + 3 + w_:SM_NTHR + 4 + w_], ALU.add))(P, widx),
            reads=("st",), writes=wbnames(8192, 256))
        op("vector", (lambda P_: lambda e: e.tensor_scalar_max(
            SM[0:P_, SM_SB1:SM_SB1 + 256], SM[0:P_, SM_SB1:SM_SB1 + 256],
            1e-30))(P), writes=("smsb1",))
        op("vector", (lambda P_: lambda e: e.reciprocal(
            SM[0:P_, SM_SB2:SM_SB2 + 256], SM[0:P_, SM_SB1:SM_SB1 + 256]))(P),
            reads=("smsb1",), writes=("smsb2",))
        dma("sync", d_rt[widx].flatten()[0:flat].rearrange(
            "(s c) -> s c", c=256)[0:P, :], WB[0:P, 8192:8448],
            reads=wbnames(8192, 256), writes=("d_rt",))
        dma("sync", r_rt[widx].flatten()[0:flat].rearrange(
            "(s c) -> s c", c=256)[0:P, :], SM[0:P, SM_SB2:SM_SB2 + 256],
            reads=("smsb2",), writes=("r_rt",))
        dma("sync", WB[:, 8448:8448 + nsc].rearrange(
            "p (t c) -> p t c", c=rowc),
            d_rt[widx].flatten()[0:flat].rearrange(
                "(t p c) -> p t c", p=128, c=rowc),
            reads=("d_rt",), writes=wbnames(8448, nsc))
        dma("sync", FS[:, 0:nsc].rearrange("p (t c) -> p t c", c=rowc),
            r_rt[widx].flatten()[0:flat].rearrange(
                "(t p c) -> p t c", p=128, c=rowc),
            reads=("r_rt",), writes=("fs",))
        for t in range(tiles):
            nsub = (width + 2047) // 2048
            for s in range(nsub):
                c0 = 2048 * s
                w = min(2048, width - c0)
                bc = t * nblk + c0 // 64
                dma("sync", wbv(0, w), dram[128 * t:128 * (t + 1), c0:c0 + w],
                    writes=wbnames(0, w))
                op("vector", (lambda w_, b_: lambda e: e.tensor_tensor(
                    wbv(2048, w_).rearrange("p (b i) -> p b i", i=64),
                    wbv(0, w_).rearrange("p (b i) -> p b i", i=64),
                    FS[:, b_:b_ + w_ // 64].unsqueeze(2)
                    .broadcast_to([128, w_ // 64, 64]), ALU.mult))(w, bc),
                    reads=wbnames(0, w) + ("fs",), writes=wbnames(2048, w))
                op("vector", (lambda w_: lambda e: e.tensor_scalar(
                    wbv(4096, w_), wbv(2048, w_), smc(SM_NF4M),
                    float(NF4_DELTA[0]), ALU.is_gt, ALU.mult))(w),
                    reads=wbnames(2048, w) + ("st",), writes=wbnames(4096, w))
                op("vector", (lambda w_: lambda e: e.tensor_scalar_add(
                    wbv(4096, w_), wbv(4096, w_), NF4_BASE))(w),
                    writes=wbnames(4096, w))
                for kk in range(1, 15):
                    op("vector", (lambda w_, k_: lambda e: e.tensor_scalar(
                        wbv(6144, w_), wbv(2048, w_), smc(SM_NF4M + k_),
                        float(NF4_DELTA[k_]), ALU.is_gt, ALU.mult))(w, kk),
                        reads=wbnames(2048, w) + ("st",),
                        writes=wbnames(6144, w))
                    op("vector", (lambda w_: lambda e: e.tensor_add(
                        wbv(4096, w_), wbv(4096, w_), wbv(6144, w_)))(w),
                        reads=wbnames(6144, w), writes=wbnames(4096, w))
                op("vector", (lambda w_, b_: lambda e: e.tensor_tensor(
                    wbv(0, w_).rearrange("p (b i) -> p b i", i=64),
                    wbv(4096, w_).rearrange("p (b i) -> p b i", i=64),
                    WB[:, 8448 + b_:8448 + b_ + w_ // 64].unsqueeze(2)
                    .broadcast_to([128, w_ // 64, 64]), ALU.mult))(w, bc),
                    reads=wbnames(4096, w) + wbnames(8448, nsc),
                    writes=wbnames(0, w))
                transpose_tile_out(0, w, 9472, wbnames(0, w))
                dma("sync",
                    dst[c0:c0 + w, 128 * t:128 * (t + 1)]
                    .rearrange("(f p) c -> p f c", p=128),
                    wbv(9472, w).rearrange("p (f c) -> p f c", c=128),
                    reads=wbnames(9472, w), writes=(dstname,))

    for widx, scol, tiles, width in ((0, 1024, 6, 2048), (1, 1216, 6, 2048),
                                     (2, 1408, 2, 5632)):
        nblk = width // 64
        nsc = tiles * nblk
        flat = nsc * 128
        rowc = nblk if width == 5632 else 32
        dma("sync",
            s_rt[widx].flatten()[0:flat].rearrange(
                "(t p c) -> p t c", p=128, c=rowc),
            SM[:, scol:scol + nsc].rearrange("p (t c) -> p t c", c=rowc),
            reads=("smsb",), writes=("s_rt",))

    nf4_pass2(w1_sh, 0, 6, 2048, 1024, w1T_in, "w1T_in")
    ag(w1T_in, w1T_ag, "w1T_in", "w1T_ag", ALL8)
    nf4_pass2(w2_sh, 1, 6, 2048, 1216, w2T_in, "w2T_in")
    ag(w2T_in, w2T_ag, "w2T_in", "w2T_ag", ALL8)
    nf4_pass2(w3_sh, 2, 2, 5632, 1408, w3T_in, "w3T_in")
    ag(w3T_in, w3T_ag, "w3T_in", "w3T_ag", ALL8)

    def emit_ffn0():
        for m in range(8):
            op("vector", (lambda mi: lambda e: e.tensor_tensor(
                smc(SM_SCG + mi), smc(SM_ISX + mi), smc(SM_THR + 3),
                ALU.mult))(m), writes=("st",))
            op("vector", (lambda mi: lambda e: e.tensor_tensor(
                smc(SM_SCU + mi), smc(SM_ISX + mi), smc(SM_THR + 4),
                ALU.mult))(m), writes=("st",))
        for th in range(2):
            ms = [4 * th + i for i in range(4)]
            hslot = [8192, 16384, 24576, 32768]
            for i, m in enumerate(ms):
                xt = x_sb[:, 2048 * m:2048 * (m + 1)]
                for cchunk in range(4):
                    c0 = 512 * cchunk
                    op("vector", (lambda xt_, mi, c_: lambda e:
                        e.scalar_tensor_tensor(
                            out=SM[:, SM_QT:SM_QT + 512],
                            in0=xt_[:, c_:c_ + 512],
                            scalar=smc(SM_MEAN0 + mi),
                            in1=smc(SM_C1X + mi).broadcast_to([128, 512]),
                            op0=ALU.subtract, op1=ALU.mult))(xt, m, c0),
                        reads=("x", "st"), writes=("smqt",))
                    op("vector", lambda e: e.tensor_scalar(
                        SM[:, SM_QT:SM_QT + 512], SM[:, SM_QT:SM_QT + 512],
                        MAGIC, -MAGIC, ALU.add, ALU.add), writes=("smqt",))
                    op("vector", (lambda c_: lambda e: e.tensor_copy(
                        aav(SCR + c_, 512), SM[:, SM_QT:SM_QT + 512]))(c0),
                        reads=("smqt",), writes=aanames(SCR, 2048))
                items = [mmf(ps_big[:, 128 * ft:128 * (ft + 1)],
                             aav(SCR + 128 * ft, 128), IDT[:], True, True)
                         for ft in range(16)]
                mm_group(items, reads=aanames(SCR, 2048) + ("idt",),
                         writes=("ps_big",))
                op("vector", (lambda ii: lambda e: e.tensor_copy(
                    AA[:, 0:8192].rearrange("p (t q) -> p t q", q=512)
                    [:, :, 128 * ii:128 * (ii + 1)],
                    ps_big[:].rearrange("p (t q) -> p t q", q=128)))(i),
                    reads=("ps_big",), writes=aanames(0, 8192))
            for m in ms:
                op("vector", (lambda mi: lambda e: e.memset(
                    smc(SM_VAR + mi), 0.0))(m), writes=("st",))
            for i in range(4):
                op("vector", (lambda ii: lambda e: e.memset(
                    smc(SM_T + 12 + ii), 0.0))(i), writes=("st",))
            for n in range(16):
                r, off = (512 * n) // 1024, (512 * n) % 1024
                dma("sync", WB[:, 0:8192].rearrange("p (f c) -> p f c", c=512),
                    wgT_ag[2048 * r:2048 * (r + 1), off:off + 512]
                    .rearrange("(f p) c -> p f c", p=128),
                    writes=wbnames(0, 8192), extra=res.rd("wgT_ag"))
                dma("sync", AA[:, 40960:49152].rearrange(
                    "p (f c) -> p f c", c=512),
                    wuT_ag[2048 * r:2048 * (r + 1), off:off + 512]
                    .rearrange("(f p) c -> p f c", p=128),
                    writes=aanames(40960, 8192), extra=res.rd("wuT_ag"))
                dma("gpsimd", wbv(8960, 512),
                    lnw_d[:, 512 * n:512 * (n + 1)],
                    writes=wbnames(8960, 512))
                for i, m in enumerate(ms):
                    items = [mmf(ps_a[:, 0:512], aav(512 * ft + 128 * i, 128),
                                 wbv(512 * ft, 512), ft == 0, ft == 15)
                             for ft in range(16)]
                    mm_group(items, reads=aanames(0, 8192) + wbnames(0, 8192),
                             writes=("ps_a",))
                    items = [mmf(ps_b[:, 0:512], aav(512 * ft + 128 * i, 128),
                                 aav(40960 + 512 * ft, 512), ft == 0, ft == 15)
                             for ft in range(16)]
                    mm_group(items,
                             reads=aanames(0, 8192) + aanames(40960, 8192),
                             writes=("ps_b",))
                    op("scalar", (lambda mi: lambda e: e.activation(
                        wbv(8192, 512), ps_a[:, 0:512], AF.Silu,
                        scale=smc(SM_SCG + mi)))(m),
                        reads=("ps_a", "st"), writes=wbnames(8192, 512))
                    hc = hslot[i] + 512 * n
                    op("vector", (lambda mi, hc_: lambda e:
                        e.scalar_tensor_tensor(
                            out=aav(hc_, 512), in0=ps_b[:, 0:512],
                            scalar=smc(SM_SCU + mi), in1=wbv(8192, 512),
                            op0=ALU.mult, op1=ALU.mult))(m, hc),
                        reads=("ps_b", "st") + wbnames(8192, 512),
                        writes=aanames(hc, 512))
                    op("scalar", (lambda hc_: lambda e: e.activation(
                        wbv(9984, 512), aav(hc_, 512), AF.Square,
                        accum_out=smc(SM_T + 2)))(hc),
                        reads=aanames(hc, 512),
                        writes=wbnames(9984, 512) + ("st",))
                    op("vector", (lambda mi: lambda e: e.tensor_add(
                        smc(SM_VAR + mi), smc(SM_VAR + mi),
                        smc(SM_T + 2)))(m), writes=("st",))
                    op("vector", (lambda hc_: lambda e: e.tensor_tensor(
                        SM[:, SM_QT:SM_QT + 512], aav(hc_, 512),
                        wbv(8960, 512), ALU.mult))(hc),
                        reads=aanames(hc, 512) + wbnames(8960, 512),
                        writes=("smqt",))
                    op("vector", lambda e: e.reduce_max(
                        smc(SM_T + 2), SM[:, SM_QT:SM_QT + 512], axis=AX.X,
                        apply_absolute_value=True),
                        reads=("smqt",), writes=("st",))
                    op("vector", (lambda ii: lambda e: e.tensor_max(
                        smc(SM_T + 12 + ii), smc(SM_T + 12 + ii),
                        smc(SM_T + 2)))(i), writes=("st",))
            for i, m in enumerate(ms):
                op("vector", (lambda mi: lambda e: e.tensor_scalar(
                    smc(SM_T + 2), smc(SM_VAR + mi), 1.0 / 8192.0, 1e-6,
                    ALU.mult, ALU.add))(m), writes=("st",))
                op("scalar", lambda e: e.activation(
                    smc(SM_T + 2), smc(SM_T + 2), AF.Ln),
                    reads=("st",), writes=("st",))
                op("scalar", lambda e: e.activation(
                    smc(SM_T + 3), smc(SM_T + 2), AF.Exp, scale=-0.5),
                    reads=("st",), writes=("st",))
                op("vector", (lambda ii: lambda e: e.tensor_tensor(
                    smc(SM_T + 2), smc(SM_T + 3), smc(SM_T + 12 + ii),
                    ALU.mult))(i), writes=("st",))
                op("vector", lambda e: e.tensor_scalar_max(
                    smc(SM_T + 2), smc(SM_T + 2), 1e-5), writes=("st",))
                op("vector", (lambda mi: lambda e: e.tensor_scalar_mul(
                    smc(SM_C3 + mi), smc(SM_T + 2), 1.0 / 127.0))(m),
                    writes=("st",))
                op("vector", lambda e: e.reciprocal(
                    smc(SM_T + 2), smc(SM_T + 2)), writes=("st",))
                op("vector", lambda e: e.scalar_tensor_tensor(
                    out=smc(SM_T + 2), in0=smc(SM_T + 2), scalar=127.0,
                    in1=smc(SM_T + 3), op0=ALU.mult, op1=ALU.mult),
                    writes=("st",))
                op("vector", (lambda mi: lambda e: e.tensor_tensor(
                    smc(SM_C3 + mi), smc(SM_C3 + mi), smc(SM_THR + 5),
                    ALU.mult))(m), writes=("st",))
                for n in range(16):
                    hc = hslot[i] + 512 * n
                    dma("gpsimd", wbv(8960, 512),
                        lnw_d[:, 512 * n:512 * (n + 1)],
                        writes=wbnames(8960, 512))
                    op("vector", (lambda hc_: lambda e: e.tensor_tensor(
                        SM[:, SM_QT:SM_QT + 512], aav(hc_, 512),
                        wbv(8960, 512), ALU.mult))(hc),
                        reads=aanames(hc, 512) + wbnames(8960, 512),
                        writes=("smqt",))
                    op("vector", lambda e: e.tensor_scalar(
                        SM[:, SM_QT:SM_QT + 512], SM[:, SM_QT:SM_QT + 512],
                        smc(SM_T + 2), MAGIC, ALU.mult, ALU.add),
                        reads=("st",), writes=("smqt",))
                    op("vector", (lambda hc_: lambda e: e.tensor_scalar_add(
                        aav(hc_, 512), SM[:, SM_QT:SM_QT + 512], -MAGIC))(hc),
                        reads=("smqt",), writes=aanames(hc, 512))
                for batch in range(4):
                    c0 = hslot[i] + 2048 * batch
                    items = [mmf(ps_big[:, 128 * j:128 * (j + 1)],
                                 aav(c0 + 128 * j, 128), IDT[:], True, True)
                             for j in range(16)]
                    mm_group(items, reads=aanames(c0, 2048) + ("idt",),
                             writes=("ps_big",))
                    op("vector", (lambda c_: lambda e: e.tensor_copy(
                        aav(c_, 2048), ps_big[:]))(c0),
                        reads=("ps_big",), writes=aanames(c0, 2048))
            for dg in range(8):
                wdc = []
                for jj in range(8):
                    d = 8 * dg + jj
                    dst_c = (2048 * jj) if jj < 4 else (40960 + 2048 * (jj - 4))
                    wdc.append(dst_c)
                    dma("sync", aav(dst_c, 2048),
                        wdT_ag[128 * d:128 * (d + 1), :],
                        writes=aanames(dst_c, 2048),
                        extra=res.rd("wdT_ag"))
                for i, m in enumerate(ms):
                    items = []
                    for jj in range(8):
                        d = 8 * dg + jj
                        for n4 in range(4):
                            items.append(mmf(
                                ps_big[:, 512 * n4:512 * (n4 + 1)],
                                aav(hslot[i] + 128 * d, 128),
                                aav(wdc[jj] + 512 * n4, 512),
                                jj == 0, jj == 7))
                    rdn = aanames(hslot[i], 8192) + aanames(0, 8192) \
                        + aanames(40960, 8192)
                    mm_group(items, reads=rdn, writes=("ps_big",))
                    op("vector", (lambda mi: lambda e: e.scalar_tensor_tensor(
                        out=x_sb[:, 2048 * mi:2048 * (mi + 1)],
                        in0=ps_big[:], scalar=smc(SM_C3 + mi),
                        in1=x_sb[:, 2048 * mi:2048 * (mi + 1)],
                        op0=ALU.mult, op1=ALU.add))(m),
                        reads=("ps_big", "st", "x"), writes=("x",))

    emit_ffn0()

    def emit_attn(layer):
        use_mask = layer == 0
        wo_src = wo_d if layer == 0 else wout_d
        kag, vag, qdr = kT_ag[layer], vT_ag[layer], qT_dr[layer]
        kname, vname = f"kT{layer}_ag", f"vT{layer}_ag"
        qname = f"qT{layer}_in"
        tls = TLS if layer == 0 else [list(range(16))] * 8
        offs = [0]
        for i in range(8):
            offs.append(offs[-1] + 128 * len(tls[i]))
        if layer == 1:
            dma("sync", wbv(0, 2048), bout_d, writes=wbnames(0, 2048))
            for m in range(8):
                op("gpsimd", (lambda mi: lambda e: e.tensor_add(
                    x_sb[:, 2048 * mi:2048 * (mi + 1)],
                    x_sb[:, 2048 * mi:2048 * (mi + 1)], wbv(0, 2048)))(m),
                    reads=wbnames(0, 2048) + ("x",), writes=("x",))

        def psb_names(qt):
            if layer == 1 or len(tls[qt]) > 8:
                return ("ps_big", "psbg0", "psbg1")
            return (f"psbg{qt % 2}",)

        def softmax_stage(qt):
            tl = tls[qt]
            nt = len(tl)
            W = 128 * nt
            par = qt % 2
            dense = nt > 8
            st0 = SM_AT + 4 * par
            if layer == 1:
                pc = 6144 + 2048 * par
                po = 0
            else:
                pc = 6144 if dense else 6144 + 1024 * par
                po = 0 if dense else 1024 * par
            mc = 8192 if dense else 8192 + 1024 * par
            psn = psb_names(qt)
            san = (f"sa{par}",)
            items = []
            for s0, t0, L in _segs(tl):
                items.append(mmf(ps_big[:, po + 128 * s0:po + 128 * (s0 + L)],
                                 wbv(10240 + 128 * qt, 128),
                                 wbv(128 * t0, 128 * L), True, True))
            mm_group(items, reads=wbnames(10240, 1024) + wbnames(0, 2048),
                     writes=psn)
            if use_mask:
                dma("sync", wbv(mc, W), mask_d[qt][:, 0:W],
                    writes=wbnames(mc, W))
                op("vector", (lambda W_, mc_, po_: lambda e: e.tensor_add(
                    ps_big[:, po_:po_ + W_], ps_big[:, po_:po_ + W_],
                    wbv(mc_, W_)))(W, mc, po),
                    reads=wbnames(mc, W), writes=psn)
            op("vector", (lambda W_, c, po_: lambda e: e.reduce_max(
                smc(c), ps_big[:, po_:po_ + W_], axis=AX.X))(W, st0, po),
                reads=psn, writes=san)
            op("vector", (lambda c: lambda e: e.tensor_scalar_mul(
                smc(c + 1), smc(c), -C_ATTN))(st0), writes=san)
            op("scalar", (lambda W_, c, pc_, po_: lambda e: e.activation(
                wbv(pc_, W_), ps_big[:, po_:po_ + W_], AF.Exp,
                bias=smc(c + 1), scale=C_ATTN,
                accum_out=smc(c + 2)))(W, st0, pc, po),
                reads=psn + san, writes=wbnames(pc, W) + san)
            op("vector", (lambda c: lambda e: e.reciprocal(
                smc(c + 3), smc(c + 2)))(st0), writes=san)
            op("scalar", (lambda W_, c, pc_: lambda e: e.activation(
                wbv(pc_, W_), wbv(pc_, W_), AF.Identity,
                scale=smc(c + 3)))(W, st0, pc),
                reads=san, writes=wbnames(pc, W))

        def transpose_stage(qt):
            tl = tls[qt]
            nt = len(tl)
            par = qt % 2
            dense = nt > 8
            if layer == 1:
                pc = 6144 + 2048 * par
            else:
                pc = 6144 if dense else 6144 + 1024 * par
            if layer == 0:
                for w0 in range(0, nt, 8):
                    nw = min(8, nt - w0)
                    items = [mmf(ps_b[:, 128 * j:128 * (j + 1)],
                                 wbv(pc + 128 * (w0 + j), 128), IDT[:],
                                 True, True) for j in range(nw)]
                    mm_group(items, reads=wbnames(pc, 128 * nt) + ("idt",),
                             writes=("ps_b",))
                    dst = offs[qt] + 128 * w0
                    op("scalar", (lambda d_, n_: lambda e: e.copy(
                        AA[:, d_:d_ + 128 * n_],
                        ps_b[:, 0:128 * n_]))(dst, nw),
                        reads=("ps_b",), writes=aanames(dst, 128 * nw))
            else:
                items = [mmf(ps_a[:, 128 * j:128 * (j + 1)],
                             wbv(pc + 128 * j, 128), IDT[:], True, True)
                         for j in range(8)]
                mm_group(items, reads=wbnames(pc, 2048) + ("idt",),
                         writes=("ps_a",))
                items = [mmf(ps_b[:, 128 * j:128 * (j + 1)],
                             wbv(pc + 1024 + 128 * j, 128), IDT[:],
                             True, True) for j in range(8)]
                mm_group(items, reads=wbnames(pc, 2048) + ("idt",),
                         writes=("ps_b",))
                op("scalar", (lambda q_: lambda e: e.copy(
                    AA[:, 0:16384].rearrange("p (j q) -> p j q", q=1024)
                    [:, 0:8, 128 * q_:128 * (q_ + 1)],
                    ps_a[:].rearrange("p (j q) -> p j q", q=128)))(qt),
                    reads=("ps_a",), writes=aanames(0, 8192))
                op("scalar", (lambda q_: lambda e: e.copy(
                    AA[:, 0:16384].rearrange("p (j q) -> p j q", q=1024)
                    [:, 8:16, 128 * q_:128 * (q_ + 1)],
                    ps_b[:].rearrange("p (j q) -> p j q", q=128)))(qt),
                    reads=("ps_b",), writes=aanames(8192, 8192))

        for g in range(4):
            for hh in range(4):
                dma("gpsimd", aav(20480 + 2048 * hh, 2048),
                    wo_src[512 * g + 128 * hh:512 * g + 128 * (hh + 1), :],
                    writes=aanames(20480 + 2048 * hh, 2048))
            for hh in range(4):
                h = 4 * g + hh
                for half in range(2):
                    dma("sync", wbv(1024 * half, 1024),
                        kag[2048 * half + 128 * h:2048 * half + 128 * (h + 1), :],
                        writes=wbnames(1024 * half, 1024),
                        extra=res.rd(kname))
                    dma("sync", wbv(2048 + 1024 * half, 1024),
                        vag[2048 * half + 128 * h:2048 * half + 128 * (h + 1), :],
                        writes=wbnames(2048 + 1024 * half, 1024),
                        extra=res.rd(vname))
                dma("sync", wbv(10240, 1024), qdr[128 * h:128 * (h + 1), :],
                    writes=wbnames(10240, 1024), extra=res.rd(qname))
                items = [mmf(ps_a[:, 128 * j:128 * (j + 1)],
                             wbv(2048 + 128 * j, 128), IDT[:], True, True)
                         for j in range(8)]
                mm_group(items, reads=wbnames(2048, 2048) + ("idt",),
                         writes=("ps_a",))
                items = [mmf(ps_b[:, 128 * j:128 * (j + 1)],
                             wbv(2048 + 1024 + 128 * j, 128), IDT[:],
                             True, True) for j in range(8)]
                mm_group(items, reads=wbnames(2048, 2048) + ("idt",),
                         writes=("ps_b",))
                op("vector", lambda e: e.tensor_copy(wbv(4096, 1024),
                                                     ps_a[:]),
                   reads=("ps_a",), writes=wbnames(4096, 1024))
                op("vector", lambda e: e.tensor_copy(wbv(4096 + 1024, 1024),
                                                     ps_b[:]),
                   reads=("ps_b",), writes=wbnames(4096 + 1024, 1024))
                first_def = 1 if layer == 0 else 0
                for qt in range(8):
                    softmax_stage(qt)
                    if qt == 0 and first_def == 1:
                        transpose_stage(0)
                    elif qt > first_def:
                        transpose_stage(qt - 1)
                transpose_stage(7)
                if layer == 0:
                    for qt in range(8):
                        tl = tls[qt]
                        items = []
                        for s, t in enumerate(tl):
                            items.append(mmf(
                                ps_a[:, 128 * qt:128 * (qt + 1)],
                                wbv(4096 + 128 * t, 128),
                                AA[:, offs[qt] + 128 * s:
                                   offs[qt] + 128 * (s + 1)],
                                s == 0, s == len(tl) - 1))
                        mm_group(items,
                                 reads=aanames(offs[qt], 128 * len(tl))
                                 + wbnames(4096, 2048), writes=("ps_a",))
                else:
                    items = []
                    for j in range(16):
                        for n2 in range(2):
                            items.append(mmf(
                                ps_a[:, 512 * n2:512 * (n2 + 1)],
                                wbv(4096 + 128 * j, 128),
                                aav(1024 * j + 512 * n2, 512),
                                j == 0, j == 15))
                    mm_group(items,
                             reads=aanames(0, 16384) + wbnames(4096, 2048),
                             writes=("ps_a",))
                op("scalar", (lambda hh_: lambda e: e.copy(
                    aav(16384 + 1024 * hh_, 1024), ps_a[:]))(hh),
                    reads=("ps_a",), writes=aanames(16384 + 1024 * hh, 1024))
            for m in range(8):
                items = []
                for hh in range(4):
                    for n4 in range(4):
                        items.append(mmf(
                            ps_big[:, 512 * n4:512 * (n4 + 1)],
                            aav(16384 + 1024 * hh + 128 * m, 128),
                            aav(20480 + 2048 * hh + 512 * n4, 512),
                            hh == 0, hh == 3))
                mm_group(items, reads=aanames(16384, 4096) + aanames(20480, 8192),
                         writes=("ps_big", "psbg0", "psbg1"))
                op("vector", (lambda mi: lambda e: e.tensor_add(
                    x_sb[:, 2048 * mi:2048 * (mi + 1)],
                    x_sb[:, 2048 * mi:2048 * (mi + 1)], ps_big[:]))(m),
                    reads=("ps_big", "x"), writes=("x",))

    emit_attn(0)

    emit_ln(1)
    emit_proj(1)

    def emit_ffn1():
        for qd in range(4):
            dls = list(range(11 * qd, 11 * qd + 11))
            for idx, dl in enumerate(dls):
                r, off = (128 * dl) // 768, (128 * dl) % 768
                dma("sync", wbv(0, 2048).rearrange("p (f c) -> p f c", c=128),
                    w1T_ag[2048 * r:2048 * (r + 1), off:off + 128]
                    .rearrange("(f p) c -> p f c", p=128),
                    writes=wbnames(0, 2048), extra=res.rd("w1T_ag"))
                dma("sync", wbv(2048, 2048).rearrange("p (f c) -> p f c", c=128),
                    w2T_ag[2048 * r:2048 * (r + 1), off:off + 128]
                    .rearrange("(f p) c -> p f c", p=128),
                    writes=wbnames(2048, 2048), extra=res.rd("w2T_ag"))
                items = []
                for n2 in range(2):
                    for ft in range(16):
                        items.append(mmf(
                            ps_a[:, 512 * n2:512 * (n2 + 1)],
                            wbv(128 * ft, 128),
                            aav(1024 * ft + 512 * n2, 512),
                            ft == 0, ft == 15))
                mm_group(items, reads=wbnames(0, 2048) + aanames(0, 16384),
                         writes=("ps_a",))
                items = []
                for n2 in range(2):
                    for ft in range(16):
                        items.append(mmf(
                            ps_b[:, 512 * n2:512 * (n2 + 1)],
                            wbv(2048 + 128 * ft, 128),
                            aav(1024 * ft + 512 * n2, 512),
                            ft == 0, ft == 15))
                mm_group(items, reads=wbnames(2048, 2048) + aanames(0, 16384),
                         writes=("ps_b",))
                op("scalar", lambda e: e.activation(
                    wbv(4096, 1024), ps_a[:], AF.Silu),
                    reads=("ps_a",), writes=wbnames(4096, 1024))
                hc = 16384 + 1024 * idx
                op("vector", (lambda hc_: lambda e: e.tensor_tensor(
                    aav(hc_, 1024), wbv(4096, 1024), ps_b[:], ALU.mult))(hc),
                    reads=wbnames(4096, 1024) + ("ps_b",),
                    writes=aanames(hc, 1024))
            for idx, dl in enumerate(dls):
                wc = 27648 + 2048 * idx
                dma("sync", aav(wc, 2048).rearrange("p (r c) -> p r c", c=256),
                    w3T_ag.rearrange("(r d) c -> d r c", d=5632)
                    [128 * dl:128 * (dl + 1)],
                    writes=aanames(wc, 2048), extra=res.rd("w3T_ag"))
            for m in range(8):
                items = []
                for idx in range(11):
                    for n4 in range(4):
                        items.append(mmf(
                            ps_big[:, 512 * n4:512 * (n4 + 1)],
                            aav(16384 + 1024 * idx + 128 * m, 128),
                            aav(27648 + 2048 * idx + 512 * n4, 512),
                            idx == 0, idx == 10))
                mm_group(items,
                         reads=aanames(16384, 11264) + aanames(27648, 22528),
                         writes=("ps_big",))
                op("vector", (lambda mi: lambda e: e.tensor_add(
                    x_sb[:, 2048 * mi:2048 * (mi + 1)],
                    x_sb[:, 2048 * mi:2048 * (mi + 1)], ps_big[:]))(m),
                    reads=("ps_big", "x"), writes=("x",))

    emit_ffn1()
    emit_attn(1)

    fin = dma("sync", out_d.rearrange("(m p) f -> p m f", p=128),
              x_sb[:].rearrange("p (m f) -> p m f", f=2048), reads=("x",),
              writes=("outd",))
    p.emit("sync", lambda e: None, waits=[("d_outd", fin)])

    with nc.Block() as block:
        def mk(engname):
            recs = p.recs[engname]

            def body(e, recs=recs):
                for fn, waits, inc, n in recs:
                    for s, v in waits:
                        e.wait_ge(sems[s], v)
                    instr = fn(e)
                    if inc is not None:
                        instr.then_inc(sems[inc], n)
            return body

        block.sync(mk("sync"))
        block.gpsimd(mk("gpsimd"))
        block.tensor(mk("tensor"))
        block.vector(mk("vector"))
        block.scalar(mk("scalar"))

    ctx.close()
    return nc


def _blk_idx(r):
    return np.concatenate([np.arange(128 * (2 * i + r), 128 * (2 * i + r) + 128)
                           for i in range(8)])


def _mask_for_core(r):
    out = np.zeros((8, 128, 2048), np.float32)
    for i in range(8):
        tl = TLS[i]
        q = (128 * (2 * i + r) + np.arange(128))[:, None]
        k = np.concatenate([128 * (2 * (t % 8) + t // 8) + np.arange(128)
                            for t in tl])[None, :]
        allow = (np.abs(q - k) < 256) | (q < 16) | (k < 16)
        out[i, :, 0:128 * len(tl)] = np.where(allow, np.float32(0.0),
                                              np.float32(NEG))
    return out.astype(ml_dtypes.bfloat16)


_NC_CACHE = [None]


def kernel(L0_attn_hidden_states, L0_attn_Wq, L0_attn_Wk, L0_attn_Wv,
           L0_attn_Wo, L0_ffn_w_gate, L0_ffn_w_up, L0_ffn_w_down,
           L0_ffn_ln_weight, L1_attn_w_qkv, L1_attn_b_qkv, L1_attn_w_out,
           L1_attn_b_out, L1_ffn_weight1, L1_ffn_weight2, L1_ffn_weight3):
    xs = np.asarray(L0_attn_hidden_states)
    f32 = np.float32
    bq = np.ascontiguousarray(np.asarray(L1_attn_b_qkv, f32)
                              .reshape(48, 128).T)
    bout = np.broadcast_to(np.asarray(L1_attn_b_out).astype(ml_dtypes.bfloat16),
                           (128, 2048)).copy()
    lnw = np.broadcast_to(np.asarray(L0_ffn_ln_weight, f32), (128, 8192)).copy()
    ident = np.eye(128, dtype=ml_dtypes.bfloat16)
    nf4c = np.zeros((32,), f32)
    nf4c[0:15] = NF4_MID
    nf4c[16:31] = NF4_DELTA
    nf4c = np.broadcast_to(nf4c, (128, 32)).copy()
    w1p = np.zeros((6144, 2048), ml_dtypes.bfloat16)
    w1p[0:5632] = np.asarray(L1_ffn_weight1)
    w2p = np.zeros((6144, 2048), ml_dtypes.bfloat16)
    w2p[0:5632] = np.asarray(L1_ffn_weight2)
    w3 = np.asarray(L1_ffn_weight3)
    wg = np.asarray(L0_ffn_w_gate, f32)
    wu = np.asarray(L0_ffn_w_up, f32)
    wd = np.asarray(L0_ffn_w_down, f32)

    bidx = [_blk_idx(0), _blk_idx(1)]
    in_maps = []
    for c in range(8):
        b, hf = c // 2, c % 2
        in_maps.append({
            "x_in": np.ascontiguousarray(xs[b][bidx[hf]], f32),
            "wq": np.asarray(L0_attn_Wq, f32),
            "wk": np.asarray(L0_attn_Wk, f32),
            "wv": np.asarray(L0_attn_Wv, f32),
            "wo": np.asarray(L0_attn_Wo, f32),
            "wqkv": np.asarray(L1_attn_w_qkv, f32),
            "wout": np.asarray(L1_attn_w_out, f32),
            "bqkvT": bq,
            "bout_bc": bout,
            "wg_sh": np.ascontiguousarray(wg[1024 * c:1024 * (c + 1), :]),
            "wu_sh": np.ascontiguousarray(wu[1024 * c:1024 * (c + 1), :]),
            "wd_sh": np.ascontiguousarray(wd[:, 1024 * c:1024 * (c + 1)]),
            "lnw_bc": lnw,
            "w1_sh": np.ascontiguousarray(w1p[768 * c:768 * (c + 1), :]),
            "w2_sh": np.ascontiguousarray(w2p[768 * c:768 * (c + 1), :]),
            "w3_sh": np.ascontiguousarray(w3[256 * c:256 * (c + 1), :]),
            "mask_in": _mask_for_core(hf),
            "ident_in": ident,
            "nf4c_in": nf4c,
        })
    if _NC_CACHE[0] is None:
        _NC_CACHE[0] = build_program()
    nc = _NC_CACHE[0]
    r = run_bass_kernel_spmd(nc, in_maps, core_ids=list(range(8)))
    out = np.empty((4, 2048, 2048), np.float32)
    for c in range(8):
        b, hf = c // 2, c % 2
        out[b][bidx[hf]] = r.results[c]["out"]
    return out


if __name__ == "__main__":
    nc = build_program()
    print("build ok")


# version 11
# speedup vs baseline: 1.4504x; 1.0333x over previous
import numpy as np
import ml_dtypes
import concourse.bass as bass
import concourse.mybir as mybir
from concourse.bass_utils import run_bass_kernel_spmd

F32 = mybir.dt.float32
BF16 = mybir.dt.bfloat16
AF = mybir.ActivationFunctionType
ALU = mybir.AluOpType
AX = mybir.AxisListType

MAGIC = float(np.float32(3 << 22))
NEG = -1.0e30
C_ATTN = float(128.0 ** -0.5)

NF4 = np.array([-1.0, -0.6962, -0.5251, -0.3949, -0.2844, -0.1848, -0.0911,
                0.0, 0.0796, 0.1609, 0.2461, 0.3379, 0.4407, 0.5626, 0.723,
                1.0], dtype=ml_dtypes.bfloat16).astype(np.float64)
NF4_MID = ((NF4[:-1] + NF4[1:]) / 2.0).astype(np.float32)
NF4_DELTA = np.diff(NF4).astype(np.float32)
NF4_BASE = float(NF4[0])
NBLK_REAL = 5632 * 2048 // 64

ENGINES = ("sync", "gpsimd", "tensor", "vector", "scalar")

SM_MEAN0, SM_RS0, SM_C1X, SM_ISX = 0, 8, 16, 24
SM_MEAN1, SM_RS1 = 48, 56
SM_G = 64
SM_THR = 72
SM_NTHR = 81
SM_NF4M, SM_NF4D = 96, 112
SM_BQ = 128
SM_ONE = 176
SM_T = 178
SM_SCG, SM_SCU, SM_C3, SM_VAR = 224, 232, 240, 248
SM_ROW = 256
SM_AT = 384
SM_SB1, SM_SB2, SM_QT = 1024, 1280, 1536



def _tl(i):
    if i == 0:
        return list(range(16))
    lo = [t for t in (i - 1, i, i + 1) if 0 <= t <= 7]
    return sorted(set([0] + lo + [8 + t for t in lo]))


TLS = [_tl(i) for i in range(8)]


def _segs(tl):
    runs = []
    for s, t in enumerate(tl):
        if runs and runs[-1][0] + runs[-1][2] == s \
                and runs[-1][1] + runs[-1][2] == t:
            runs[-1][2] += 1
        else:
            runs.append([s, t, 1])
    segs = []
    for s0, t0, L in runs:
        s = s0
        while s < s0 + L:
            e = min(s0 + L, (s // 4 + 1) * 4)
            segs.append((s, t0 + (s - s0), e - s))
            s = e
    return segs

AA_COLS = 50176
WB_COLS = 12288


class Prog:
    def __init__(self):
        self.recs = {e: [] for e in ENGINES}
        self.cnt = {}

    def emit(self, eng, fn, waits=(), inc=None, n=1):
        self.recs[eng].append((fn, tuple(waits), inc, n))
        if inc is not None:
            self.cnt[inc] = self.cnt.get(inc, 0) + n
        return self.cnt.get(inc)


class Res:
    def __init__(self):
        self.w = {}
        self.r = {}

    def rd(self, name):
        return [self.w[name]] if name in self.w else []

    def wr(self, name):
        ws = list(self.r.get(name, []))
        if name in self.w:
            ws.append(self.w[name])
        return ws

    def did_rd(self, name, sem, cnt):
        self.r.setdefault(name, []).append((sem, cnt))

    def did_wr(self, name, sem, cnt):
        self.w[name] = (sem, cnt)
        self.r[name] = []


def merge_waits(*wls):
    best = {}
    for wl in wls:
        for s, v in wl:
            if v > best.get(s, -1):
                best[s] = v
    return list(best.items())


def build_program():
    nc = bass.Bass("TRN2", target_bir_lowering=False, debug=False)
    p = Prog()
    res = Res()

    x_in = nc.dram_tensor("x_in", [1024, 2048], F32, kind="ExternalInput").ap()
    wq_d = nc.dram_tensor("wq", [2048, 2048], F32, kind="ExternalInput").ap()
    wk_d = nc.dram_tensor("wk", [2048, 2048], F32, kind="ExternalInput").ap()
    wv_d = nc.dram_tensor("wv", [2048, 2048], F32, kind="ExternalInput").ap()
    wo_d = nc.dram_tensor("wo", [2048, 2048], F32, kind="ExternalInput").ap()
    wqkv_d = nc.dram_tensor("wqkv", [2048, 6144], F32, kind="ExternalInput").ap()
    wout_d = nc.dram_tensor("wout", [2048, 2048], F32, kind="ExternalInput").ap()
    bqkvT_d = nc.dram_tensor("bqkvT", [128, 48], F32, kind="ExternalInput").ap()
    bout_d = nc.dram_tensor("bout_bc", [128, 2048], BF16, kind="ExternalInput").ap()
    wg_sh = nc.dram_tensor("wg_sh", [1024, 2048], F32, kind="ExternalInput").ap()
    wu_sh = nc.dram_tensor("wu_sh", [1024, 2048], F32, kind="ExternalInput").ap()
    wd_sh = nc.dram_tensor("wd_sh", [2048, 1024], F32, kind="ExternalInput").ap()
    lnw_d = nc.dram_tensor("lnw_bc", [128, 8192], F32, kind="ExternalInput").ap()
    w1_sh = nc.dram_tensor("w1_sh", [768, 2048], BF16, kind="ExternalInput").ap()
    w2_sh = nc.dram_tensor("w2_sh", [768, 2048], BF16, kind="ExternalInput").ap()
    w3_sh = nc.dram_tensor("w3_sh", [256, 5632], BF16, kind="ExternalInput").ap()
    mask_d = nc.dram_tensor("mask_in", [8, 128, 2048], BF16, kind="ExternalInput").ap()
    ident_d = nc.dram_tensor("ident_in", [128, 128], BF16, kind="ExternalInput").ap()
    nf4c_d = nc.dram_tensor("nf4c_in", [128, 32], F32, kind="ExternalInput").ap()
    out_d = nc.dram_tensor("out", [1024, 2048], F32, kind="ExternalOutput").ap()

    def idram(name, shape, dtype=BF16):
        return nc.dram_tensor(name, shape, dtype).ap()

    ar_in = idram("ar_in", [128, 8], F32)
    ar_out = idram("ar_out", [128, 8], F32)
    s_rt = idram("s_rt", [3, 768, 32], F32)
    r_rt = idram("r_rt", [3, 768, 32], F32)
    d_rt = idram("d_rt", [3, 768, 32], BF16)
    def sdram(name, shape, dtype=BF16):
        return nc.dram_tensor(name, shape, dtype, addr_space="Shared").ap()

    wgT_in = idram("wgT_in", [2048, 1024]); wgT_ag = sdram("wgT_ag", [16384, 1024])
    wuT_in = idram("wuT_in", [2048, 1024]); wuT_ag = sdram("wuT_ag", [16384, 1024])
    wdT_in = idram("wdT_in", [1024, 2048]); wdT_ag = sdram("wdT_ag", [8192, 2048])
    w1T_in = idram("w1T_in", [2048, 768]); w1T_ag = sdram("w1T_ag", [16384, 768])
    w2T_in = idram("w2T_in", [2048, 768]); w2T_ag = sdram("w2T_ag", [16384, 768])
    w3T_in = idram("w3T_in", [5632, 256]); w3T_ag = sdram("w3T_ag", [45056, 256])
    kT_in = [idram("kT0_in", [2048, 1024]), idram("kT1_in", [2048, 1024])]
    kT_ag = [idram("kT0_ag", [4096, 1024]), idram("kT1_ag", [4096, 1024])]
    vT_in = [idram("vT0_in", [2048, 1024]), idram("vT1_in", [2048, 1024])]
    vT_ag = [idram("vT0_ag", [4096, 1024]), idram("vT1_ag", [4096, 1024])]
    qT_dr = [idram("qT0_dr", [2048, 1024]), idram("qT1_dr", [2048, 1024])]

    from contextlib import ExitStack
    ctx = ExitStack()
    E = ctx.enter_context
    x_sb = E(nc.sbuf_tensor("x_sb", [128, 16384], F32))
    AA = E(nc.sbuf_tensor("AA", [128, AA_COLS], BF16))
    WB = E(nc.sbuf_tensor("WB", [128, WB_COLS], BF16))
    SM = E(nc.sbuf_tensor("SM", [128, 2048], F32))
    FS = E(nc.sbuf_tensor("FS", [128, 2048], F32))
    IDT = E(nc.sbuf_tensor("IDT", [128, 128], BF16))
    ps_big = E(nc.psum_tensor("ps_big", [128, 2048], F32))
    ps_a = E(nc.psum_tensor("ps_a", [128, 1024], F32))
    ps_b = E(nc.psum_tensor("ps_b", [128, 1024], F32))
    sems = {}
    for s in ("cc", "pe", "ve", "se"):
        sems[s] = E(nc.semaphore(s))

    def getsem(name):
        if name not in sems:
            sems[name] = E(nc.semaphore(name))
        return sems[name]

    ENG_SEM = {"tensor": "pe", "vector": "ve", "scalar": "se",
               "gpsimd": "sw", "sync": "hw"}
    PAIRS = [[0, 1], [2, 3], [4, 5], [6, 7]]
    ALL8 = [list(range(8))]

    def aanames(c0, n):
        return tuple(f"aa{k}" for k in range(c0 // 4096, (c0 + n - 1) // 4096 + 1))

    def wbnames(c0, n):
        return tuple(f"wb{k}" for k in range(c0 // 1024, (c0 + n - 1) // 1024 + 1))

    def dma(engname, out_ap, in_ap, reads=(), writes=(), extra=()):
        assert writes, "dma needs a write-resource for its stream semaphore"
        semname = ("dg_" if engname == "gpsimd" else "d_") + writes[0]
        getsem(semname)
        waits = merge_waits(*([res.rd(r) for r in reads]
                              + [res.wr(w) for w in writes] + [list(extra)]))
        cnt = p.emit(engname,
                     (lambda o, i: lambda e: e.dma_start(out=o, in_=i))(out_ap, in_ap),
                     waits=waits, inc=semname, n=16)
        for r in reads:
            res.did_rd(r, semname, cnt)
        for w in writes:
            res.did_wr(w, semname, cnt)
        return cnt

    def op(engname, fn, reads=(), writes=(), extra=(), sem=None):
        semname = sem or ENG_SEM[engname]
        getsem(semname)
        waits = merge_waits(*([res.rd(r) for r in reads]
                              + [res.wr(w) for w in writes] + [list(extra)]))
        cnt = p.emit(engname, fn, waits=waits, inc=semname, n=1)
        for r in reads:
            res.did_rd(r, semname, cnt)
        for w in writes:
            res.did_wr(w, semname, cnt)
        return cnt

    def mmf(o, l, r, st, sp):
        return lambda e: e.matmul(o, l, r, start=st, stop=sp)

    def mm_group(items, reads=(), writes=(), extra=()):
        waits = merge_waits(*([res.rd(r) for r in reads]
                              + [res.wr(w) for w in writes] + [list(extra)]))
        for it in items[:-1]:
            p.emit("tensor", it, waits=waits)
            waits = ()
        cnt = p.emit("tensor", items[-1], waits=waits, inc="pe", n=1)
        for r in reads:
            res.did_rd(r, "pe", cnt)
        for w in writes:
            res.did_wr(w, "pe", cnt)
        return cnt

    def aav(c0, n):
        return AA[:, c0:c0 + n]

    def wbv(c0, n):
        return WB[:, c0:c0 + n]

    def smc(col, n=1):
        return SM[:, col:col + n]

    def ag(in_ap, out_ap, rname, wname, groups):
        op("gpsimd", (lambda i_, o_, g_: lambda e: e.collective_compute(
            "AllGather", ALU.bypass, replica_groups=g_,
            ins=[i_.opt()], outs=[o_.opt()]))(in_ap, out_ap, groups),
            reads=(rname,), writes=(wname,), sem="cc")

    dma("sync", x_sb[:].rearrange("p (m f) -> p m f", f=2048),
        x_in.rearrange("(m p) f -> p m f", p=128), writes=("x",))
    dma("sync", IDT[:], ident_d, writes=("idt",))
    dma("sync", SM[:, SM_NF4M:SM_NF4M + 32], nf4c_d, writes=("st",))
    dma("sync", SM[:, SM_BQ:SM_BQ + 48], bqkvT_d, writes=("st",))
    op("vector", lambda e: e.memset(smc(SM_ONE), 1.0), writes=("st",))
    op("vector", lambda e: e.memset(SM[0:1, SM_ROW:SM_ROW + 128], 1.0),
       writes=("st",))

    SCR = 36864

    def emit_ln(layer):
        mean_c = SM_MEAN0 if layer == 0 else SM_MEAN1
        rs_c = SM_RS0 if layer == 0 else SM_RS1
        for m in range(8):
            xt = x_sb[:, 2048 * m:2048 * (m + 1)]
            op("vector", (lambda xt_, c: lambda e: e.reduce_sum(
                smc(c), xt_, axis=AX.X))(xt, mean_c + m),
                reads=("x",), writes=("st",))
            op("vector", (lambda c: lambda e: e.tensor_scalar_mul(
                smc(c), smc(c), 1.0 / 2048.0))(mean_c + m), writes=("st",))
            op("scalar", (lambda xt_: lambda e: e.activation(
                aav(SCR, 2048), xt_, AF.Square, accum_out=smc(SM_T)))(xt),
                reads=("x",), writes=aanames(SCR, 2048) + ("st",))
            op("vector", (lambda c: lambda e: e.tensor_tensor(
                smc(SM_T + 1), smc(c), smc(c), ALU.mult))(mean_c + m),
                writes=("st",))
            op("vector", lambda e: e.tensor_scalar(
                smc(SM_T), smc(SM_T), 1.0 / 2048.0, 1e-5, ALU.mult, ALU.add),
                writes=("st",))
            op("vector", lambda e: e.tensor_sub(
                smc(SM_T), smc(SM_T), smc(SM_T + 1)), writes=("st",))
            op("scalar", lambda e: e.activation(smc(SM_T), smc(SM_T), AF.Ln),
               reads=("st",), writes=("st",))
            op("scalar", (lambda c: lambda e: e.activation(
                smc(c), smc(SM_T), AF.Exp, scale=-0.5))(rs_c + m),
                reads=("st",), writes=("st",))
            op("vector", (lambda xt_, mc, rc: lambda e: e.scalar_tensor_tensor(
                out=aav(SCR, 2048), in0=xt_, scalar=smc(mc),
                in1=smc(rc).broadcast_to([128, 2048]), op0=ALU.subtract,
                op1=ALU.mult))(xt, mean_c + m, rs_c + m),
                reads=("x", "st"), writes=aanames(SCR, 2048))
            if layer == 0:
                op("vector", lambda e: e.reduce_max(
                    smc(SM_T), aav(SCR, 2048), axis=AX.X,
                    apply_absolute_value=True),
                    reads=aanames(SCR, 2048), writes=("st",))
                op("vector", lambda e: e.tensor_scalar_max(
                    smc(SM_T), smc(SM_T), 1e-5), writes=("st",))
                op("vector", (lambda mi: lambda e: e.tensor_scalar_mul(
                    smc(SM_ISX + mi), smc(SM_T), 1.0 / 127.0))(m),
                    writes=("st",))
                op("vector", lambda e: e.reciprocal(
                    smc(SM_T + 1), smc(SM_T)), writes=("st",))
                op("vector", (lambda mi: lambda e: e.scalar_tensor_tensor(
                    out=smc(SM_C1X + mi), in0=smc(SM_T + 1), scalar=127.0,
                    in1=smc(SM_RS0 + mi), op0=ALU.mult, op1=ALU.mult))(m),
                    writes=("st",))
            items = [mmf(ps_big[:, 128 * ft:128 * (ft + 1)],
                         aav(SCR + 128 * ft, 128), IDT[:], True, True)
                     for ft in range(16)]
            mm_group(items, reads=aanames(SCR, 2048) + ("idt",),
                     writes=("ps_big",))
            op("vector", (lambda mi: lambda e: e.tensor_copy(
                AA[:, 0:16384].rearrange("p (t q) -> p t q", q=1024)
                [:, :, 128 * mi:128 * (mi + 1)],
                ps_big[:].rearrange("p (t q) -> p t q", q=128)))(m),
                reads=("ps_big",), writes=aanames(0, 16384))

    emit_ln(0)

    for colz in range(8):
        op("vector", (lambda c: lambda e: e.memset(smc(SM_G + c), 0.0))(colz),
           writes=("st",))

    def tern_pass1(dram, gcol, tiles, width):
        for t in range(tiles):
            dma("sync", FS[:, 0:width], dram[128 * t:128 * (t + 1), :],
                writes=("fs",))
            op("vector", (lambda w: lambda e: e.reduce_sum(
                smc(SM_T + 2), FS[:, 0:w], axis=AX.X,
                apply_absolute_value=True))(width),
                reads=("fs",), writes=("st",))
            op("vector", (lambda c: lambda e: e.tensor_add(
                smc(SM_G + c), smc(SM_G + c), smc(SM_T + 2)))(gcol),
                writes=("st",))

    tern_pass1(wg_sh, 0, 8, 2048)
    tern_pass1(wu_sh, 1, 8, 2048)
    tern_pass1(wd_sh, 2, 16, 1024)

    NF4S = [(w1_sh, 0, 1024, 6, 2048, 3), (w2_sh, 1, 1216, 6, 2048, 4),
            (w3_sh, 2, 1408, 2, 5632, 5)]
    for dram, widx, scol, tiles, width, gcol in NF4S:
        nblk = width // 64
        for t in range(tiles):
            nsub = (width + 2047) // 2048
            for s in range(nsub):
                c0 = 2048 * s
                w = min(2048, width - c0)
                dma("sync", wbv(0, w), dram[128 * t:128 * (t + 1), c0:c0 + w],
                    writes=wbnames(0, w))
                op("vector", (lambda w_, sc: lambda e: e.reduce_max(
                    SM[:, sc:sc + w_ // 64].unsqueeze(2),
                    WB[:, 0:w_].rearrange("p (b i) -> p b i", i=64),
                    axis=AX.X, apply_absolute_value=True))(
                        w, scol + t * nblk + c0 // 64),
                    reads=wbnames(0, w), writes=("smsb",))
        op("vector", (lambda sc, n: lambda e: e.reduce_sum(
            smc(SM_T + 2), SM[:, sc:sc + n], axis=AX.X))(scol, tiles * nblk),
            reads=("smsb",), writes=("st",))
        op("vector", (lambda c: lambda e: e.tensor_add(
            smc(SM_G + c), smc(SM_G + c), smc(SM_T + 2)))(gcol),
            writes=("st",))

    dma("sync", ar_in, SM[:, SM_G:SM_G + 8], reads=("st",), writes=("ar_in",))
    op("gpsimd", lambda e: e.collective_compute(
        "AllReduce", ALU.add, replica_groups=ALL8,
        ins=[ar_in.opt()], outs=[ar_out.opt()]),
        reads=("ar_in",), writes=("ar_out",), sem="cc")
    dma("sync", SM[:, SM_G:SM_G + 8], ar_out, reads=("ar_out",), writes=("st",))
    op("tensor", lambda e: e.matmul(ps_a[0:1, 0:8], smc(SM_ONE),
                                    SM[:, SM_G:SM_G + 8], start=True, stop=True),
       reads=("st",), writes=("ps_a",))
    op("vector", lambda e: e.tensor_copy(SM[0:1, SM_T + 4:SM_T + 12],
                                         ps_a[0:1, 0:8]),
       reads=("ps_a",), writes=("st",))
    op("tensor", lambda e: e.matmul(ps_a[:, 0:8], SM[0:1, SM_ROW:SM_ROW + 128],
                                    SM[0:1, SM_T + 4:SM_T + 12],
                                    start=True, stop=True),
       reads=("st",), writes=("ps_a",))
    op("vector", lambda e: e.tensor_copy(SM[:, SM_G:SM_G + 8], ps_a[:, 0:8]),
       reads=("ps_a",), writes=("st",))
    NW = float(8192 * 2048)
    for j in range(3):
        op("vector", (lambda jj: lambda e: e.tensor_scalar(
            smc(SM_THR + 3 + jj), smc(SM_G + jj), 1.0 / NW, 1e-5,
            ALU.mult, ALU.max))(j), writes=("st",))
        op("vector", (lambda jj: lambda e: e.tensor_scalar_mul(
            smc(SM_THR + jj), smc(SM_THR + 3 + jj), 0.5))(j), writes=("st",))
        op("vector", (lambda jj: lambda e: e.tensor_scalar_mul(
            smc(SM_NTHR + jj), smc(SM_THR + jj), -1.0))(j), writes=("st",))
        op("vector", (lambda jj: lambda e: e.tensor_scalar_mul(
            smc(SM_THR + 6 + jj), smc(SM_G + 3 + jj), 1.0 / NBLK_REAL))(j),
            writes=("st",))
        op("scalar", (lambda jj: lambda e: e.copy(
            wbv(11264 + jj, 1), smc(SM_THR + 6 + jj)))(j),
            reads=("st",), writes=("wbm",))
        op("scalar", (lambda jj: lambda e: e.copy(
            smc(SM_NTHR + 3 + jj), wbv(11264 + jj, 1)))(j),
            reads=("wbm",), writes=("st",))

    def emit_proj(layer):
        srcs = {"q": (wq_d, 0), "k": (wk_d, 0), "v": (wv_d, 0)} if layer == 0 \
            else {"q": (wqkv_d, 0), "k": (wqkv_d, 2048), "v": (wqkv_d, 4096)}
        dsts = {"k": kT_in[layer], "v": vT_in[layer], "q": qT_dr[layer]}
        for which in ("k", "v", "q"):
            wsrc, coff = srcs[which]
            for t in range(16):
                slot = 18432 + 2048 * (t % 2)
                dma("gpsimd",
                    aav(slot, 2048).rearrange("p (ft c) -> p ft c", c=128),
                    wsrc[:, coff + 128 * t:coff + 128 * (t + 1)]
                    .rearrange("(ft p) c -> p ft c", p=128),
                    writes=aanames(slot, 2048))
                items = []
                for n2 in range(2):
                    for ft in range(16):
                        items.append(mmf(
                            ps_a[:, 512 * n2:512 * (n2 + 1)],
                            aav(slot + 128 * ft, 128),
                            aav(1024 * ft + 512 * n2, 512),
                            ft == 0, ft == 15))
                mm_group(items, reads=aanames(slot, 2048) + aanames(0, 16384),
                         writes=("ps_a",))
                stg = 22528 + 1024 * (t % 2)
                if layer == 1:
                    boff = {"q": 0, "k": 16, "v": 32}[which] + t
                    op("scalar", (lambda st_, bo: lambda e: e.activation(
                        aav(st_, 1024), ps_a[:], AF.Identity,
                        bias=smc(SM_BQ + bo)))(stg, boff),
                        reads=("ps_a", "st"), writes=aanames(stg, 1024))
                else:
                    op("scalar", (lambda st_: lambda e: e.copy(
                        aav(st_, 1024), ps_a[:]))(stg),
                        reads=("ps_a",), writes=aanames(stg, 1024))
                dma("sync", dsts[which][128 * t:128 * (t + 1), :],
                    aav(stg, 1024), reads=aanames(stg, 1024),
                    writes=(f"{which}T{layer}_in",))
                yield
            if which in ("k", "v"):
                agi = kT_in[layer] if which == "k" else vT_in[layer]
                ago = kT_ag[layer] if which == "k" else vT_ag[layer]
                ag(agi, ago, f"{which}T{layer}_in", f"{which}T{layer}_ag",
                   PAIRS)

    def transpose_tile_out(src_c0, width, stage_c0, reads):
        nt = width // 128
        items = [mmf(ps_big[:, 128 * j:128 * (j + 1)],
                     wbv(src_c0 + 128 * j, 128), IDT[:], True, True)
                 for j in range(nt)]
        mm_group(items, reads=tuple(reads) + ("idt",), writes=("ps_big",))
        op("vector", (lambda w: lambda e: e.tensor_copy(
            wbv(stage_c0, w), ps_big[:, 0:w]))(width),
            reads=("ps_big",), writes=wbnames(stage_c0, width))

    def tern_pass2(dram, gidx, tiles, width, dst, dstname):
        for t in range(tiles):
            dma("sync", FS[:, 0:width], dram[128 * t:128 * (t + 1), :],
                writes=("fs",))
            op("vector", (lambda w, g: lambda e: e.tensor_single_scalar(
                wbv(0, w), FS[:, 0:w], smc(SM_THR + g), ALU.is_gt))(width, gidx),
                reads=("fs", "st"), writes=wbnames(0, width))
            op("vector", (lambda w, g: lambda e: e.tensor_scalar(
                wbv(2048, w), FS[:, 0:w], smc(SM_NTHR + g), -1.0,
                ALU.is_gt, ALU.add))(width, gidx),
                reads=("fs", "st"), writes=wbnames(2048, width))
            op("vector", (lambda w: lambda e: e.tensor_add(
                wbv(4096, w), wbv(0, w), wbv(2048, w)))(width),
                reads=wbnames(0, width) + wbnames(2048, width),
                writes=wbnames(4096, width))
            transpose_tile_out(4096, width, 8192, wbnames(4096, width))
            dma("sync",
                dst[:, 128 * t:128 * (t + 1)]
                .rearrange("(f p) c -> p f c", p=128),
                wbv(8192, width).rearrange("p (f c) -> p f c", c=128),
                reads=wbnames(8192, width), writes=(dstname,))
            yield

    def nf4_pass2(dram, widx, tiles, width, scol, dst, dstname):
        nblk = width // 64
        nsc = tiles * nblk
        nsb = nsc * 128 // 256
        P = nsb
        flat = nsc * 128
        rowc = nblk if width == 5632 else 32
        dma("sync", SM[0:P, SM_SB1:SM_SB1 + 256],
            s_rt[widx].flatten()[0:flat].rearrange("(s c) -> s c", c=256),
            reads=("s_rt",), writes=("smsb1",))
        op("vector", (lambda P_, w_: lambda e: e.tensor_single_scalar(
            SM[0:P_, SM_SB2:SM_SB2 + 256], SM[0:P_, SM_SB1:SM_SB1 + 256],
            SM[0:P_, SM_THR + 6 + w_:SM_THR + 7 + w_], ALU.subtract))(P, widx),
            reads=("smsb1", "st"), writes=("smsb2",))
        op("vector", (lambda P_: lambda e: e.reduce_max(
            SM[0:P_, SM_T + 4:SM_T + 5], SM[0:P_, SM_SB2:SM_SB2 + 256],
            axis=AX.X, apply_absolute_value=True))(P),
            reads=("smsb2",), writes=("st",))
        op("vector", (lambda P_: lambda e: e.reciprocal(
            SM[0:P_, SM_T + 5:SM_T + 6], SM[0:P_, SM_T + 4:SM_T + 5]))(P),
            writes=("st",))
        op("vector", (lambda P_: lambda e: e.tensor_scalar_mul(
            SM[0:P_, SM_T + 5:SM_T + 6], SM[0:P_, SM_T + 5:SM_T + 6],
            128.0))(P), writes=("st",))
        op("vector", (lambda P_: lambda e: e.tensor_scalar(
            SM[0:P_, SM_SB2:SM_SB2 + 256], SM[0:P_, SM_SB2:SM_SB2 + 256],
            SM[0:P_, SM_T + 5:SM_T + 6], MAGIC, ALU.mult, ALU.add))(P),
            reads=("st",), writes=("smsb2",))
        op("vector", (lambda P_: lambda e: e.tensor_scalar(
            SM[0:P_, SM_SB2:SM_SB2 + 256], SM[0:P_, SM_SB2:SM_SB2 + 256],
            -MAGIC, -128.0, ALU.add, ALU.max))(P), writes=("smsb2",))
        op("vector", (lambda P_: lambda e: e.tensor_scalar_min(
            SM[0:P_, SM_SB2:SM_SB2 + 256], SM[0:P_, SM_SB2:SM_SB2 + 256],
            127.0))(P), writes=("smsb2",))
        op("vector", (lambda P_: lambda e: e.tensor_scalar_mul(
            SM[0:P_, SM_T + 4:SM_T + 5], SM[0:P_, SM_T + 4:SM_T + 5],
            1.0 / 128.0))(P), writes=("st",))
        op("vector", (lambda P_: lambda e: e.tensor_single_scalar(
            WB[0:P_, 8192:8448], SM[0:P_, SM_SB2:SM_SB2 + 256],
            SM[0:P_, SM_T + 4:SM_T + 5], ALU.mult))(P),
            reads=("smsb2", "st"), writes=wbnames(8192, 256))
        op("vector", (lambda P_, w_: lambda e: e.tensor_single_scalar(
            WB[0:P_, 8192:8448], WB[0:P_, 8192:8448],
            SM[0:P_, SM_NTHR + 3 + w_:SM_NTHR + 4 + w_], ALU.add))(P, widx),
            reads=("st",), writes=wbnames(8192, 256))
        op("vector", (lambda P_: lambda e: e.tensor_scalar_max(
            SM[0:P_, SM_SB1:SM_SB1 + 256], SM[0:P_, SM_SB1:SM_SB1 + 256],
            1e-30))(P), writes=("smsb1",))
        op("vector", (lambda P_: lambda e: e.reciprocal(
            SM[0:P_, SM_SB2:SM_SB2 + 256], SM[0:P_, SM_SB1:SM_SB1 + 256]))(P),
            reads=("smsb1",), writes=("smsb2",))
        dma("sync", d_rt[widx].flatten()[0:flat].rearrange(
            "(s c) -> s c", c=256)[0:P, :], WB[0:P, 8192:8448],
            reads=wbnames(8192, 256), writes=("d_rt",))
        dma("sync", r_rt[widx].flatten()[0:flat].rearrange(
            "(s c) -> s c", c=256)[0:P, :], SM[0:P, SM_SB2:SM_SB2 + 256],
            reads=("smsb2",), writes=("r_rt",))
        dma("sync", WB[:, 8448:8448 + nsc].rearrange(
            "p (t c) -> p t c", c=rowc),
            d_rt[widx].flatten()[0:flat].rearrange(
                "(t p c) -> p t c", p=128, c=rowc),
            reads=("d_rt",), writes=wbnames(8448, nsc))
        dma("sync", FS[:, 0:nsc].rearrange("p (t c) -> p t c", c=rowc),
            r_rt[widx].flatten()[0:flat].rearrange(
                "(t p c) -> p t c", p=128, c=rowc),
            reads=("r_rt",), writes=("fs",))
        yield
        for t in range(tiles):
            nsub = (width + 2047) // 2048
            for s in range(nsub):
                c0 = 2048 * s
                w = min(2048, width - c0)
                bc = t * nblk + c0 // 64
                dma("sync", wbv(0, w), dram[128 * t:128 * (t + 1), c0:c0 + w],
                    writes=wbnames(0, w))
                op("vector", (lambda w_, b_: lambda e: e.tensor_tensor(
                    wbv(2048, w_).rearrange("p (b i) -> p b i", i=64),
                    wbv(0, w_).rearrange("p (b i) -> p b i", i=64),
                    FS[:, b_:b_ + w_ // 64].unsqueeze(2)
                    .broadcast_to([128, w_ // 64, 64]), ALU.mult))(w, bc),
                    reads=wbnames(0, w) + ("fs",), writes=wbnames(2048, w))
                op("vector", (lambda w_: lambda e: e.tensor_scalar(
                    wbv(4096, w_), wbv(2048, w_), smc(SM_NF4M),
                    float(NF4_DELTA[0]), ALU.is_gt, ALU.mult))(w),
                    reads=wbnames(2048, w) + ("st",), writes=wbnames(4096, w))
                op("vector", (lambda w_: lambda e: e.tensor_scalar_add(
                    wbv(4096, w_), wbv(4096, w_), NF4_BASE))(w),
                    writes=wbnames(4096, w))
                for kk in range(1, 15):
                    op("vector", (lambda w_, k_: lambda e: e.tensor_scalar(
                        wbv(6144, w_), wbv(2048, w_), smc(SM_NF4M + k_),
                        float(NF4_DELTA[k_]), ALU.is_gt, ALU.mult))(w, kk),
                        reads=wbnames(2048, w) + ("st",),
                        writes=wbnames(6144, w))
                    op("vector", (lambda w_: lambda e: e.tensor_add(
                        wbv(4096, w_), wbv(4096, w_), wbv(6144, w_)))(w),
                        reads=wbnames(6144, w), writes=wbnames(4096, w))
                op("vector", (lambda w_, b_: lambda e: e.tensor_tensor(
                    wbv(0, w_).rearrange("p (b i) -> p b i", i=64),
                    wbv(4096, w_).rearrange("p (b i) -> p b i", i=64),
                    WB[:, 8448 + b_:8448 + b_ + w_ // 64].unsqueeze(2)
                    .broadcast_to([128, w_ // 64, 64]), ALU.mult))(w, bc),
                    reads=wbnames(4096, w) + wbnames(8448, nsc),
                    writes=wbnames(0, w))
                transpose_tile_out(0, w, 9472, wbnames(0, w))
                dma("sync",
                    dst[c0:c0 + w, 128 * t:128 * (t + 1)]
                    .rearrange("(f p) c -> p f c", p=128),
                    wbv(9472, w).rearrange("p (f c) -> p f c", c=128),
                    reads=wbnames(9472, w), writes=(dstname,))
                yield

    def side_steps():
        yield from tern_pass2(wg_sh, 0, 8, 2048, wgT_in, "wgT_in")
        ag(wgT_in, wgT_ag, "wgT_in", "wgT_ag", ALL8)
        yield from tern_pass2(wu_sh, 1, 8, 2048, wuT_in, "wuT_in")
        ag(wuT_in, wuT_ag, "wuT_in", "wuT_ag", ALL8)
        yield from tern_pass2(wd_sh, 2, 16, 1024, wdT_in, "wdT_in")
        ag(wdT_in, wdT_ag, "wdT_in", "wdT_ag", ALL8)
        for widx, scol, tiles, width in ((0, 1024, 6, 2048),
                                         (1, 1216, 6, 2048),
                                         (2, 1408, 2, 5632)):
            nblk = width // 64
            nsc = tiles * nblk
            flat = nsc * 128
            rowc = nblk if width == 5632 else 32
            dma("sync",
                s_rt[widx].flatten()[0:flat].rearrange(
                    "(t p c) -> p t c", p=128, c=rowc),
                SM[:, scol:scol + nsc].rearrange("p (t c) -> p t c", c=rowc),
                reads=("smsb",), writes=("s_rt",))
        yield
        yield from nf4_pass2(w1_sh, 0, 6, 2048, 1024, w1T_in, "w1T_in")
        yield from nf4_pass2(w2_sh, 1, 6, 2048, 1216, w2T_in, "w2T_in")
        yield from nf4_pass2(w3_sh, 2, 2, 5632, 1408, w3T_in, "w3T_in")

    _sg = side_steps()
    for _ in emit_proj(0):
        next(_sg, None)
    for _ in _sg:
        pass

    def emit_ffn0():
        for m in range(8):
            op("vector", (lambda mi: lambda e: e.tensor_tensor(
                smc(SM_SCG + mi), smc(SM_ISX + mi), smc(SM_THR + 3),
                ALU.mult))(m), writes=("st",))
            op("vector", (lambda mi: lambda e: e.tensor_tensor(
                smc(SM_SCU + mi), smc(SM_ISX + mi), smc(SM_THR + 4),
                ALU.mult))(m), writes=("st",))
        for th in range(2):
            ms = [4 * th + i for i in range(4)]
            hslot = [8192, 16384, 24576, 32768]
            for i, m in enumerate(ms):
                xt = x_sb[:, 2048 * m:2048 * (m + 1)]
                for cchunk in range(4):
                    c0 = 512 * cchunk
                    op("vector", (lambda xt_, mi, c_: lambda e:
                        e.scalar_tensor_tensor(
                            out=SM[:, SM_QT:SM_QT + 512],
                            in0=xt_[:, c_:c_ + 512],
                            scalar=smc(SM_MEAN0 + mi),
                            in1=smc(SM_C1X + mi).broadcast_to([128, 512]),
                            op0=ALU.subtract, op1=ALU.mult))(xt, m, c0),
                        reads=("x", "st"), writes=("smqt",))
                    op("vector", lambda e: e.tensor_scalar(
                        SM[:, SM_QT:SM_QT + 512], SM[:, SM_QT:SM_QT + 512],
                        MAGIC, -MAGIC, ALU.add, ALU.add), writes=("smqt",))
                    op("vector", (lambda c_: lambda e: e.tensor_copy(
                        aav(SCR + c_, 512), SM[:, SM_QT:SM_QT + 512]))(c0),
                        reads=("smqt",), writes=aanames(SCR, 2048))
                items = [mmf(ps_big[:, 128 * ft:128 * (ft + 1)],
                             aav(SCR + 128 * ft, 128), IDT[:], True, True)
                         for ft in range(16)]
                mm_group(items, reads=aanames(SCR, 2048) + ("idt",),
                         writes=("ps_big",))
                op("vector", (lambda ii: lambda e: e.tensor_copy(
                    AA[:, 0:8192].rearrange("p (t q) -> p t q", q=512)
                    [:, :, 128 * ii:128 * (ii + 1)],
                    ps_big[:].rearrange("p (t q) -> p t q", q=128)))(i),
                    reads=("ps_big",), writes=aanames(0, 8192))
            for m in ms:
                op("vector", (lambda mi: lambda e: e.memset(
                    smc(SM_VAR + mi), 0.0))(m), writes=("st",))
            for i in range(4):
                op("vector", (lambda ii: lambda e: e.memset(
                    smc(SM_T + 12 + ii), 0.0))(i), writes=("st",))
            for n in range(16):
                r, off = (512 * n) // 1024, (512 * n) % 1024
                dma("sync", WB[:, 0:8192].rearrange("p (f c) -> p f c", c=512),
                    wgT_ag[2048 * r:2048 * (r + 1), off:off + 512]
                    .rearrange("(f p) c -> p f c", p=128),
                    writes=wbnames(0, 8192), extra=res.rd("wgT_ag"))
                dma("sync", AA[:, 40960:49152].rearrange(
                    "p (f c) -> p f c", c=512),
                    wuT_ag[2048 * r:2048 * (r + 1), off:off + 512]
                    .rearrange("(f p) c -> p f c", p=128),
                    writes=aanames(40960, 8192), extra=res.rd("wuT_ag"))
                dma("gpsimd", wbv(8960, 512),
                    lnw_d[:, 512 * n:512 * (n + 1)],
                    writes=wbnames(8960, 512))
                for i, m in enumerate(ms):
                    items = [mmf(ps_a[:, 0:512], aav(512 * ft + 128 * i, 128),
                                 wbv(512 * ft, 512), ft == 0, ft == 15)
                             for ft in range(16)]
                    mm_group(items, reads=aanames(0, 8192) + wbnames(0, 8192),
                             writes=("ps_a",))
                    items = [mmf(ps_b[:, 0:512], aav(512 * ft + 128 * i, 128),
                                 aav(40960 + 512 * ft, 512), ft == 0, ft == 15)
                             for ft in range(16)]
                    mm_group(items,
                             reads=aanames(0, 8192) + aanames(40960, 8192),
                             writes=("ps_b",))
                    op("scalar", (lambda mi: lambda e: e.activation(
                        wbv(8192, 512), ps_a[:, 0:512], AF.Silu,
                        scale=smc(SM_SCG + mi)))(m),
                        reads=("ps_a", "st"), writes=wbnames(8192, 512))
                    hc = hslot[i] + 512 * n
                    op("vector", (lambda mi, hc_: lambda e:
                        e.scalar_tensor_tensor(
                            out=aav(hc_, 512), in0=ps_b[:, 0:512],
                            scalar=smc(SM_SCU + mi), in1=wbv(8192, 512),
                            op0=ALU.mult, op1=ALU.mult))(m, hc),
                        reads=("ps_b", "st") + wbnames(8192, 512),
                        writes=aanames(hc, 512))
                    op("scalar", (lambda hc_: lambda e: e.activation(
                        wbv(9984, 512), aav(hc_, 512), AF.Square,
                        accum_out=smc(SM_T + 2)))(hc),
                        reads=aanames(hc, 512),
                        writes=wbnames(9984, 512) + ("st",))
                    op("vector", (lambda mi: lambda e: e.tensor_add(
                        smc(SM_VAR + mi), smc(SM_VAR + mi),
                        smc(SM_T + 2)))(m), writes=("st",))
                    op("vector", (lambda hc_: lambda e: e.tensor_tensor(
                        SM[:, SM_QT:SM_QT + 512], aav(hc_, 512),
                        wbv(8960, 512), ALU.mult))(hc),
                        reads=aanames(hc, 512) + wbnames(8960, 512),
                        writes=("smqt",))
                    op("vector", lambda e: e.reduce_max(
                        smc(SM_T + 2), SM[:, SM_QT:SM_QT + 512], axis=AX.X,
                        apply_absolute_value=True),
                        reads=("smqt",), writes=("st",))
                    op("vector", (lambda ii: lambda e: e.tensor_max(
                        smc(SM_T + 12 + ii), smc(SM_T + 12 + ii),
                        smc(SM_T + 2)))(i), writes=("st",))
            for i, m in enumerate(ms):
                op("vector", (lambda mi: lambda e: e.tensor_scalar(
                    smc(SM_T + 2), smc(SM_VAR + mi), 1.0 / 8192.0, 1e-6,
                    ALU.mult, ALU.add))(m), writes=("st",))
                op("scalar", lambda e: e.activation(
                    smc(SM_T + 2), smc(SM_T + 2), AF.Ln),
                    reads=("st",), writes=("st",))
                op("scalar", lambda e: e.activation(
                    smc(SM_T + 3), smc(SM_T + 2), AF.Exp, scale=-0.5),
                    reads=("st",), writes=("st",))
                op("vector", (lambda ii: lambda e: e.tensor_tensor(
                    smc(SM_T + 2), smc(SM_T + 3), smc(SM_T + 12 + ii),
                    ALU.mult))(i), writes=("st",))
                op("vector", lambda e: e.tensor_scalar_max(
                    smc(SM_T + 2), smc(SM_T + 2), 1e-5), writes=("st",))
                op("vector", (lambda mi: lambda e: e.tensor_scalar_mul(
                    smc(SM_C3 + mi), smc(SM_T + 2), 1.0 / 127.0))(m),
                    writes=("st",))
                op("vector", lambda e: e.reciprocal(
                    smc(SM_T + 2), smc(SM_T + 2)), writes=("st",))
                op("vector", lambda e: e.scalar_tensor_tensor(
                    out=smc(SM_T + 2), in0=smc(SM_T + 2), scalar=127.0,
                    in1=smc(SM_T + 3), op0=ALU.mult, op1=ALU.mult),
                    writes=("st",))
                op("vector", (lambda mi: lambda e: e.tensor_tensor(
                    smc(SM_C3 + mi), smc(SM_C3 + mi), smc(SM_THR + 5),
                    ALU.mult))(m), writes=("st",))
                for n in range(16):
                    hc = hslot[i] + 512 * n
                    dma("gpsimd", wbv(8960, 512),
                        lnw_d[:, 512 * n:512 * (n + 1)],
                        writes=wbnames(8960, 512))
                    op("vector", (lambda hc_: lambda e: e.tensor_tensor(
                        SM[:, SM_QT:SM_QT + 512], aav(hc_, 512),
                        wbv(8960, 512), ALU.mult))(hc),
                        reads=aanames(hc, 512) + wbnames(8960, 512),
                        writes=("smqt",))
                    op("vector", lambda e: e.tensor_scalar(
                        SM[:, SM_QT:SM_QT + 512], SM[:, SM_QT:SM_QT + 512],
                        smc(SM_T + 2), MAGIC, ALU.mult, ALU.add),
                        reads=("st",), writes=("smqt",))
                    op("vector", (lambda hc_: lambda e: e.tensor_scalar_add(
                        aav(hc_, 512), SM[:, SM_QT:SM_QT + 512], -MAGIC))(hc),
                        reads=("smqt",), writes=aanames(hc, 512))
                for batch in range(4):
                    c0 = hslot[i] + 2048 * batch
                    items = [mmf(ps_big[:, 128 * j:128 * (j + 1)],
                                 aav(c0 + 128 * j, 128), IDT[:], True, True)
                             for j in range(16)]
                    mm_group(items, reads=aanames(c0, 2048) + ("idt",),
                             writes=("ps_big",))
                    op("vector", (lambda c_: lambda e: e.tensor_copy(
                        aav(c_, 2048), ps_big[:]))(c0),
                        reads=("ps_big",), writes=aanames(c0, 2048))
            for dg in range(8):
                wdc = []
                for jj in range(8):
                    d = 8 * dg + jj
                    dst_c = (2048 * jj) if jj < 4 else (40960 + 2048 * (jj - 4))
                    wdc.append(dst_c)
                    dma("sync", aav(dst_c, 2048),
                        wdT_ag[128 * d:128 * (d + 1), :],
                        writes=aanames(dst_c, 2048),
                        extra=res.rd("wdT_ag"))
                for i, m in enumerate(ms):
                    items = []
                    for jj in range(8):
                        d = 8 * dg + jj
                        for n4 in range(4):
                            items.append(mmf(
                                ps_big[:, 512 * n4:512 * (n4 + 1)],
                                aav(hslot[i] + 128 * d, 128),
                                aav(wdc[jj] + 512 * n4, 512),
                                jj == 0, jj == 7))
                    rdn = aanames(hslot[i], 8192) + aanames(0, 8192) \
                        + aanames(40960, 8192)
                    mm_group(items, reads=rdn, writes=("ps_big",))
                    op("vector", (lambda mi: lambda e: e.scalar_tensor_tensor(
                        out=x_sb[:, 2048 * mi:2048 * (mi + 1)],
                        in0=ps_big[:], scalar=smc(SM_C3 + mi),
                        in1=x_sb[:, 2048 * mi:2048 * (mi + 1)],
                        op0=ALU.mult, op1=ALU.add))(m),
                        reads=("ps_big", "st", "x"), writes=("x",))

    emit_ffn0()
    ag(w1T_in, w1T_ag, "w1T_in", "w1T_ag", ALL8)
    ag(w2T_in, w2T_ag, "w2T_in", "w2T_ag", ALL8)
    ag(w3T_in, w3T_ag, "w3T_in", "w3T_ag", ALL8)

    def emit_attn(layer):
        use_mask = layer == 0
        wo_src = wo_d if layer == 0 else wout_d
        kag, vag, qdr = kT_ag[layer], vT_ag[layer], qT_dr[layer]
        kname, vname = f"kT{layer}_ag", f"vT{layer}_ag"
        qname = f"qT{layer}_in"
        tls = TLS if layer == 0 else [list(range(16))] * 8
        offs = [0]
        for i in range(8):
            offs.append(offs[-1] + 128 * len(tls[i]))
        if layer == 1:
            dma("sync", wbv(0, 2048), bout_d, writes=wbnames(0, 2048))
            for m in range(8):
                op("gpsimd", (lambda mi: lambda e: e.tensor_add(
                    x_sb[:, 2048 * mi:2048 * (mi + 1)],
                    x_sb[:, 2048 * mi:2048 * (mi + 1)], wbv(0, 2048)))(m),
                    reads=wbnames(0, 2048) + ("x",), writes=("x",))

        def psb_names(qt):
            if layer == 1 or len(tls[qt]) > 8:
                return ("ps_big", "psbg0", "psbg1")
            return (f"psbg{qt % 2}",)

        def softmax_stage(qt):
            tl = tls[qt]
            nt = len(tl)
            W = 128 * nt
            par = qt % 2
            dense = nt > 8
            st0 = SM_AT + 4 * par
            if layer == 1:
                pc = 6144 + 2048 * par
                po = 0
            else:
                pc = 6144 if dense else 6144 + 1024 * par
                po = 0 if dense else 1024 * par
            mc = 8192 if dense else 8192 + 1024 * par
            psn = psb_names(qt)
            san = (f"sa{par}",)
            items = []
            for s0, t0, L in _segs(tl):
                items.append(mmf(ps_big[:, po + 128 * s0:po + 128 * (s0 + L)],
                                 wbv(10240 + 128 * qt, 128),
                                 wbv(128 * t0, 128 * L), True, True))
            mm_group(items, reads=wbnames(10240, 1024) + wbnames(0, 2048),
                     writes=psn)
            if use_mask:
                dma("sync", wbv(mc, W), mask_d[qt][:, 0:W],
                    writes=wbnames(mc, W))
                op("vector", (lambda W_, mc_, po_: lambda e: e.tensor_add(
                    ps_big[:, po_:po_ + W_], ps_big[:, po_:po_ + W_],
                    wbv(mc_, W_)))(W, mc, po),
                    reads=wbnames(mc, W), writes=psn)
            op("vector", (lambda W_, c, po_: lambda e: e.reduce_max(
                smc(c), ps_big[:, po_:po_ + W_], axis=AX.X))(W, st0, po),
                reads=psn, writes=san)
            op("vector", (lambda c: lambda e: e.tensor_scalar_mul(
                smc(c + 1), smc(c), -C_ATTN))(st0), writes=san)
            op("scalar", (lambda W_, c, pc_, po_: lambda e: e.activation(
                wbv(pc_, W_), ps_big[:, po_:po_ + W_], AF.Exp,
                bias=smc(c + 1), scale=C_ATTN,
                accum_out=smc(c + 2)))(W, st0, pc, po),
                reads=psn + san, writes=wbnames(pc, W) + san)
            op("vector", (lambda c: lambda e: e.reciprocal(
                smc(c + 3), smc(c + 2)))(st0), writes=san)
            op("scalar", (lambda W_, c, pc_: lambda e: e.activation(
                wbv(pc_, W_), wbv(pc_, W_), AF.Identity,
                scale=smc(c + 3)))(W, st0, pc),
                reads=san, writes=wbnames(pc, W))

        def transpose_stage(qt):
            tl = tls[qt]
            nt = len(tl)
            par = qt % 2
            dense = nt > 8
            if layer == 1:
                pc = 6144 + 2048 * par
            else:
                pc = 6144 if dense else 6144 + 1024 * par
            if layer == 0:
                for w0 in range(0, nt, 8):
                    nw = min(8, nt - w0)
                    items = [mmf(ps_b[:, 128 * j:128 * (j + 1)],
                                 wbv(pc + 128 * (w0 + j), 128), IDT[:],
                                 True, True) for j in range(nw)]
                    mm_group(items, reads=wbnames(pc, 128 * nt) + ("idt",),
                             writes=("ps_b",))
                    dst = offs[qt] + 128 * w0
                    op("scalar", (lambda d_, n_: lambda e: e.copy(
                        AA[:, d_:d_ + 128 * n_],
                        ps_b[:, 0:128 * n_]))(dst, nw),
                        reads=("ps_b",), writes=aanames(dst, 128 * nw))
            else:
                items = [mmf(ps_a[:, 128 * j:128 * (j + 1)],
                             wbv(pc + 128 * j, 128), IDT[:], True, True)
                         for j in range(8)]
                mm_group(items, reads=wbnames(pc, 2048) + ("idt",),
                         writes=("ps_a",))
                items = [mmf(ps_b[:, 128 * j:128 * (j + 1)],
                             wbv(pc + 1024 + 128 * j, 128), IDT[:],
                             True, True) for j in range(8)]
                mm_group(items, reads=wbnames(pc, 2048) + ("idt",),
                         writes=("ps_b",))
                op("scalar", (lambda q_: lambda e: e.copy(
                    AA[:, 0:16384].rearrange("p (j q) -> p j q", q=1024)
                    [:, 0:8, 128 * q_:128 * (q_ + 1)],
                    ps_a[:].rearrange("p (j q) -> p j q", q=128)))(qt),
                    reads=("ps_a",), writes=aanames(0, 8192))
                op("scalar", (lambda q_: lambda e: e.copy(
                    AA[:, 0:16384].rearrange("p (j q) -> p j q", q=1024)
                    [:, 8:16, 128 * q_:128 * (q_ + 1)],
                    ps_b[:].rearrange("p (j q) -> p j q", q=128)))(qt),
                    reads=("ps_b",), writes=aanames(8192, 8192))

        for g in range(4):
            for hh in range(4):
                dma("gpsimd", aav(20480 + 2048 * hh, 2048),
                    wo_src[512 * g + 128 * hh:512 * g + 128 * (hh + 1), :],
                    writes=aanames(20480 + 2048 * hh, 2048))
            for hh in range(4):
                h = 4 * g + hh
                for half in range(2):
                    dma("sync", wbv(1024 * half, 1024),
                        kag[2048 * half + 128 * h:2048 * half + 128 * (h + 1), :],
                        writes=wbnames(1024 * half, 1024),
                        extra=res.rd(kname))
                    dma("sync", wbv(2048 + 1024 * half, 1024),
                        vag[2048 * half + 128 * h:2048 * half + 128 * (h + 1), :],
                        writes=wbnames(2048 + 1024 * half, 1024),
                        extra=res.rd(vname))
                dma("sync", wbv(10240, 1024), qdr[128 * h:128 * (h + 1), :],
                    writes=wbnames(10240, 1024), extra=res.rd(qname))
                items = [mmf(ps_a[:, 128 * j:128 * (j + 1)],
                             wbv(2048 + 128 * j, 128), IDT[:], True, True)
                         for j in range(8)]
                mm_group(items, reads=wbnames(2048, 2048) + ("idt",),
                         writes=("ps_a",))
                items = [mmf(ps_b[:, 128 * j:128 * (j + 1)],
                             wbv(2048 + 1024 + 128 * j, 128), IDT[:],
                             True, True) for j in range(8)]
                mm_group(items, reads=wbnames(2048, 2048) + ("idt",),
                         writes=("ps_b",))
                op("vector", lambda e: e.tensor_copy(wbv(4096, 1024),
                                                     ps_a[:]),
                   reads=("ps_a",), writes=wbnames(4096, 1024))
                op("vector", lambda e: e.tensor_copy(wbv(4096 + 1024, 1024),
                                                     ps_b[:]),
                   reads=("ps_b",), writes=wbnames(4096 + 1024, 1024))
                first_def = 1 if layer == 0 else 0
                for qt in range(8):
                    softmax_stage(qt)
                    if qt == 0 and first_def == 1:
                        transpose_stage(0)
                    elif qt > first_def:
                        transpose_stage(qt - 1)
                transpose_stage(7)
                if layer == 0:
                    for qt in range(8):
                        tl = tls[qt]
                        items = []
                        for s, t in enumerate(tl):
                            items.append(mmf(
                                ps_a[:, 128 * qt:128 * (qt + 1)],
                                wbv(4096 + 128 * t, 128),
                                AA[:, offs[qt] + 128 * s:
                                   offs[qt] + 128 * (s + 1)],
                                s == 0, s == len(tl) - 1))
                        mm_group(items,
                                 reads=aanames(offs[qt], 128 * len(tl))
                                 + wbnames(4096, 2048), writes=("ps_a",))
                else:
                    items = []
                    for j in range(16):
                        for n2 in range(2):
                            items.append(mmf(
                                ps_a[:, 512 * n2:512 * (n2 + 1)],
                                wbv(4096 + 128 * j, 128),
                                aav(1024 * j + 512 * n2, 512),
                                j == 0, j == 15))
                    mm_group(items,
                             reads=aanames(0, 16384) + wbnames(4096, 2048),
                             writes=("ps_a",))
                op("scalar", (lambda hh_: lambda e: e.copy(
                    aav(16384 + 1024 * hh_, 1024), ps_a[:]))(hh),
                    reads=("ps_a",), writes=aanames(16384 + 1024 * hh, 1024))
            for m in range(8):
                items = []
                for hh in range(4):
                    for n4 in range(4):
                        items.append(mmf(
                            ps_big[:, 512 * n4:512 * (n4 + 1)],
                            aav(16384 + 1024 * hh + 128 * m, 128),
                            aav(20480 + 2048 * hh + 512 * n4, 512),
                            hh == 0, hh == 3))
                mm_group(items, reads=aanames(16384, 4096) + aanames(20480, 8192),
                         writes=("ps_big", "psbg0", "psbg1"))
                op("vector", (lambda mi: lambda e: e.tensor_add(
                    x_sb[:, 2048 * mi:2048 * (mi + 1)],
                    x_sb[:, 2048 * mi:2048 * (mi + 1)], ps_big[:]))(m),
                    reads=("ps_big", "x"), writes=("x",))

    emit_attn(0)

    emit_ln(1)
    for _ in emit_proj(1):
        pass

    def emit_ffn1():
        for qd in range(4):
            dls = list(range(11 * qd, 11 * qd + 11))
            for idx, dl in enumerate(dls):
                r, off = (128 * dl) // 768, (128 * dl) % 768
                dma("sync", wbv(0, 2048).rearrange("p (f c) -> p f c", c=128),
                    w1T_ag[2048 * r:2048 * (r + 1), off:off + 128]
                    .rearrange("(f p) c -> p f c", p=128),
                    writes=wbnames(0, 2048), extra=res.rd("w1T_ag"))
                dma("sync", wbv(2048, 2048).rearrange("p (f c) -> p f c", c=128),
                    w2T_ag[2048 * r:2048 * (r + 1), off:off + 128]
                    .rearrange("(f p) c -> p f c", p=128),
                    writes=wbnames(2048, 2048), extra=res.rd("w2T_ag"))
                items = []
                for n2 in range(2):
                    for ft in range(16):
                        items.append(mmf(
                            ps_a[:, 512 * n2:512 * (n2 + 1)],
                            wbv(128 * ft, 128),
                            aav(1024 * ft + 512 * n2, 512),
                            ft == 0, ft == 15))
                mm_group(items, reads=wbnames(0, 2048) + aanames(0, 16384),
                         writes=("ps_a",))
                items = []
                for n2 in range(2):
                    for ft in range(16):
                        items.append(mmf(
                            ps_b[:, 512 * n2:512 * (n2 + 1)],
                            wbv(2048 + 128 * ft, 128),
                            aav(1024 * ft + 512 * n2, 512),
                            ft == 0, ft == 15))
                mm_group(items, reads=wbnames(2048, 2048) + aanames(0, 16384),
                         writes=("ps_b",))
                op("scalar", lambda e: e.activation(
                    wbv(4096, 1024), ps_a[:], AF.Silu),
                    reads=("ps_a",), writes=wbnames(4096, 1024))
                hc = 16384 + 1024 * idx
                op("vector", (lambda hc_: lambda e: e.tensor_tensor(
                    aav(hc_, 1024), wbv(4096, 1024), ps_b[:], ALU.mult))(hc),
                    reads=wbnames(4096, 1024) + ("ps_b",),
                    writes=aanames(hc, 1024))
            for idx, dl in enumerate(dls):
                wc = 27648 + 2048 * idx
                dma("sync", aav(wc, 2048).rearrange("p (r c) -> p r c", c=256),
                    w3T_ag.rearrange("(r d) c -> d r c", d=5632)
                    [128 * dl:128 * (dl + 1)],
                    writes=aanames(wc, 2048), extra=res.rd("w3T_ag"))
            for m in range(8):
                items = []
                for idx in range(11):
                    for n4 in range(4):
                        items.append(mmf(
                            ps_big[:, 512 * n4:512 * (n4 + 1)],
                            aav(16384 + 1024 * idx + 128 * m, 128),
                            aav(27648 + 2048 * idx + 512 * n4, 512),
                            idx == 0, idx == 10))
                mm_group(items,
                         reads=aanames(16384, 11264) + aanames(27648, 22528),
                         writes=("ps_big",))
                op("vector", (lambda mi: lambda e: e.tensor_add(
                    x_sb[:, 2048 * mi:2048 * (mi + 1)],
                    x_sb[:, 2048 * mi:2048 * (mi + 1)], ps_big[:]))(m),
                    reads=("ps_big", "x"), writes=("x",))

    emit_ffn1()
    emit_attn(1)

    fin = dma("sync", out_d.rearrange("(m p) f -> p m f", p=128),
              x_sb[:].rearrange("p (m f) -> p m f", f=2048), reads=("x",),
              writes=("outd",))
    p.emit("sync", lambda e: None, waits=[("d_outd", fin)])

    with nc.Block() as block:
        def mk(engname):
            recs = p.recs[engname]

            def body(e, recs=recs):
                for fn, waits, inc, n in recs:
                    for s, v in waits:
                        e.wait_ge(sems[s], v)
                    instr = fn(e)
                    if inc is not None:
                        instr.then_inc(sems[inc], n)
            return body

        block.sync(mk("sync"))
        block.gpsimd(mk("gpsimd"))
        block.tensor(mk("tensor"))
        block.vector(mk("vector"))
        block.scalar(mk("scalar"))

    ctx.close()
    return nc


def _blk_idx(r):
    return np.concatenate([np.arange(128 * (2 * i + r), 128 * (2 * i + r) + 128)
                           for i in range(8)])


def _mask_for_core(r):
    out = np.zeros((8, 128, 2048), np.float32)
    for i in range(8):
        tl = TLS[i]
        q = (128 * (2 * i + r) + np.arange(128))[:, None]
        k = np.concatenate([128 * (2 * (t % 8) + t // 8) + np.arange(128)
                            for t in tl])[None, :]
        allow = (np.abs(q - k) < 256) | (q < 16) | (k < 16)
        out[i, :, 0:128 * len(tl)] = np.where(allow, np.float32(0.0),
                                              np.float32(NEG))
    return out.astype(ml_dtypes.bfloat16)


_NC_CACHE = [None]


def kernel(L0_attn_hidden_states, L0_attn_Wq, L0_attn_Wk, L0_attn_Wv,
           L0_attn_Wo, L0_ffn_w_gate, L0_ffn_w_up, L0_ffn_w_down,
           L0_ffn_ln_weight, L1_attn_w_qkv, L1_attn_b_qkv, L1_attn_w_out,
           L1_attn_b_out, L1_ffn_weight1, L1_ffn_weight2, L1_ffn_weight3):
    xs = np.asarray(L0_attn_hidden_states)
    f32 = np.float32
    bq = np.ascontiguousarray(np.asarray(L1_attn_b_qkv, f32)
                              .reshape(48, 128).T)
    bout = np.broadcast_to(np.asarray(L1_attn_b_out).astype(ml_dtypes.bfloat16),
                           (128, 2048)).copy()
    lnw = np.broadcast_to(np.asarray(L0_ffn_ln_weight, f32), (128, 8192)).copy()
    ident = np.eye(128, dtype=ml_dtypes.bfloat16)
    nf4c = np.zeros((32,), f32)
    nf4c[0:15] = NF4_MID
    nf4c[16:31] = NF4_DELTA
    nf4c = np.broadcast_to(nf4c, (128, 32)).copy()
    w1p = np.zeros((6144, 2048), ml_dtypes.bfloat16)
    w1p[0:5632] = np.asarray(L1_ffn_weight1)
    w2p = np.zeros((6144, 2048), ml_dtypes.bfloat16)
    w2p[0:5632] = np.asarray(L1_ffn_weight2)
    w3 = np.asarray(L1_ffn_weight3)
    wg = np.asarray(L0_ffn_w_gate, f32)
    wu = np.asarray(L0_ffn_w_up, f32)
    wd = np.asarray(L0_ffn_w_down, f32)

    bidx = [_blk_idx(0), _blk_idx(1)]
    in_maps = []
    for c in range(8):
        b, hf = c // 2, c % 2
        in_maps.append({
            "x_in": np.ascontiguousarray(xs[b][bidx[hf]], f32),
            "wq": np.asarray(L0_attn_Wq, f32),
            "wk": np.asarray(L0_attn_Wk, f32),
            "wv": np.asarray(L0_attn_Wv, f32),
            "wo": np.asarray(L0_attn_Wo, f32),
            "wqkv": np.asarray(L1_attn_w_qkv, f32),
            "wout": np.asarray(L1_attn_w_out, f32),
            "bqkvT": bq,
            "bout_bc": bout,
            "wg_sh": np.ascontiguousarray(wg[1024 * c:1024 * (c + 1), :]),
            "wu_sh": np.ascontiguousarray(wu[1024 * c:1024 * (c + 1), :]),
            "wd_sh": np.ascontiguousarray(wd[:, 1024 * c:1024 * (c + 1)]),
            "lnw_bc": lnw,
            "w1_sh": np.ascontiguousarray(w1p[768 * c:768 * (c + 1), :]),
            "w2_sh": np.ascontiguousarray(w2p[768 * c:768 * (c + 1), :]),
            "w3_sh": np.ascontiguousarray(w3[256 * c:256 * (c + 1), :]),
            "mask_in": _mask_for_core(hf),
            "ident_in": ident,
            "nf4c_in": nf4c,
        })
    if _NC_CACHE[0] is None:
        _NC_CACHE[0] = build_program()
    nc = _NC_CACHE[0]
    r = run_bass_kernel_spmd(nc, in_maps, core_ids=list(range(8)))
    out = np.empty((4, 2048, 2048), np.float32)
    for c in range(8):
        b, hf = c // 2, c % 2
        out[b][bidx[hf]] = r.results[c]["out"]
    return out


if __name__ == "__main__":
    nc = build_program()
    print("build ok")


# version 12
# speedup vs baseline: 1.4568x; 1.0044x over previous
import numpy as np
import ml_dtypes
import concourse.bass as bass
import concourse.mybir as mybir
from concourse.bass_utils import run_bass_kernel_spmd

F32 = mybir.dt.float32
BF16 = mybir.dt.bfloat16
AF = mybir.ActivationFunctionType
ALU = mybir.AluOpType
AX = mybir.AxisListType

MAGIC = float(np.float32(3 << 22))
NEG = -1.0e30
C_ATTN = float(128.0 ** -0.5)

NF4 = np.array([-1.0, -0.6962, -0.5251, -0.3949, -0.2844, -0.1848, -0.0911,
                0.0, 0.0796, 0.1609, 0.2461, 0.3379, 0.4407, 0.5626, 0.723,
                1.0], dtype=ml_dtypes.bfloat16).astype(np.float64)
NF4_MID = ((NF4[:-1] + NF4[1:]) / 2.0).astype(np.float32)
NF4_DELTA = np.diff(NF4).astype(np.float32)
NF4_BASE = float(NF4[0])
NBLK_REAL = 5632 * 2048 // 64

ENGINES = ("sync", "gpsimd", "tensor", "vector", "scalar")

SM_MEAN0, SM_RS0, SM_C1X, SM_ISX = 0, 8, 16, 24
SM_MEAN1, SM_RS1 = 48, 56
SM_G = 64
SM_THR = 72
SM_NTHR = 81
SM_NF4M, SM_NF4D = 96, 112
SM_BQ = 128
SM_ONE = 176
SM_T = 178
SM_SCG, SM_SCU, SM_C3, SM_VAR = 224, 232, 240, 248
SM_ROW = 256
SM_AT = 384
SM_SB1, SM_SB2, SM_QT = 1024, 1280, 1536



def _tl(i):
    if i == 0:
        return list(range(16))
    lo = [t for t in (i - 1, i, i + 1) if 0 <= t <= 7]
    return sorted(set([0] + lo + [8 + t for t in lo]))


TLS = [_tl(i) for i in range(8)]


def _segs(tl):
    runs = []
    for s, t in enumerate(tl):
        if runs and runs[-1][0] + runs[-1][2] == s \
                and runs[-1][1] + runs[-1][2] == t:
            runs[-1][2] += 1
        else:
            runs.append([s, t, 1])
    segs = []
    for s0, t0, L in runs:
        s = s0
        while s < s0 + L:
            e = min(s0 + L, (s // 4 + 1) * 4)
            segs.append((s, t0 + (s - s0), e - s))
            s = e
    return segs

AA_COLS = 50176
WB_COLS = 12288


class Prog:
    def __init__(self):
        self.recs = {e: [] for e in ENGINES}
        self.cnt = {}

    def emit(self, eng, fn, waits=(), inc=None, n=1):
        self.recs[eng].append((fn, tuple(waits), inc, n))
        if inc is not None:
            self.cnt[inc] = self.cnt.get(inc, 0) + n
        return self.cnt.get(inc)


class Res:
    def __init__(self):
        self.w = {}
        self.r = {}

    def rd(self, name):
        return [self.w[name]] if name in self.w else []

    def wr(self, name):
        ws = list(self.r.get(name, []))
        if name in self.w:
            ws.append(self.w[name])
        return ws

    def did_rd(self, name, sem, cnt):
        self.r.setdefault(name, []).append((sem, cnt))

    def did_wr(self, name, sem, cnt):
        self.w[name] = (sem, cnt)
        self.r[name] = []


def merge_waits(*wls):
    best = {}
    for wl in wls:
        for s, v in wl:
            if v > best.get(s, -1):
                best[s] = v
    return list(best.items())


def build_program():
    nc = bass.Bass("TRN2", target_bir_lowering=False, debug=False)
    p = Prog()
    res = Res()

    x_in = nc.dram_tensor("x_in", [1024, 2048], F32, kind="ExternalInput").ap()
    wq_d = nc.dram_tensor("wq", [2048, 2048], F32, kind="ExternalInput").ap()
    wk_d = nc.dram_tensor("wk", [2048, 2048], F32, kind="ExternalInput").ap()
    wv_d = nc.dram_tensor("wv", [2048, 2048], F32, kind="ExternalInput").ap()
    wo_d = nc.dram_tensor("wo", [2048, 2048], F32, kind="ExternalInput").ap()
    wqkv_d = nc.dram_tensor("wqkv", [2048, 6144], F32, kind="ExternalInput").ap()
    wout_d = nc.dram_tensor("wout", [2048, 2048], F32, kind="ExternalInput").ap()
    bqkvT_d = nc.dram_tensor("bqkvT", [128, 48], F32, kind="ExternalInput").ap()
    bout_d = nc.dram_tensor("bout_bc", [128, 2048], BF16, kind="ExternalInput").ap()
    wg_sh = nc.dram_tensor("wg_sh", [1024, 2048], F32, kind="ExternalInput").ap()
    wu_sh = nc.dram_tensor("wu_sh", [1024, 2048], F32, kind="ExternalInput").ap()
    wd_sh = nc.dram_tensor("wd_sh", [2048, 1024], F32, kind="ExternalInput").ap()
    lnw_d = nc.dram_tensor("lnw_bc", [128, 8192], F32, kind="ExternalInput").ap()
    w1_sh = nc.dram_tensor("w1_sh", [768, 2048], BF16, kind="ExternalInput").ap()
    w2_sh = nc.dram_tensor("w2_sh", [768, 2048], BF16, kind="ExternalInput").ap()
    w3_sh = nc.dram_tensor("w3_sh", [256, 5632], BF16, kind="ExternalInput").ap()
    mask_d = nc.dram_tensor("mask_in", [8, 128, 2048], BF16, kind="ExternalInput").ap()
    ident_d = nc.dram_tensor("ident_in", [128, 128], BF16, kind="ExternalInput").ap()
    nf4c_d = nc.dram_tensor("nf4c_in", [128, 32], F32, kind="ExternalInput").ap()
    out_d = nc.dram_tensor("out", [1024, 2048], F32, kind="ExternalOutput").ap()

    def idram(name, shape, dtype=BF16):
        return nc.dram_tensor(name, shape, dtype).ap()

    ar_in = idram("ar_in", [128, 8], F32)
    ar_out = idram("ar_out", [128, 8], F32)
    s_rt = idram("s_rt", [3, 768, 32], F32)
    r_rt = idram("r_rt", [3, 768, 32], F32)
    d_rt = idram("d_rt", [3, 768, 32], BF16)
    def sdram(name, shape, dtype=BF16):
        return nc.dram_tensor(name, shape, dtype, addr_space="Shared").ap()

    wgT_in = idram("wgT_in", [2048, 1024]); wgT_ag = sdram("wgT_ag", [16384, 1024])
    wuT_in = idram("wuT_in", [2048, 1024]); wuT_ag = sdram("wuT_ag", [16384, 1024])
    wdT_in = idram("wdT_in", [1024, 2048]); wdT_ag = sdram("wdT_ag", [8192, 2048])
    w1T_in = idram("w1T_in", [2048, 768]); w1T_ag = sdram("w1T_ag", [16384, 768])
    w2T_in = idram("w2T_in", [2048, 768]); w2T_ag = sdram("w2T_ag", [16384, 768])
    w3T_in = idram("w3T_in", [5632, 256]); w3T_ag = sdram("w3T_ag", [45056, 256])
    kT_in = [idram("kT0_in", [2048, 1024]), idram("kT1_in", [2048, 1024])]
    kT_ag = [idram("kT0_ag", [4096, 1024]), idram("kT1_ag", [4096, 1024])]
    vT_in = [idram("vT0_in", [2048, 1024]), idram("vT1_in", [2048, 1024])]
    vT_ag = [idram("vT0_ag", [4096, 1024]), idram("vT1_ag", [4096, 1024])]
    qT_dr = [idram("qT0_dr", [2048, 1024]), idram("qT1_dr", [2048, 1024])]

    from contextlib import ExitStack
    ctx = ExitStack()
    E = ctx.enter_context
    x_sb = E(nc.sbuf_tensor("x_sb", [128, 16384], F32))
    AA = E(nc.sbuf_tensor("AA", [128, AA_COLS], BF16))
    WB = E(nc.sbuf_tensor("WB", [128, WB_COLS], BF16))
    SM = E(nc.sbuf_tensor("SM", [128, 2048], F32))
    FS = E(nc.sbuf_tensor("FS", [128, 2048], F32))
    IDT = E(nc.sbuf_tensor("IDT", [128, 128], BF16))
    ps_big = E(nc.psum_tensor("ps_big", [128, 2048], F32))
    ps_a = E(nc.psum_tensor("ps_a", [128, 1024], F32))
    ps_b = E(nc.psum_tensor("ps_b", [128, 1024], F32))
    sems = {}
    for s in ("cc", "pe", "ve", "se"):
        sems[s] = E(nc.semaphore(s))

    def getsem(name):
        if name not in sems:
            sems[name] = E(nc.semaphore(name))
        return sems[name]

    ENG_SEM = {"tensor": "pe", "vector": "ve", "scalar": "se",
               "gpsimd": "sw", "sync": "hw"}
    PAIRS = [[0, 1], [2, 3], [4, 5], [6, 7]]
    ALL8 = [list(range(8))]

    def aanames(c0, n):
        return tuple(f"aa{k}" for k in range(c0 // 4096, (c0 + n - 1) // 4096 + 1))

    def wbnames(c0, n):
        return tuple(f"wb{k}" for k in range(c0 // 1024, (c0 + n - 1) // 1024 + 1))

    def dma(engname, out_ap, in_ap, reads=(), writes=(), extra=()):
        assert writes, "dma needs a write-resource for its stream semaphore"
        semname = ("dg_" if engname == "gpsimd" else "d_") + writes[0]
        getsem(semname)
        waits = merge_waits(*([res.rd(r) for r in reads]
                              + [res.wr(w) for w in writes] + [list(extra)]))
        cnt = p.emit(engname,
                     (lambda o, i: lambda e: e.dma_start(out=o, in_=i))(out_ap, in_ap),
                     waits=waits, inc=semname, n=16)
        for r in reads:
            res.did_rd(r, semname, cnt)
        for w in writes:
            res.did_wr(w, semname, cnt)
        return cnt

    def op(engname, fn, reads=(), writes=(), extra=(), sem=None):
        semname = sem or ENG_SEM[engname]
        getsem(semname)
        waits = merge_waits(*([res.rd(r) for r in reads]
                              + [res.wr(w) for w in writes] + [list(extra)]))
        cnt = p.emit(engname, fn, waits=waits, inc=semname, n=1)
        for r in reads:
            res.did_rd(r, semname, cnt)
        for w in writes:
            res.did_wr(w, semname, cnt)
        return cnt

    def mmf(o, l, r, st, sp):
        return lambda e: e.matmul(o, l, r, start=st, stop=sp)

    def mm_group(items, reads=(), writes=(), extra=()):
        waits = merge_waits(*([res.rd(r) for r in reads]
                              + [res.wr(w) for w in writes] + [list(extra)]))
        for it in items[:-1]:
            p.emit("tensor", it, waits=waits)
            waits = ()
        cnt = p.emit("tensor", items[-1], waits=waits, inc="pe", n=1)
        for r in reads:
            res.did_rd(r, "pe", cnt)
        for w in writes:
            res.did_wr(w, "pe", cnt)
        return cnt

    def aav(c0, n):
        return AA[:, c0:c0 + n]

    def wbv(c0, n):
        return WB[:, c0:c0 + n]

    def smc(col, n=1):
        return SM[:, col:col + n]

    def ag(in_ap, out_ap, rname, wname, groups):
        op("gpsimd", (lambda i_, o_, g_: lambda e: e.collective_compute(
            "AllGather", ALU.bypass, replica_groups=g_,
            ins=[i_.opt()], outs=[o_.opt()]))(in_ap, out_ap, groups),
            reads=(rname,), writes=(wname,), sem="cc")

    dma("sync", x_sb[:].rearrange("p (m f) -> p m f", f=2048),
        x_in.rearrange("(m p) f -> p m f", p=128), writes=("x",))
    dma("sync", IDT[:], ident_d, writes=("idt",))
    dma("sync", SM[:, SM_NF4M:SM_NF4M + 32], nf4c_d, writes=("st",))
    dma("sync", SM[:, SM_BQ:SM_BQ + 48], bqkvT_d, writes=("st",))
    op("vector", lambda e: e.memset(smc(SM_ONE), 1.0), writes=("st",))
    op("vector", lambda e: e.memset(SM[0:1, SM_ROW:SM_ROW + 128], 1.0),
       writes=("st",))

    SCR = 36864

    def emit_ln(layer):
        mean_c = SM_MEAN0 if layer == 0 else SM_MEAN1
        rs_c = SM_RS0 if layer == 0 else SM_RS1
        for m in range(8):
            xt = x_sb[:, 2048 * m:2048 * (m + 1)]
            op("vector", (lambda xt_, c: lambda e: e.reduce_sum(
                smc(c), xt_, axis=AX.X))(xt, mean_c + m),
                reads=("x",), writes=("st",))
            op("vector", (lambda c: lambda e: e.tensor_scalar_mul(
                smc(c), smc(c), 1.0 / 2048.0))(mean_c + m), writes=("st",))
            op("scalar", (lambda xt_: lambda e: e.activation(
                aav(SCR, 2048), xt_, AF.Square, accum_out=smc(SM_T)))(xt),
                reads=("x",), writes=aanames(SCR, 2048) + ("st",))
            op("vector", (lambda c: lambda e: e.tensor_tensor(
                smc(SM_T + 1), smc(c), smc(c), ALU.mult))(mean_c + m),
                writes=("st",))
            op("vector", lambda e: e.tensor_scalar(
                smc(SM_T), smc(SM_T), 1.0 / 2048.0, 1e-5, ALU.mult, ALU.add),
                writes=("st",))
            op("vector", lambda e: e.tensor_sub(
                smc(SM_T), smc(SM_T), smc(SM_T + 1)), writes=("st",))
            op("scalar", lambda e: e.activation(smc(SM_T), smc(SM_T), AF.Ln),
               reads=("st",), writes=("st",))
            op("scalar", (lambda c: lambda e: e.activation(
                smc(c), smc(SM_T), AF.Exp, scale=-0.5))(rs_c + m),
                reads=("st",), writes=("st",))
            op("vector", (lambda xt_, mc, rc: lambda e: e.scalar_tensor_tensor(
                out=aav(SCR, 2048), in0=xt_, scalar=smc(mc),
                in1=smc(rc).broadcast_to([128, 2048]), op0=ALU.subtract,
                op1=ALU.mult))(xt, mean_c + m, rs_c + m),
                reads=("x", "st"), writes=aanames(SCR, 2048))
            if layer == 0:
                op("vector", lambda e: e.reduce_max(
                    smc(SM_T), aav(SCR, 2048), axis=AX.X,
                    apply_absolute_value=True),
                    reads=aanames(SCR, 2048), writes=("st",))
                op("vector", lambda e: e.tensor_scalar_max(
                    smc(SM_T), smc(SM_T), 1e-5), writes=("st",))
                op("vector", (lambda mi: lambda e: e.tensor_scalar_mul(
                    smc(SM_ISX + mi), smc(SM_T), 1.0 / 127.0))(m),
                    writes=("st",))
                op("vector", lambda e: e.reciprocal(
                    smc(SM_T + 1), smc(SM_T)), writes=("st",))
                op("vector", (lambda mi: lambda e: e.scalar_tensor_tensor(
                    out=smc(SM_C1X + mi), in0=smc(SM_T + 1), scalar=127.0,
                    in1=smc(SM_RS0 + mi), op0=ALU.mult, op1=ALU.mult))(m),
                    writes=("st",))
            items = [mmf(ps_big[:, 128 * ft:128 * (ft + 1)],
                         aav(SCR + 128 * ft, 128), IDT[:], True, True)
                     for ft in range(16)]
            mm_group(items, reads=aanames(SCR, 2048) + ("idt",),
                     writes=("ps_big",))
            op("vector", (lambda mi: lambda e: e.tensor_copy(
                AA[:, 0:16384].rearrange("p (t q) -> p t q", q=1024)
                [:, :, 128 * mi:128 * (mi + 1)],
                ps_big[:].rearrange("p (t q) -> p t q", q=128)))(m),
                reads=("ps_big",), writes=aanames(0, 16384))

    emit_ln(0)

    for colz in range(8):
        op("vector", (lambda c: lambda e: e.memset(smc(SM_G + c), 0.0))(colz),
           writes=("st",))

    def tern_pass1(dram, gcol, tiles, width):
        for t in range(tiles):
            dma("sync", FS[:, 0:width], dram[128 * t:128 * (t + 1), :],
                writes=("fs",))
            op("vector", (lambda w: lambda e: e.reduce_sum(
                smc(SM_T + 2), FS[:, 0:w], axis=AX.X,
                apply_absolute_value=True))(width),
                reads=("fs",), writes=("st",))
            op("vector", (lambda c: lambda e: e.tensor_add(
                smc(SM_G + c), smc(SM_G + c), smc(SM_T + 2)))(gcol),
                writes=("st",))

    tern_pass1(wg_sh, 0, 8, 2048)
    tern_pass1(wu_sh, 1, 8, 2048)
    tern_pass1(wd_sh, 2, 16, 1024)

    NF4S = [(w1_sh, 0, 1024, 6, 2048, 3), (w2_sh, 1, 1216, 6, 2048, 4),
            (w3_sh, 2, 1408, 2, 5632, 5)]
    for dram, widx, scol, tiles, width, gcol in NF4S:
        nblk = width // 64
        for t in range(tiles):
            nsub = (width + 2047) // 2048
            for s in range(nsub):
                c0 = 2048 * s
                w = min(2048, width - c0)
                dma("sync", wbv(0, w), dram[128 * t:128 * (t + 1), c0:c0 + w],
                    writes=wbnames(0, w))
                op("vector", (lambda w_, sc: lambda e: e.reduce_max(
                    SM[:, sc:sc + w_ // 64].unsqueeze(2),
                    WB[:, 0:w_].rearrange("p (b i) -> p b i", i=64),
                    axis=AX.X, apply_absolute_value=True))(
                        w, scol + t * nblk + c0 // 64),
                    reads=wbnames(0, w), writes=("smsb",))
        op("vector", (lambda sc, n: lambda e: e.reduce_sum(
            smc(SM_T + 2), SM[:, sc:sc + n], axis=AX.X))(scol, tiles * nblk),
            reads=("smsb",), writes=("st",))
        op("vector", (lambda c: lambda e: e.tensor_add(
            smc(SM_G + c), smc(SM_G + c), smc(SM_T + 2)))(gcol),
            writes=("st",))

    dma("sync", ar_in, SM[:, SM_G:SM_G + 8], reads=("st",), writes=("ar_in",))
    op("gpsimd", lambda e: e.collective_compute(
        "AllReduce", ALU.add, replica_groups=ALL8,
        ins=[ar_in.opt()], outs=[ar_out.opt()]),
        reads=("ar_in",), writes=("ar_out",), sem="cc")
    dma("sync", SM[:, SM_G:SM_G + 8], ar_out, reads=("ar_out",), writes=("st",))
    op("tensor", lambda e: e.matmul(ps_a[0:1, 0:8], smc(SM_ONE),
                                    SM[:, SM_G:SM_G + 8], start=True, stop=True),
       reads=("st",), writes=("ps_a",))
    op("vector", lambda e: e.tensor_copy(SM[0:1, SM_T + 4:SM_T + 12],
                                         ps_a[0:1, 0:8]),
       reads=("ps_a",), writes=("st",))
    op("tensor", lambda e: e.matmul(ps_a[:, 0:8], SM[0:1, SM_ROW:SM_ROW + 128],
                                    SM[0:1, SM_T + 4:SM_T + 12],
                                    start=True, stop=True),
       reads=("st",), writes=("ps_a",))
    op("vector", lambda e: e.tensor_copy(SM[:, SM_G:SM_G + 8], ps_a[:, 0:8]),
       reads=("ps_a",), writes=("st",))
    NW = float(8192 * 2048)
    for j in range(3):
        op("vector", (lambda jj: lambda e: e.tensor_scalar(
            smc(SM_THR + 3 + jj), smc(SM_G + jj), 1.0 / NW, 1e-5,
            ALU.mult, ALU.max))(j), writes=("st",))
        op("vector", (lambda jj: lambda e: e.tensor_scalar_mul(
            smc(SM_THR + jj), smc(SM_THR + 3 + jj), 0.5))(j), writes=("st",))
        op("vector", (lambda jj: lambda e: e.tensor_scalar_mul(
            smc(SM_NTHR + jj), smc(SM_THR + jj), -1.0))(j), writes=("st",))
        op("vector", (lambda jj: lambda e: e.tensor_scalar_mul(
            smc(SM_THR + 6 + jj), smc(SM_G + 3 + jj), 1.0 / NBLK_REAL))(j),
            writes=("st",))
        op("scalar", (lambda jj: lambda e: e.copy(
            wbv(11264 + jj, 1), smc(SM_THR + 6 + jj)))(j),
            reads=("st",), writes=("wbm",))
        op("scalar", (lambda jj: lambda e: e.copy(
            smc(SM_NTHR + 3 + jj), wbv(11264 + jj, 1)))(j),
            reads=("wbm",), writes=("st",))

    def emit_proj(layer):
        srcs = {"q": (wq_d, 0), "k": (wk_d, 0), "v": (wv_d, 0)} if layer == 0 \
            else {"q": (wqkv_d, 0), "k": (wqkv_d, 2048), "v": (wqkv_d, 4096)}
        dsts = {"k": kT_in[layer], "v": vT_in[layer], "q": qT_dr[layer]}
        for which in ("k", "v", "q"):
            wsrc, coff = srcs[which]
            for t in range(16):
                slot = 18432 + 2048 * (t % 2)
                dma("gpsimd",
                    aav(slot, 2048).rearrange("p (ft c) -> p ft c", c=128),
                    wsrc[:, coff + 128 * t:coff + 128 * (t + 1)]
                    .rearrange("(ft p) c -> p ft c", p=128),
                    writes=aanames(slot, 2048))
                items = []
                for n2 in range(2):
                    for ft in range(16):
                        items.append(mmf(
                            ps_a[:, 512 * n2:512 * (n2 + 1)],
                            aav(slot + 128 * ft, 128),
                            aav(1024 * ft + 512 * n2, 512),
                            ft == 0, ft == 15))
                mm_group(items, reads=aanames(slot, 2048) + aanames(0, 16384),
                         writes=("ps_a",))
                stg = 22528 + 1024 * (t % 2)
                if layer == 1:
                    boff = {"q": 0, "k": 16, "v": 32}[which] + t
                    op("scalar", (lambda st_, bo: lambda e: e.activation(
                        aav(st_, 1024), ps_a[:], AF.Identity,
                        bias=smc(SM_BQ + bo)))(stg, boff),
                        reads=("ps_a", "st"), writes=aanames(stg, 1024))
                else:
                    op("scalar", (lambda st_: lambda e: e.copy(
                        aav(st_, 1024), ps_a[:]))(stg),
                        reads=("ps_a",), writes=aanames(stg, 1024))
                dma("sync", dsts[which][128 * t:128 * (t + 1), :],
                    aav(stg, 1024), reads=aanames(stg, 1024),
                    writes=(f"{which}T{layer}_in",))
                yield
            if which in ("k", "v"):
                agi = kT_in[layer] if which == "k" else vT_in[layer]
                ago = kT_ag[layer] if which == "k" else vT_ag[layer]
                ag(agi, ago, f"{which}T{layer}_in", f"{which}T{layer}_ag",
                   PAIRS)

    def transpose_tile_out(src_c0, width, stage_c0, reads):
        nt = width // 128
        items = [mmf(ps_big[:, 128 * j:128 * (j + 1)],
                     wbv(src_c0 + 128 * j, 128), IDT[:], True, True)
                 for j in range(nt)]
        mm_group(items, reads=tuple(reads) + ("idt",), writes=("ps_big",))
        op("vector", (lambda w: lambda e: e.tensor_copy(
            wbv(stage_c0, w), ps_big[:, 0:w]))(width),
            reads=("ps_big",), writes=wbnames(stage_c0, width))

    def tern_pass2(dram, gidx, tiles, width, dst, dstname):
        for t in range(tiles):
            dma("sync", FS[:, 0:width], dram[128 * t:128 * (t + 1), :],
                writes=("fs",))
            op("vector", (lambda w, g: lambda e: e.tensor_single_scalar(
                wbv(0, w), FS[:, 0:w], smc(SM_THR + g), ALU.is_gt))(width, gidx),
                reads=("fs", "st"), writes=wbnames(0, width))
            op("vector", (lambda w, g: lambda e: e.tensor_scalar(
                wbv(2048, w), FS[:, 0:w], smc(SM_NTHR + g), -1.0,
                ALU.is_gt, ALU.add))(width, gidx),
                reads=("fs", "st"), writes=wbnames(2048, width))
            op("vector", (lambda w: lambda e: e.tensor_add(
                wbv(4096, w), wbv(0, w), wbv(2048, w)))(width),
                reads=wbnames(0, width) + wbnames(2048, width),
                writes=wbnames(4096, width))
            transpose_tile_out(4096, width, 8192, wbnames(4096, width))
            dma("sync",
                dst[:, 128 * t:128 * (t + 1)]
                .rearrange("(f p) c -> p f c", p=128),
                wbv(8192, width).rearrange("p (f c) -> p f c", c=128),
                reads=wbnames(8192, width), writes=(dstname,))
            if t % 2 == 1:
                yield

    def nf4_pass2(dram, widx, tiles, width, scol, dst, dstname):
        nblk = width // 64
        nsc = tiles * nblk
        nsb = nsc * 128 // 256
        P = nsb
        flat = nsc * 128
        rowc = nblk if width == 5632 else 32
        dma("sync", SM[0:P, SM_SB1:SM_SB1 + 256],
            s_rt[widx].flatten()[0:flat].rearrange("(s c) -> s c", c=256),
            reads=("s_rt",), writes=("smsb1",))
        op("vector", (lambda P_, w_: lambda e: e.tensor_single_scalar(
            SM[0:P_, SM_SB2:SM_SB2 + 256], SM[0:P_, SM_SB1:SM_SB1 + 256],
            SM[0:P_, SM_THR + 6 + w_:SM_THR + 7 + w_], ALU.subtract))(P, widx),
            reads=("smsb1", "st"), writes=("smsb2",))
        op("vector", (lambda P_: lambda e: e.reduce_max(
            SM[0:P_, SM_T + 4:SM_T + 5], SM[0:P_, SM_SB2:SM_SB2 + 256],
            axis=AX.X, apply_absolute_value=True))(P),
            reads=("smsb2",), writes=("st",))
        op("vector", (lambda P_: lambda e: e.reciprocal(
            SM[0:P_, SM_T + 5:SM_T + 6], SM[0:P_, SM_T + 4:SM_T + 5]))(P),
            writes=("st",))
        op("vector", (lambda P_: lambda e: e.tensor_scalar_mul(
            SM[0:P_, SM_T + 5:SM_T + 6], SM[0:P_, SM_T + 5:SM_T + 6],
            128.0))(P), writes=("st",))
        op("vector", (lambda P_: lambda e: e.tensor_scalar(
            SM[0:P_, SM_SB2:SM_SB2 + 256], SM[0:P_, SM_SB2:SM_SB2 + 256],
            SM[0:P_, SM_T + 5:SM_T + 6], MAGIC, ALU.mult, ALU.add))(P),
            reads=("st",), writes=("smsb2",))
        op("vector", (lambda P_: lambda e: e.tensor_scalar(
            SM[0:P_, SM_SB2:SM_SB2 + 256], SM[0:P_, SM_SB2:SM_SB2 + 256],
            -MAGIC, -128.0, ALU.add, ALU.max))(P), writes=("smsb2",))
        op("vector", (lambda P_: lambda e: e.tensor_scalar_min(
            SM[0:P_, SM_SB2:SM_SB2 + 256], SM[0:P_, SM_SB2:SM_SB2 + 256],
            127.0))(P), writes=("smsb2",))
        op("vector", (lambda P_: lambda e: e.tensor_scalar_mul(
            SM[0:P_, SM_T + 4:SM_T + 5], SM[0:P_, SM_T + 4:SM_T + 5],
            1.0 / 128.0))(P), writes=("st",))
        op("vector", (lambda P_: lambda e: e.tensor_single_scalar(
            WB[0:P_, 8192:8448], SM[0:P_, SM_SB2:SM_SB2 + 256],
            SM[0:P_, SM_T + 4:SM_T + 5], ALU.mult))(P),
            reads=("smsb2", "st"), writes=wbnames(8192, 256))
        op("vector", (lambda P_, w_: lambda e: e.tensor_single_scalar(
            WB[0:P_, 8192:8448], WB[0:P_, 8192:8448],
            SM[0:P_, SM_NTHR + 3 + w_:SM_NTHR + 4 + w_], ALU.add))(P, widx),
            reads=("st",), writes=wbnames(8192, 256))
        op("vector", (lambda P_: lambda e: e.tensor_scalar_max(
            SM[0:P_, SM_SB1:SM_SB1 + 256], SM[0:P_, SM_SB1:SM_SB1 + 256],
            1e-30))(P), writes=("smsb1",))
        op("vector", (lambda P_: lambda e: e.reciprocal(
            SM[0:P_, SM_SB2:SM_SB2 + 256], SM[0:P_, SM_SB1:SM_SB1 + 256]))(P),
            reads=("smsb1",), writes=("smsb2",))
        dma("sync", d_rt[widx].flatten()[0:flat].rearrange(
            "(s c) -> s c", c=256)[0:P, :], WB[0:P, 8192:8448],
            reads=wbnames(8192, 256), writes=("d_rt",))
        dma("sync", r_rt[widx].flatten()[0:flat].rearrange(
            "(s c) -> s c", c=256)[0:P, :], SM[0:P, SM_SB2:SM_SB2 + 256],
            reads=("smsb2",), writes=("r_rt",))
        dma("sync", WB[:, 8448:8448 + nsc].rearrange(
            "p (t c) -> p t c", c=rowc),
            d_rt[widx].flatten()[0:flat].rearrange(
                "(t p c) -> p t c", p=128, c=rowc),
            reads=("d_rt",), writes=wbnames(8448, nsc))
        dma("sync", FS[:, 0:nsc].rearrange("p (t c) -> p t c", c=rowc),
            r_rt[widx].flatten()[0:flat].rearrange(
                "(t p c) -> p t c", p=128, c=rowc),
            reads=("r_rt",), writes=("fs",))
        yield
        for t in range(tiles):
            nsub = (width + 2047) // 2048
            for s in range(nsub):
                c0 = 2048 * s
                w = min(2048, width - c0)
                bc = t * nblk + c0 // 64
                dma("sync", wbv(0, w), dram[128 * t:128 * (t + 1), c0:c0 + w],
                    writes=wbnames(0, w))
                op("vector", (lambda w_, b_: lambda e: e.tensor_tensor(
                    wbv(2048, w_).rearrange("p (b i) -> p b i", i=64),
                    wbv(0, w_).rearrange("p (b i) -> p b i", i=64),
                    FS[:, b_:b_ + w_ // 64].unsqueeze(2)
                    .broadcast_to([128, w_ // 64, 64]), ALU.mult))(w, bc),
                    reads=wbnames(0, w) + ("fs",), writes=wbnames(2048, w))
                op("vector", (lambda w_: lambda e: e.tensor_scalar(
                    wbv(4096, w_), wbv(2048, w_), smc(SM_NF4M),
                    float(NF4_DELTA[0]), ALU.is_gt, ALU.mult))(w),
                    reads=wbnames(2048, w) + ("st",), writes=wbnames(4096, w))
                op("vector", (lambda w_: lambda e: e.tensor_scalar_add(
                    wbv(4096, w_), wbv(4096, w_), NF4_BASE))(w),
                    writes=wbnames(4096, w))
                for kk in range(1, 15):
                    op("vector", (lambda w_, k_: lambda e: e.tensor_scalar(
                        wbv(6144, w_), wbv(2048, w_), smc(SM_NF4M + k_),
                        float(NF4_DELTA[k_]), ALU.is_gt, ALU.mult))(w, kk),
                        reads=wbnames(2048, w) + ("st",),
                        writes=wbnames(6144, w))
                    op("vector", (lambda w_: lambda e: e.tensor_add(
                        wbv(4096, w_), wbv(4096, w_), wbv(6144, w_)))(w),
                        reads=wbnames(6144, w), writes=wbnames(4096, w))
                op("vector", (lambda w_, b_: lambda e: e.tensor_tensor(
                    wbv(0, w_).rearrange("p (b i) -> p b i", i=64),
                    wbv(4096, w_).rearrange("p (b i) -> p b i", i=64),
                    WB[:, 8448 + b_:8448 + b_ + w_ // 64].unsqueeze(2)
                    .broadcast_to([128, w_ // 64, 64]), ALU.mult))(w, bc),
                    reads=wbnames(4096, w) + wbnames(8448, nsc),
                    writes=wbnames(0, w))
                transpose_tile_out(0, w, 9472, wbnames(0, w))
                dma("sync",
                    dst[c0:c0 + w, 128 * t:128 * (t + 1)]
                    .rearrange("(f p) c -> p f c", p=128),
                    wbv(9472, w).rearrange("p (f c) -> p f c", c=128),
                    reads=wbnames(9472, w), writes=(dstname,))
                yield

    def side_steps():
        yield from tern_pass2(wg_sh, 0, 8, 2048, wgT_in, "wgT_in")
        ag(wgT_in, wgT_ag, "wgT_in", "wgT_ag", ALL8)
        yield from tern_pass2(wu_sh, 1, 8, 2048, wuT_in, "wuT_in")
        ag(wuT_in, wuT_ag, "wuT_in", "wuT_ag", ALL8)
        yield from tern_pass2(wd_sh, 2, 16, 1024, wdT_in, "wdT_in")
        ag(wdT_in, wdT_ag, "wdT_in", "wdT_ag", ALL8)
        for widx, scol, tiles, width in ((0, 1024, 6, 2048),
                                         (1, 1216, 6, 2048),
                                         (2, 1408, 2, 5632)):
            nblk = width // 64
            nsc = tiles * nblk
            flat = nsc * 128
            rowc = nblk if width == 5632 else 32
            dma("sync",
                s_rt[widx].flatten()[0:flat].rearrange(
                    "(t p c) -> p t c", p=128, c=rowc),
                SM[:, scol:scol + nsc].rearrange("p (t c) -> p t c", c=rowc),
                reads=("smsb",), writes=("s_rt",))
        yield
        yield from nf4_pass2(w1_sh, 0, 6, 2048, 1024, w1T_in, "w1T_in")
        yield from nf4_pass2(w2_sh, 1, 6, 2048, 1216, w2T_in, "w2T_in")
        yield from nf4_pass2(w3_sh, 2, 2, 5632, 1408, w3T_in, "w3T_in")

    _sg = side_steps()
    for _ in emit_proj(0):
        next(_sg, None)
    for _ in _sg:
        pass

    def emit_ffn0():
        for m in range(8):
            op("vector", (lambda mi: lambda e: e.tensor_tensor(
                smc(SM_SCG + mi), smc(SM_ISX + mi), smc(SM_THR + 3),
                ALU.mult))(m), writes=("st",))
            op("vector", (lambda mi: lambda e: e.tensor_tensor(
                smc(SM_SCU + mi), smc(SM_ISX + mi), smc(SM_THR + 4),
                ALU.mult))(m), writes=("st",))
        for th in range(2):
            ms = [4 * th + i for i in range(4)]
            hslot = [8192, 16384, 24576, 32768]
            for i, m in enumerate(ms):
                xt = x_sb[:, 2048 * m:2048 * (m + 1)]
                for cchunk in range(4):
                    c0 = 512 * cchunk
                    op("vector", (lambda xt_, mi, c_: lambda e:
                        e.scalar_tensor_tensor(
                            out=SM[:, SM_QT:SM_QT + 512],
                            in0=xt_[:, c_:c_ + 512],
                            scalar=smc(SM_MEAN0 + mi),
                            in1=smc(SM_C1X + mi).broadcast_to([128, 512]),
                            op0=ALU.subtract, op1=ALU.mult))(xt, m, c0),
                        reads=("x", "st"), writes=("smqt",))
                    op("vector", lambda e: e.tensor_scalar(
                        SM[:, SM_QT:SM_QT + 512], SM[:, SM_QT:SM_QT + 512],
                        MAGIC, -MAGIC, ALU.add, ALU.add), writes=("smqt",))
                    op("vector", (lambda c_: lambda e: e.tensor_copy(
                        aav(SCR + c_, 512), SM[:, SM_QT:SM_QT + 512]))(c0),
                        reads=("smqt",), writes=aanames(SCR, 2048))
                items = [mmf(ps_big[:, 128 * ft:128 * (ft + 1)],
                             aav(SCR + 128 * ft, 128), IDT[:], True, True)
                         for ft in range(16)]
                mm_group(items, reads=aanames(SCR, 2048) + ("idt",),
                         writes=("ps_big",))
                op("vector", (lambda ii: lambda e: e.tensor_copy(
                    AA[:, 0:8192].rearrange("p (t q) -> p t q", q=512)
                    [:, :, 128 * ii:128 * (ii + 1)],
                    ps_big[:].rearrange("p (t q) -> p t q", q=128)))(i),
                    reads=("ps_big",), writes=aanames(0, 8192))
            for m in ms:
                op("vector", (lambda mi: lambda e: e.memset(
                    smc(SM_VAR + mi), 0.0))(m), writes=("st",))
            for i in range(4):
                op("vector", (lambda ii: lambda e: e.memset(
                    smc(SM_T + 12 + ii), 0.0))(i), writes=("st",))
            for n in range(16):
                r, off = (512 * n) // 1024, (512 * n) % 1024
                dma("sync", WB[:, 0:8192].rearrange("p (f c) -> p f c", c=512),
                    wgT_ag[2048 * r:2048 * (r + 1), off:off + 512]
                    .rearrange("(f p) c -> p f c", p=128),
                    writes=wbnames(0, 8192), extra=res.rd("wgT_ag"))
                dma("sync", AA[:, 40960:49152].rearrange(
                    "p (f c) -> p f c", c=512),
                    wuT_ag[2048 * r:2048 * (r + 1), off:off + 512]
                    .rearrange("(f p) c -> p f c", p=128),
                    writes=aanames(40960, 8192), extra=res.rd("wuT_ag"))
                dma("gpsimd", wbv(8960, 512),
                    lnw_d[:, 512 * n:512 * (n + 1)],
                    writes=wbnames(8960, 512))
                for i, m in enumerate(ms):
                    items = [mmf(ps_a[:, 0:512], aav(512 * ft + 128 * i, 128),
                                 wbv(512 * ft, 512), ft == 0, ft == 15)
                             for ft in range(16)]
                    mm_group(items, reads=aanames(0, 8192) + wbnames(0, 8192),
                             writes=("ps_a",))
                    items = [mmf(ps_b[:, 0:512], aav(512 * ft + 128 * i, 128),
                                 aav(40960 + 512 * ft, 512), ft == 0, ft == 15)
                             for ft in range(16)]
                    mm_group(items,
                             reads=aanames(0, 8192) + aanames(40960, 8192),
                             writes=("ps_b",))
                    op("scalar", (lambda mi: lambda e: e.activation(
                        wbv(8192, 512), ps_a[:, 0:512], AF.Silu,
                        scale=smc(SM_SCG + mi)))(m),
                        reads=("ps_a", "st"), writes=wbnames(8192, 512))
                    hc = hslot[i] + 512 * n
                    op("vector", (lambda mi, hc_: lambda e:
                        e.scalar_tensor_tensor(
                            out=aav(hc_, 512), in0=ps_b[:, 0:512],
                            scalar=smc(SM_SCU + mi), in1=wbv(8192, 512),
                            op0=ALU.mult, op1=ALU.mult))(m, hc),
                        reads=("ps_b", "st") + wbnames(8192, 512),
                        writes=aanames(hc, 512))
                    op("scalar", (lambda hc_: lambda e: e.activation(
                        wbv(9984, 512), aav(hc_, 512), AF.Square,
                        accum_out=smc(SM_T + 2)))(hc),
                        reads=aanames(hc, 512),
                        writes=wbnames(9984, 512) + ("st",))
                    op("vector", (lambda mi: lambda e: e.tensor_add(
                        smc(SM_VAR + mi), smc(SM_VAR + mi),
                        smc(SM_T + 2)))(m), writes=("st",))
                    op("vector", (lambda hc_: lambda e: e.tensor_tensor(
                        SM[:, SM_QT:SM_QT + 512], aav(hc_, 512),
                        wbv(8960, 512), ALU.mult))(hc),
                        reads=aanames(hc, 512) + wbnames(8960, 512),
                        writes=("smqt",))
                    op("vector", lambda e: e.reduce_max(
                        smc(SM_T + 2), SM[:, SM_QT:SM_QT + 512], axis=AX.X,
                        apply_absolute_value=True),
                        reads=("smqt",), writes=("st",))
                    op("vector", (lambda ii: lambda e: e.tensor_max(
                        smc(SM_T + 12 + ii), smc(SM_T + 12 + ii),
                        smc(SM_T + 2)))(i), writes=("st",))
            for i, m in enumerate(ms):
                op("vector", (lambda mi: lambda e: e.tensor_scalar(
                    smc(SM_T + 2), smc(SM_VAR + mi), 1.0 / 8192.0, 1e-6,
                    ALU.mult, ALU.add))(m), writes=("st",))
                op("scalar", lambda e: e.activation(
                    smc(SM_T + 2), smc(SM_T + 2), AF.Ln),
                    reads=("st",), writes=("st",))
                op("scalar", lambda e: e.activation(
                    smc(SM_T + 3), smc(SM_T + 2), AF.Exp, scale=-0.5),
                    reads=("st",), writes=("st",))
                op("vector", (lambda ii: lambda e: e.tensor_tensor(
                    smc(SM_T + 2), smc(SM_T + 3), smc(SM_T + 12 + ii),
                    ALU.mult))(i), writes=("st",))
                op("vector", lambda e: e.tensor_scalar_max(
                    smc(SM_T + 2), smc(SM_T + 2), 1e-5), writes=("st",))
                op("vector", (lambda mi: lambda e: e.tensor_scalar_mul(
                    smc(SM_C3 + mi), smc(SM_T + 2), 1.0 / 127.0))(m),
                    writes=("st",))
                op("vector", lambda e: e.reciprocal(
                    smc(SM_T + 2), smc(SM_T + 2)), writes=("st",))
                op("vector", lambda e: e.scalar_tensor_tensor(
                    out=smc(SM_T + 2), in0=smc(SM_T + 2), scalar=127.0,
                    in1=smc(SM_T + 3), op0=ALU.mult, op1=ALU.mult),
                    writes=("st",))
                op("vector", (lambda mi: lambda e: e.tensor_tensor(
                    smc(SM_C3 + mi), smc(SM_C3 + mi), smc(SM_THR + 5),
                    ALU.mult))(m), writes=("st",))
                for n in range(16):
                    hc = hslot[i] + 512 * n
                    dma("gpsimd", wbv(8960, 512),
                        lnw_d[:, 512 * n:512 * (n + 1)],
                        writes=wbnames(8960, 512))
                    op("vector", (lambda hc_: lambda e: e.tensor_tensor(
                        SM[:, SM_QT:SM_QT + 512], aav(hc_, 512),
                        wbv(8960, 512), ALU.mult))(hc),
                        reads=aanames(hc, 512) + wbnames(8960, 512),
                        writes=("smqt",))
                    op("vector", lambda e: e.tensor_scalar(
                        SM[:, SM_QT:SM_QT + 512], SM[:, SM_QT:SM_QT + 512],
                        smc(SM_T + 2), MAGIC, ALU.mult, ALU.add),
                        reads=("st",), writes=("smqt",))
                    op("vector", (lambda hc_: lambda e: e.tensor_scalar_add(
                        aav(hc_, 512), SM[:, SM_QT:SM_QT + 512], -MAGIC))(hc),
                        reads=("smqt",), writes=aanames(hc, 512))
                for batch in range(4):
                    c0 = hslot[i] + 2048 * batch
                    items = [mmf(ps_big[:, 128 * j:128 * (j + 1)],
                                 aav(c0 + 128 * j, 128), IDT[:], True, True)
                             for j in range(16)]
                    mm_group(items, reads=aanames(c0, 2048) + ("idt",),
                             writes=("ps_big",))
                    op("vector", (lambda c_: lambda e: e.tensor_copy(
                        aav(c_, 2048), ps_big[:]))(c0),
                        reads=("ps_big",), writes=aanames(c0, 2048))
            for dg in range(8):
                wdc = []
                for jj in range(8):
                    d = 8 * dg + jj
                    dst_c = (2048 * jj) if jj < 4 else (40960 + 2048 * (jj - 4))
                    wdc.append(dst_c)
                    dma("sync", aav(dst_c, 2048),
                        wdT_ag[128 * d:128 * (d + 1), :],
                        writes=aanames(dst_c, 2048),
                        extra=res.rd("wdT_ag"))
                for i, m in enumerate(ms):
                    items = []
                    for jj in range(8):
                        d = 8 * dg + jj
                        for n4 in range(4):
                            items.append(mmf(
                                ps_big[:, 512 * n4:512 * (n4 + 1)],
                                aav(hslot[i] + 128 * d, 128),
                                aav(wdc[jj] + 512 * n4, 512),
                                jj == 0, jj == 7))
                    rdn = aanames(hslot[i], 8192) + aanames(0, 8192) \
                        + aanames(40960, 8192)
                    mm_group(items, reads=rdn, writes=("ps_big",))
                    op("vector", (lambda mi: lambda e: e.scalar_tensor_tensor(
                        out=x_sb[:, 2048 * mi:2048 * (mi + 1)],
                        in0=ps_big[:], scalar=smc(SM_C3 + mi),
                        in1=x_sb[:, 2048 * mi:2048 * (mi + 1)],
                        op0=ALU.mult, op1=ALU.add))(m),
                        reads=("ps_big", "st", "x"), writes=("x",))

    emit_ffn0()
    ag(w1T_in, w1T_ag, "w1T_in", "w1T_ag", ALL8)
    ag(w2T_in, w2T_ag, "w2T_in", "w2T_ag", ALL8)
    ag(w3T_in, w3T_ag, "w3T_in", "w3T_ag", ALL8)

    def emit_attn(layer):
        use_mask = layer == 0
        wo_src = wo_d if layer == 0 else wout_d
        kag, vag, qdr = kT_ag[layer], vT_ag[layer], qT_dr[layer]
        kname, vname = f"kT{layer}_ag", f"vT{layer}_ag"
        qname = f"qT{layer}_in"
        tls = TLS if layer == 0 else [list(range(16))] * 8
        offs = [0]
        for i in range(8):
            offs.append(offs[-1] + 128 * len(tls[i]))
        if layer == 1:
            dma("sync", wbv(0, 2048), bout_d, writes=wbnames(0, 2048))
            for m in range(8):
                op("vector", (lambda mi: lambda e: e.tensor_add(
                    x_sb[:, 2048 * mi:2048 * (mi + 1)],
                    x_sb[:, 2048 * mi:2048 * (mi + 1)], wbv(0, 2048)))(m),
                    reads=wbnames(0, 2048) + ("x",), writes=("x",))
        if use_mask:
            for qt in range(8):
                Wm = 128 * len(tls[qt])
                dma("sync", aav(40960 + offs[qt], Wm), mask_d[qt][:, 0:Wm],
                    writes=aanames(40960 + offs[qt], Wm))

        def psb_names(qt):
            if layer == 1 or len(tls[qt]) > 8:
                return ("ps_big", "psbg0", "psbg1")
            return (f"psbg{qt % 2}",)

        def softmax_stage(qt):
            tl = tls[qt]
            nt = len(tl)
            W = 128 * nt
            par = qt % 2
            dense = nt > 8
            st0 = SM_AT + 4 * par
            if layer == 1:
                pc = 6144 + 2048 * par
                po = 0
            else:
                pc = 6144 if dense else 6144 + 1024 * par
                po = 0 if dense else 1024 * par
            psn = psb_names(qt)
            san = (f"sa{par}",)
            items = []
            for s0, t0, L in _segs(tl):
                items.append(mmf(ps_big[:, po + 128 * s0:po + 128 * (s0 + L)],
                                 wbv(10240 + 128 * qt, 128),
                                 wbv(128 * t0, 128 * L), True, True))
            mm_group(items, reads=wbnames(10240, 1024) + wbnames(0, 2048),
                     writes=psn)
            if use_mask:
                op("vector", (lambda W_, mo, po_: lambda e: e.tensor_add(
                    ps_big[:, po_:po_ + W_], ps_big[:, po_:po_ + W_],
                    aav(mo, W_)))(W, 40960 + offs[qt], po),
                    reads=aanames(40960 + offs[qt], W), writes=psn)
            op("vector", (lambda W_, c, po_: lambda e: e.reduce_max(
                smc(c), ps_big[:, po_:po_ + W_], axis=AX.X))(W, st0, po),
                reads=psn, writes=san)
            op("vector", (lambda c: lambda e: e.tensor_scalar_mul(
                smc(c + 1), smc(c), -C_ATTN))(st0), writes=san)
            op("scalar", (lambda W_, c, pc_, po_: lambda e: e.activation(
                wbv(pc_, W_), ps_big[:, po_:po_ + W_], AF.Exp,
                bias=smc(c + 1), scale=C_ATTN,
                accum_out=smc(c + 2)))(W, st0, pc, po),
                reads=psn + san, writes=wbnames(pc, W) + san)
            op("vector", (lambda c: lambda e: e.reciprocal(
                smc(c + 3), smc(c + 2)))(st0), writes=san)
            op("scalar", (lambda W_, c, pc_: lambda e: e.activation(
                wbv(pc_, W_), wbv(pc_, W_), AF.Identity,
                scale=smc(c + 3)))(W, st0, pc),
                reads=san, writes=wbnames(pc, W))

        def transpose_stage(qt):
            tl = tls[qt]
            nt = len(tl)
            par = qt % 2
            dense = nt > 8
            if layer == 1:
                pc = 6144 + 2048 * par
            else:
                pc = 6144 if dense else 6144 + 1024 * par
            if layer == 0:
                for w0 in range(0, nt, 8):
                    nw = min(8, nt - w0)
                    items = [mmf(ps_b[:, 128 * j:128 * (j + 1)],
                                 wbv(pc + 128 * (w0 + j), 128), IDT[:],
                                 True, True) for j in range(nw)]
                    mm_group(items, reads=wbnames(pc, 128 * nt) + ("idt",),
                             writes=("ps_b",))
                    dst = offs[qt] + 128 * w0
                    op("scalar", (lambda d_, n_: lambda e: e.copy(
                        AA[:, d_:d_ + 128 * n_],
                        ps_b[:, 0:128 * n_]))(dst, nw),
                        reads=("ps_b",), writes=aanames(dst, 128 * nw))
            else:
                items = [mmf(ps_a[:, 128 * j:128 * (j + 1)],
                             wbv(pc + 128 * j, 128), IDT[:], True, True)
                         for j in range(8)]
                mm_group(items, reads=wbnames(pc, 2048) + ("idt",),
                         writes=("ps_a",))
                items = [mmf(ps_b[:, 128 * j:128 * (j + 1)],
                             wbv(pc + 1024 + 128 * j, 128), IDT[:],
                             True, True) for j in range(8)]
                mm_group(items, reads=wbnames(pc, 2048) + ("idt",),
                         writes=("ps_b",))
                op("scalar", (lambda q_: lambda e: e.copy(
                    AA[:, 0:16384].rearrange("p (j q) -> p j q", q=1024)
                    [:, 0:8, 128 * q_:128 * (q_ + 1)],
                    ps_a[:].rearrange("p (j q) -> p j q", q=128)))(qt),
                    reads=("ps_a",), writes=aanames(0, 8192))
                op("scalar", (lambda q_: lambda e: e.copy(
                    AA[:, 0:16384].rearrange("p (j q) -> p j q", q=1024)
                    [:, 8:16, 128 * q_:128 * (q_ + 1)],
                    ps_b[:].rearrange("p (j q) -> p j q", q=128)))(qt),
                    reads=("ps_b",), writes=aanames(8192, 8192))

        for g in range(4):
            for hh in range(4):
                dma("gpsimd", aav(20480 + 2048 * hh, 2048),
                    wo_src[512 * g + 128 * hh:512 * g + 128 * (hh + 1), :],
                    writes=aanames(20480 + 2048 * hh, 2048))
            for hh in range(4):
                h = 4 * g + hh
                for half in range(2):
                    dma("sync", wbv(1024 * half, 1024),
                        kag[2048 * half + 128 * h:2048 * half + 128 * (h + 1), :],
                        writes=wbnames(1024 * half, 1024),
                        extra=res.rd(kname))
                    dma("sync", wbv(2048 + 1024 * half, 1024),
                        vag[2048 * half + 128 * h:2048 * half + 128 * (h + 1), :],
                        writes=wbnames(2048 + 1024 * half, 1024),
                        extra=res.rd(vname))
                dma("sync", wbv(10240, 1024), qdr[128 * h:128 * (h + 1), :],
                    writes=wbnames(10240, 1024), extra=res.rd(qname))
                items = [mmf(ps_a[:, 128 * j:128 * (j + 1)],
                             wbv(2048 + 128 * j, 128), IDT[:], True, True)
                         for j in range(8)]
                mm_group(items, reads=wbnames(2048, 2048) + ("idt",),
                         writes=("ps_a",))
                items = [mmf(ps_b[:, 128 * j:128 * (j + 1)],
                             wbv(2048 + 1024 + 128 * j, 128), IDT[:],
                             True, True) for j in range(8)]
                mm_group(items, reads=wbnames(2048, 2048) + ("idt",),
                         writes=("ps_b",))
                op("vector", lambda e: e.tensor_copy(wbv(4096, 1024),
                                                     ps_a[:]),
                   reads=("ps_a",), writes=wbnames(4096, 1024))
                op("vector", lambda e: e.tensor_copy(wbv(4096 + 1024, 1024),
                                                     ps_b[:]),
                   reads=("ps_b",), writes=wbnames(4096 + 1024, 1024))
                first_def = 1 if layer == 0 else 0
                for qt in range(8):
                    softmax_stage(qt)
                    if qt == 0 and first_def == 1:
                        transpose_stage(0)
                    elif qt > first_def:
                        transpose_stage(qt - 1)
                transpose_stage(7)
                if layer == 0:
                    for qt in range(8):
                        tl = tls[qt]
                        items = []
                        for s, t in enumerate(tl):
                            items.append(mmf(
                                ps_a[:, 128 * qt:128 * (qt + 1)],
                                wbv(4096 + 128 * t, 128),
                                AA[:, offs[qt] + 128 * s:
                                   offs[qt] + 128 * (s + 1)],
                                s == 0, s == len(tl) - 1))
                        mm_group(items,
                                 reads=aanames(offs[qt], 128 * len(tl))
                                 + wbnames(4096, 2048), writes=("ps_a",))
                else:
                    items = []
                    for j in range(16):
                        for n2 in range(2):
                            items.append(mmf(
                                ps_a[:, 512 * n2:512 * (n2 + 1)],
                                wbv(4096 + 128 * j, 128),
                                aav(1024 * j + 512 * n2, 512),
                                j == 0, j == 15))
                    mm_group(items,
                             reads=aanames(0, 16384) + wbnames(4096, 2048),
                             writes=("ps_a",))
                op("scalar", (lambda hh_: lambda e: e.copy(
                    aav(16384 + 1024 * hh_, 1024), ps_a[:]))(hh),
                    reads=("ps_a",), writes=aanames(16384 + 1024 * hh, 1024))
            for m in range(8):
                items = []
                for hh in range(4):
                    for n4 in range(4):
                        items.append(mmf(
                            ps_big[:, 512 * n4:512 * (n4 + 1)],
                            aav(16384 + 1024 * hh + 128 * m, 128),
                            aav(20480 + 2048 * hh + 512 * n4, 512),
                            hh == 0, hh == 3))
                mm_group(items, reads=aanames(16384, 4096) + aanames(20480, 8192),
                         writes=("ps_big", "psbg0", "psbg1"))
                op("vector", (lambda mi: lambda e: e.tensor_add(
                    x_sb[:, 2048 * mi:2048 * (mi + 1)],
                    x_sb[:, 2048 * mi:2048 * (mi + 1)], ps_big[:]))(m),
                    reads=("ps_big", "x"), writes=("x",))

    emit_attn(0)

    emit_ln(1)
    for _ in emit_proj(1):
        pass

    def emit_ffn1():
        for qd in range(4):
            dls = list(range(11 * qd, 11 * qd + 11))
            for idx, dl in enumerate(dls):
                r, off = (128 * dl) // 768, (128 * dl) % 768
                dma("sync", wbv(0, 2048).rearrange("p (f c) -> p f c", c=128),
                    w1T_ag[2048 * r:2048 * (r + 1), off:off + 128]
                    .rearrange("(f p) c -> p f c", p=128),
                    writes=wbnames(0, 2048), extra=res.rd("w1T_ag"))
                dma("sync", wbv(2048, 2048).rearrange("p (f c) -> p f c", c=128),
                    w2T_ag[2048 * r:2048 * (r + 1), off:off + 128]
                    .rearrange("(f p) c -> p f c", p=128),
                    writes=wbnames(2048, 2048), extra=res.rd("w2T_ag"))
                items = []
                for n2 in range(2):
                    for ft in range(16):
                        items.append(mmf(
                            ps_a[:, 512 * n2:512 * (n2 + 1)],
                            wbv(128 * ft, 128),
                            aav(1024 * ft + 512 * n2, 512),
                            ft == 0, ft == 15))
                mm_group(items, reads=wbnames(0, 2048) + aanames(0, 16384),
                         writes=("ps_a",))
                items = []
                for n2 in range(2):
                    for ft in range(16):
                        items.append(mmf(
                            ps_b[:, 512 * n2:512 * (n2 + 1)],
                            wbv(2048 + 128 * ft, 128),
                            aav(1024 * ft + 512 * n2, 512),
                            ft == 0, ft == 15))
                mm_group(items, reads=wbnames(2048, 2048) + aanames(0, 16384),
                         writes=("ps_b",))
                op("scalar", lambda e: e.activation(
                    wbv(4096, 1024), ps_a[:], AF.Silu),
                    reads=("ps_a",), writes=wbnames(4096, 1024))
                hc = 16384 + 1024 * idx
                op("vector", (lambda hc_: lambda e: e.tensor_tensor(
                    aav(hc_, 1024), wbv(4096, 1024), ps_b[:], ALU.mult))(hc),
                    reads=wbnames(4096, 1024) + ("ps_b",),
                    writes=aanames(hc, 1024))
            for idx, dl in enumerate(dls):
                wc = 27648 + 2048 * idx
                dma("sync", aav(wc, 2048).rearrange("p (r c) -> p r c", c=256),
                    w3T_ag.rearrange("(r d) c -> d r c", d=5632)
                    [128 * dl:128 * (dl + 1)],
                    writes=aanames(wc, 2048), extra=res.rd("w3T_ag"))
            for m in range(8):
                items = []
                for idx in range(11):
                    for n4 in range(4):
                        items.append(mmf(
                            ps_big[:, 512 * n4:512 * (n4 + 1)],
                            aav(16384 + 1024 * idx + 128 * m, 128),
                            aav(27648 + 2048 * idx + 512 * n4, 512),
                            idx == 0, idx == 10))
                mm_group(items,
                         reads=aanames(16384, 11264) + aanames(27648, 22528),
                         writes=("ps_big",))
                op("vector", (lambda mi: lambda e: e.tensor_add(
                    x_sb[:, 2048 * mi:2048 * (mi + 1)],
                    x_sb[:, 2048 * mi:2048 * (mi + 1)], ps_big[:]))(m),
                    reads=("ps_big", "x"), writes=("x",))

    emit_ffn1()
    emit_attn(1)

    fin = dma("sync", out_d.rearrange("(m p) f -> p m f", p=128),
              x_sb[:].rearrange("p (m f) -> p m f", f=2048), reads=("x",),
              writes=("outd",))
    p.emit("sync", lambda e: None, waits=[("d_outd", fin)])

    with nc.Block() as block:
        def mk(engname):
            recs = p.recs[engname]

            def body(e, recs=recs):
                for fn, waits, inc, n in recs:
                    for s, v in waits:
                        e.wait_ge(sems[s], v)
                    instr = fn(e)
                    if inc is not None:
                        instr.then_inc(sems[inc], n)
            return body

        block.sync(mk("sync"))
        block.gpsimd(mk("gpsimd"))
        block.tensor(mk("tensor"))
        block.vector(mk("vector"))
        block.scalar(mk("scalar"))

    ctx.close()
    return nc


def _blk_idx(r):
    return np.concatenate([np.arange(128 * (2 * i + r), 128 * (2 * i + r) + 128)
                           for i in range(8)])


def _mask_for_core(r):
    out = np.zeros((8, 128, 2048), np.float32)
    for i in range(8):
        tl = TLS[i]
        q = (128 * (2 * i + r) + np.arange(128))[:, None]
        k = np.concatenate([128 * (2 * (t % 8) + t // 8) + np.arange(128)
                            for t in tl])[None, :]
        allow = (np.abs(q - k) < 256) | (q < 16) | (k < 16)
        out[i, :, 0:128 * len(tl)] = np.where(allow, np.float32(0.0),
                                              np.float32(NEG))
    return out.astype(ml_dtypes.bfloat16)


_NC_CACHE = [None]


def kernel(L0_attn_hidden_states, L0_attn_Wq, L0_attn_Wk, L0_attn_Wv,
           L0_attn_Wo, L0_ffn_w_gate, L0_ffn_w_up, L0_ffn_w_down,
           L0_ffn_ln_weight, L1_attn_w_qkv, L1_attn_b_qkv, L1_attn_w_out,
           L1_attn_b_out, L1_ffn_weight1, L1_ffn_weight2, L1_ffn_weight3):
    xs = np.asarray(L0_attn_hidden_states)
    f32 = np.float32
    bq = np.ascontiguousarray(np.asarray(L1_attn_b_qkv, f32)
                              .reshape(48, 128).T)
    bout = np.broadcast_to(np.asarray(L1_attn_b_out).astype(ml_dtypes.bfloat16),
                           (128, 2048)).copy()
    lnw = np.broadcast_to(np.asarray(L0_ffn_ln_weight, f32), (128, 8192)).copy()
    ident = np.eye(128, dtype=ml_dtypes.bfloat16)
    nf4c = np.zeros((32,), f32)
    nf4c[0:15] = NF4_MID
    nf4c[16:31] = NF4_DELTA
    nf4c = np.broadcast_to(nf4c, (128, 32)).copy()
    w1p = np.zeros((6144, 2048), ml_dtypes.bfloat16)
    w1p[0:5632] = np.asarray(L1_ffn_weight1)
    w2p = np.zeros((6144, 2048), ml_dtypes.bfloat16)
    w2p[0:5632] = np.asarray(L1_ffn_weight2)
    w3 = np.asarray(L1_ffn_weight3)
    wg = np.asarray(L0_ffn_w_gate, f32)
    wu = np.asarray(L0_ffn_w_up, f32)
    wd = np.asarray(L0_ffn_w_down, f32)

    bidx = [_blk_idx(0), _blk_idx(1)]
    in_maps = []
    for c in range(8):
        b, hf = c // 2, c % 2
        in_maps.append({
            "x_in": np.ascontiguousarray(xs[b][bidx[hf]], f32),
            "wq": np.asarray(L0_attn_Wq, f32),
            "wk": np.asarray(L0_attn_Wk, f32),
            "wv": np.asarray(L0_attn_Wv, f32),
            "wo": np.asarray(L0_attn_Wo, f32),
            "wqkv": np.asarray(L1_attn_w_qkv, f32),
            "wout": np.asarray(L1_attn_w_out, f32),
            "bqkvT": bq,
            "bout_bc": bout,
            "wg_sh": np.ascontiguousarray(wg[1024 * c:1024 * (c + 1), :]),
            "wu_sh": np.ascontiguousarray(wu[1024 * c:1024 * (c + 1), :]),
            "wd_sh": np.ascontiguousarray(wd[:, 1024 * c:1024 * (c + 1)]),
            "lnw_bc": lnw,
            "w1_sh": np.ascontiguousarray(w1p[768 * c:768 * (c + 1), :]),
            "w2_sh": np.ascontiguousarray(w2p[768 * c:768 * (c + 1), :]),
            "w3_sh": np.ascontiguousarray(w3[256 * c:256 * (c + 1), :]),
            "mask_in": _mask_for_core(hf),
            "ident_in": ident,
            "nf4c_in": nf4c,
        })
    if _NC_CACHE[0] is None:
        _NC_CACHE[0] = build_program()
    nc = _NC_CACHE[0]
    r = run_bass_kernel_spmd(nc, in_maps, core_ids=list(range(8)))
    out = np.empty((4, 2048, 2048), np.float32)
    for c in range(8):
        b, hf = c // 2, c % 2
        out[b][bidx[hf]] = r.results[c]["out"]
    return out


if __name__ == "__main__":
    nc = build_program()
    print("build ok")


# version 13
# speedup vs baseline: 1.4666x; 1.0067x over previous
import numpy as np
import ml_dtypes
import concourse.bass as bass
import concourse.mybir as mybir
from concourse.bass_utils import run_bass_kernel_spmd

F32 = mybir.dt.float32
BF16 = mybir.dt.bfloat16
AF = mybir.ActivationFunctionType
ALU = mybir.AluOpType
AX = mybir.AxisListType

MAGIC = float(np.float32(3 << 22))
NEG = -1.0e30
C_ATTN = float(128.0 ** -0.5)

NF4 = np.array([-1.0, -0.6962, -0.5251, -0.3949, -0.2844, -0.1848, -0.0911,
                0.0, 0.0796, 0.1609, 0.2461, 0.3379, 0.4407, 0.5626, 0.723,
                1.0], dtype=ml_dtypes.bfloat16).astype(np.float64)
NF4_MID = ((NF4[:-1] + NF4[1:]) / 2.0).astype(np.float32)
NF4_DELTA = np.diff(NF4).astype(np.float32)
NF4_BASE = float(NF4[0])
NBLK_REAL = 5632 * 2048 // 64

ENGINES = ("sync", "gpsimd", "tensor", "vector", "scalar")

SM_MEAN0, SM_RS0, SM_C1X, SM_ISX = 0, 8, 16, 24
SM_MEAN1, SM_RS1 = 48, 56
SM_G = 64
SM_THR = 72
SM_NTHR = 81
SM_NF4M, SM_NF4D = 96, 112
SM_BQ = 128
SM_ONE = 176
SM_T = 178
SM_SCG, SM_SCU, SM_C3, SM_VAR = 224, 232, 240, 248
SM_ROW = 256
SM_AT = 384
SM_SB1, SM_SB2, SM_QT = 1024, 1280, 1536



def _tl(i):
    if i == 0:
        return list(range(16))
    lo = [t for t in (i - 1, i, i + 1) if 0 <= t <= 7]
    return sorted(set([0] + lo + [8 + t for t in lo]))


TLS = [_tl(i) for i in range(8)]


def _segs(tl):
    runs = []
    for s, t in enumerate(tl):
        if runs and runs[-1][0] + runs[-1][2] == s \
                and runs[-1][1] + runs[-1][2] == t:
            runs[-1][2] += 1
        else:
            runs.append([s, t, 1])
    segs = []
    for s0, t0, L in runs:
        s = s0
        while s < s0 + L:
            e = min(s0 + L, (s // 4 + 1) * 4)
            segs.append((s, t0 + (s - s0), e - s))
            s = e
    return segs

AA_COLS = 50176
WB_COLS = 12288


class Prog:
    def __init__(self):
        self.recs = {e: [] for e in ENGINES}
        self.cnt = {}

    def emit(self, eng, fn, waits=(), inc=None, n=1):
        self.recs[eng].append((fn, tuple(waits), inc, n))
        if inc is not None:
            self.cnt[inc] = self.cnt.get(inc, 0) + n
        return self.cnt.get(inc)


class Res:
    def __init__(self):
        self.w = {}
        self.r = {}

    def rd(self, name):
        return [self.w[name]] if name in self.w else []

    def wr(self, name):
        ws = list(self.r.get(name, []))
        if name in self.w:
            ws.append(self.w[name])
        return ws

    def did_rd(self, name, sem, cnt):
        self.r.setdefault(name, []).append((sem, cnt))

    def did_wr(self, name, sem, cnt):
        self.w[name] = (sem, cnt)
        self.r[name] = []


def merge_waits(*wls):
    best = {}
    for wl in wls:
        for s, v in wl:
            if v > best.get(s, -1):
                best[s] = v
    return list(best.items())


def build_program():
    nc = bass.Bass("TRN2", target_bir_lowering=False, debug=False)
    p = Prog()
    res = Res()

    x_in = nc.dram_tensor("x_in", [1024, 2048], F32, kind="ExternalInput").ap()
    wq_d = nc.dram_tensor("wq", [2048, 2048], F32, kind="ExternalInput").ap()
    wk_d = nc.dram_tensor("wk", [2048, 2048], F32, kind="ExternalInput").ap()
    wv_d = nc.dram_tensor("wv", [2048, 2048], F32, kind="ExternalInput").ap()
    wo_d = nc.dram_tensor("wo", [2048, 2048], F32, kind="ExternalInput").ap()
    wqkv_d = nc.dram_tensor("wqkv", [2048, 6144], F32, kind="ExternalInput").ap()
    wout_d = nc.dram_tensor("wout", [2048, 2048], F32, kind="ExternalInput").ap()
    bqkvT_d = nc.dram_tensor("bqkvT", [128, 48], F32, kind="ExternalInput").ap()
    bout_d = nc.dram_tensor("bout_bc", [128, 2048], BF16, kind="ExternalInput").ap()
    wg_sh = nc.dram_tensor("wg_sh", [1024, 2048], F32, kind="ExternalInput").ap()
    wu_sh = nc.dram_tensor("wu_sh", [1024, 2048], F32, kind="ExternalInput").ap()
    wd_sh = nc.dram_tensor("wd_sh", [2048, 1024], F32, kind="ExternalInput").ap()
    lnw_d = nc.dram_tensor("lnw_bc", [128, 8192], F32, kind="ExternalInput").ap()
    w1_sh = nc.dram_tensor("w1_sh", [768, 2048], BF16, kind="ExternalInput").ap()
    w2_sh = nc.dram_tensor("w2_sh", [768, 2048], BF16, kind="ExternalInput").ap()
    w3_sh = nc.dram_tensor("w3_sh", [256, 5632], BF16, kind="ExternalInput").ap()
    mask_d = nc.dram_tensor("mask_in", [8, 128, 2048], BF16, kind="ExternalInput").ap()
    ident_d = nc.dram_tensor("ident_in", [128, 128], BF16, kind="ExternalInput").ap()
    nf4c_d = nc.dram_tensor("nf4c_in", [128, 32], F32, kind="ExternalInput").ap()
    out_d = nc.dram_tensor("out", [1024, 2048], F32, kind="ExternalOutput").ap()

    def idram(name, shape, dtype=BF16):
        return nc.dram_tensor(name, shape, dtype).ap()

    ar_in = idram("ar_in", [128, 8], F32)
    ar_out = idram("ar_out", [128, 8], F32)
    s_rt = idram("s_rt", [3, 768, 32], F32)
    r_rt = idram("r_rt", [3, 768, 32], F32)
    d_rt = idram("d_rt", [3, 768, 32], BF16)
    def sdram(name, shape, dtype=BF16):
        return nc.dram_tensor(name, shape, dtype, addr_space="Shared").ap()

    wgT_in = idram("wgT_in", [2048, 1024]); wgT_ag = sdram("wgT_ag", [16384, 1024])
    wuT_in = idram("wuT_in", [2048, 1024]); wuT_ag = sdram("wuT_ag", [16384, 1024])
    wdT_in = idram("wdT_in", [1024, 2048]); wdT_ag = sdram("wdT_ag", [8192, 2048])
    w1T_in = idram("w1T_in", [2048, 768]); w1T_ag = sdram("w1T_ag", [16384, 768])
    w2T_in = idram("w2T_in", [2048, 768]); w2T_ag = sdram("w2T_ag", [16384, 768])
    w3T_in = idram("w3T_in", [5632, 256]); w3T_ag = sdram("w3T_ag", [45056, 256])
    kT_in = [idram("kT0_in", [2048, 1024]), idram("kT1_in", [2048, 1024])]
    kT_ag = [idram("kT0_ag", [4096, 1024]), idram("kT1_ag", [4096, 1024])]
    vT_in = [idram("vT0_in", [2048, 1024]), idram("vT1_in", [2048, 1024])]
    vT_ag = [idram("vT0_ag", [4096, 1024]), idram("vT1_ag", [4096, 1024])]
    qT_dr = [idram("qT0_dr", [2048, 1024]), idram("qT1_dr", [2048, 1024])]

    from contextlib import ExitStack
    ctx = ExitStack()
    E = ctx.enter_context
    x_sb = E(nc.sbuf_tensor("x_sb", [128, 16384], F32))
    AA = E(nc.sbuf_tensor("AA", [128, AA_COLS], BF16))
    WB = E(nc.sbuf_tensor("WB", [128, WB_COLS], BF16))
    SM = E(nc.sbuf_tensor("SM", [128, 2048], F32))
    FS = E(nc.sbuf_tensor("FS", [128, 2048], F32))
    IDT = E(nc.sbuf_tensor("IDT", [128, 128], BF16))
    ps_big = E(nc.psum_tensor("ps_big", [128, 2048], F32))
    ps_a = E(nc.psum_tensor("ps_a", [128, 1024], F32))
    ps_b = E(nc.psum_tensor("ps_b", [128, 1024], F32))
    sems = {}
    for s in ("cc", "pe", "ve", "se"):
        sems[s] = E(nc.semaphore(s))

    def getsem(name):
        if name not in sems:
            sems[name] = E(nc.semaphore(name))
        return sems[name]

    ENG_SEM = {"tensor": "pe", "vector": "ve", "scalar": "se",
               "gpsimd": "sw", "sync": "hw"}
    PAIRS = [[0, 1], [2, 3], [4, 5], [6, 7]]
    ALL8 = [list(range(8))]

    def aanames(c0, n):
        return tuple(f"aa{k}" for k in range(c0 // 4096, (c0 + n - 1) // 4096 + 1))

    def wbnames(c0, n):
        return tuple(f"wb{k}" for k in range(c0 // 1024, (c0 + n - 1) // 1024 + 1))

    def dma(engname, out_ap, in_ap, reads=(), writes=(), extra=()):
        assert writes, "dma needs a write-resource for its stream semaphore"
        semname = ("dg_" if engname == "gpsimd" else "d_") + writes[0]
        getsem(semname)
        waits = merge_waits(*([res.rd(r) for r in reads]
                              + [res.wr(w) for w in writes] + [list(extra)]))
        cnt = p.emit(engname,
                     (lambda o, i: lambda e: e.dma_start(out=o, in_=i))(out_ap, in_ap),
                     waits=waits, inc=semname, n=16)
        for r in reads:
            res.did_rd(r, semname, cnt)
        for w in writes:
            res.did_wr(w, semname, cnt)
        return cnt

    def op(engname, fn, reads=(), writes=(), extra=(), sem=None):
        semname = sem or ENG_SEM[engname]
        getsem(semname)
        waits = merge_waits(*([res.rd(r) for r in reads]
                              + [res.wr(w) for w in writes] + [list(extra)]))
        cnt = p.emit(engname, fn, waits=waits, inc=semname, n=1)
        for r in reads:
            res.did_rd(r, semname, cnt)
        for w in writes:
            res.did_wr(w, semname, cnt)
        return cnt

    def mmf(o, l, r, st, sp):
        return lambda e: e.matmul(o, l, r, start=st, stop=sp)

    def mm_group(items, reads=(), writes=(), extra=()):
        waits = merge_waits(*([res.rd(r) for r in reads]
                              + [res.wr(w) for w in writes] + [list(extra)]))
        for it in items[:-1]:
            p.emit("tensor", it, waits=waits)
            waits = ()
        cnt = p.emit("tensor", items[-1], waits=waits, inc="pe", n=1)
        for r in reads:
            res.did_rd(r, "pe", cnt)
        for w in writes:
            res.did_wr(w, "pe", cnt)
        return cnt

    def aav(c0, n):
        return AA[:, c0:c0 + n]

    def wbv(c0, n):
        return WB[:, c0:c0 + n]

    def smc(col, n=1):
        return SM[:, col:col + n]

    def ag(in_ap, out_ap, rname, wname, groups):
        op("gpsimd", (lambda i_, o_, g_: lambda e: e.collective_compute(
            "AllGather", ALU.bypass, replica_groups=g_,
            ins=[i_.opt()], outs=[o_.opt()]))(in_ap, out_ap, groups),
            reads=(rname,), writes=(wname,), sem="cc")

    dma("sync", x_sb[:].rearrange("p (m f) -> p m f", f=2048),
        x_in.rearrange("(m p) f -> p m f", p=128), writes=("x",))
    dma("sync", IDT[:], ident_d, writes=("idt",))
    dma("sync", SM[:, SM_NF4M:SM_NF4M + 32], nf4c_d, writes=("st",))
    dma("sync", SM[:, SM_BQ:SM_BQ + 48], bqkvT_d, writes=("st",))
    op("vector", lambda e: e.memset(smc(SM_ONE), 1.0), writes=("st",))
    op("vector", lambda e: e.memset(SM[0:1, SM_ROW:SM_ROW + 128], 1.0),
       writes=("st",))

    SCR = 36864

    def emit_ln(layer):
        mean_c = SM_MEAN0 if layer == 0 else SM_MEAN1
        rs_c = SM_RS0 if layer == 0 else SM_RS1
        for m in range(8):
            xt = x_sb[:, 2048 * m:2048 * (m + 1)]
            op("vector", (lambda xt_, c: lambda e: e.reduce_sum(
                smc(c), xt_, axis=AX.X))(xt, mean_c + m),
                reads=("x",), writes=("st",))
            op("vector", (lambda c: lambda e: e.tensor_scalar_mul(
                smc(c), smc(c), 1.0 / 2048.0))(mean_c + m), writes=("st",))
            op("scalar", (lambda xt_: lambda e: e.activation(
                aav(SCR, 2048), xt_, AF.Square, accum_out=smc(SM_T)))(xt),
                reads=("x",), writes=aanames(SCR, 2048) + ("st",))
            op("vector", (lambda c: lambda e: e.tensor_tensor(
                smc(SM_T + 1), smc(c), smc(c), ALU.mult))(mean_c + m),
                writes=("st",))
            op("vector", lambda e: e.tensor_scalar(
                smc(SM_T), smc(SM_T), 1.0 / 2048.0, 1e-5, ALU.mult, ALU.add),
                writes=("st",))
            op("vector", lambda e: e.tensor_sub(
                smc(SM_T), smc(SM_T), smc(SM_T + 1)), writes=("st",))
            op("scalar", lambda e: e.activation(smc(SM_T), smc(SM_T), AF.Ln),
               reads=("st",), writes=("st",))
            op("scalar", (lambda c: lambda e: e.activation(
                smc(c), smc(SM_T), AF.Exp, scale=-0.5))(rs_c + m),
                reads=("st",), writes=("st",))
            op("vector", (lambda xt_, mc, rc: lambda e: e.scalar_tensor_tensor(
                out=aav(SCR, 2048), in0=xt_, scalar=smc(mc),
                in1=smc(rc).broadcast_to([128, 2048]), op0=ALU.subtract,
                op1=ALU.mult))(xt, mean_c + m, rs_c + m),
                reads=("x", "st"), writes=aanames(SCR, 2048))
            if layer == 0:
                op("vector", lambda e: e.reduce_max(
                    smc(SM_T), aav(SCR, 2048), axis=AX.X,
                    apply_absolute_value=True),
                    reads=aanames(SCR, 2048), writes=("st",))
                op("vector", lambda e: e.tensor_scalar_max(
                    smc(SM_T), smc(SM_T), 1e-5), writes=("st",))
                op("vector", (lambda mi: lambda e: e.tensor_scalar_mul(
                    smc(SM_ISX + mi), smc(SM_T), 1.0 / 127.0))(m),
                    writes=("st",))
                op("vector", lambda e: e.reciprocal(
                    smc(SM_T + 1), smc(SM_T)), writes=("st",))
                op("vector", (lambda mi: lambda e: e.scalar_tensor_tensor(
                    out=smc(SM_C1X + mi), in0=smc(SM_T + 1), scalar=127.0,
                    in1=smc(SM_RS0 + mi), op0=ALU.mult, op1=ALU.mult))(m),
                    writes=("st",))
            items = [mmf(ps_big[:, 128 * ft:128 * (ft + 1)],
                         aav(SCR + 128 * ft, 128), IDT[:], True, True)
                     for ft in range(16)]
            mm_group(items, reads=aanames(SCR, 2048) + ("idt",),
                     writes=("ps_big",))
            op("vector", (lambda mi: lambda e: e.tensor_copy(
                AA[:, 0:16384].rearrange("p (t q) -> p t q", q=1024)
                [:, :, 128 * mi:128 * (mi + 1)],
                ps_big[:].rearrange("p (t q) -> p t q", q=128)))(m),
                reads=("ps_big",), writes=aanames(0, 16384))

    emit_ln(0)

    for colz in range(8):
        op("vector", (lambda c: lambda e: e.memset(smc(SM_G + c), 0.0))(colz),
           writes=("st",))

    def tern_pass1(dram, gcol, tiles, width):
        for t in range(tiles):
            dma("sync", FS[:, 0:width], dram[128 * t:128 * (t + 1), :],
                writes=("fs",))
            op("vector", (lambda w: lambda e: e.reduce_sum(
                smc(SM_T + 2), FS[:, 0:w], axis=AX.X,
                apply_absolute_value=True))(width),
                reads=("fs",), writes=("st",))
            op("vector", (lambda c: lambda e: e.tensor_add(
                smc(SM_G + c), smc(SM_G + c), smc(SM_T + 2)))(gcol),
                writes=("st",))

    tern_pass1(wg_sh, 0, 8, 2048)
    tern_pass1(wu_sh, 1, 8, 2048)
    tern_pass1(wd_sh, 2, 16, 1024)

    NF4S = [(w1_sh, 0, 1024, 6, 2048, 3), (w2_sh, 1, 1216, 6, 2048, 4),
            (w3_sh, 2, 1408, 2, 5632, 5)]
    for dram, widx, scol, tiles, width, gcol in NF4S:
        nblk = width // 64
        for t in range(tiles):
            nsub = (width + 2047) // 2048
            for s in range(nsub):
                c0 = 2048 * s
                w = min(2048, width - c0)
                dma("sync", wbv(0, w), dram[128 * t:128 * (t + 1), c0:c0 + w],
                    writes=wbnames(0, w))
                op("vector", (lambda w_, sc: lambda e: e.reduce_max(
                    SM[:, sc:sc + w_ // 64].unsqueeze(2),
                    WB[:, 0:w_].rearrange("p (b i) -> p b i", i=64),
                    axis=AX.X, apply_absolute_value=True))(
                        w, scol + t * nblk + c0 // 64),
                    reads=wbnames(0, w), writes=("smsb",))
        op("vector", (lambda sc, n: lambda e: e.reduce_sum(
            smc(SM_T + 2), SM[:, sc:sc + n], axis=AX.X))(scol, tiles * nblk),
            reads=("smsb",), writes=("st",))
        op("vector", (lambda c: lambda e: e.tensor_add(
            smc(SM_G + c), smc(SM_G + c), smc(SM_T + 2)))(gcol),
            writes=("st",))

    dma("sync", ar_in, SM[:, SM_G:SM_G + 8], reads=("st",), writes=("ar_in",))
    op("gpsimd", lambda e: e.collective_compute(
        "AllReduce", ALU.add, replica_groups=ALL8,
        ins=[ar_in.opt()], outs=[ar_out.opt()]),
        reads=("ar_in",), writes=("ar_out",), sem="cc")
    dma("sync", SM[:, SM_G:SM_G + 8], ar_out, reads=("ar_out",), writes=("st",))
    op("tensor", lambda e: e.matmul(ps_a[0:1, 0:8], smc(SM_ONE),
                                    SM[:, SM_G:SM_G + 8], start=True, stop=True),
       reads=("st",), writes=("ps_a",))
    op("vector", lambda e: e.tensor_copy(SM[0:1, SM_T + 4:SM_T + 12],
                                         ps_a[0:1, 0:8]),
       reads=("ps_a",), writes=("st",))
    op("tensor", lambda e: e.matmul(ps_a[:, 0:8], SM[0:1, SM_ROW:SM_ROW + 128],
                                    SM[0:1, SM_T + 4:SM_T + 12],
                                    start=True, stop=True),
       reads=("st",), writes=("ps_a",))
    op("vector", lambda e: e.tensor_copy(SM[:, SM_G:SM_G + 8], ps_a[:, 0:8]),
       reads=("ps_a",), writes=("st",))
    NW = float(8192 * 2048)
    for j in range(3):
        op("vector", (lambda jj: lambda e: e.tensor_scalar(
            smc(SM_THR + 3 + jj), smc(SM_G + jj), 1.0 / NW, 1e-5,
            ALU.mult, ALU.max))(j), writes=("st",))
        op("vector", (lambda jj: lambda e: e.tensor_scalar_mul(
            smc(SM_THR + jj), smc(SM_THR + 3 + jj), 0.5))(j), writes=("st",))
        op("vector", (lambda jj: lambda e: e.tensor_scalar_mul(
            smc(SM_NTHR + jj), smc(SM_THR + jj), -1.0))(j), writes=("st",))
        op("vector", (lambda jj: lambda e: e.tensor_scalar_mul(
            smc(SM_THR + 6 + jj), smc(SM_G + 3 + jj), 1.0 / NBLK_REAL))(j),
            writes=("st",))
        op("scalar", (lambda jj: lambda e: e.copy(
            wbv(11264 + jj, 1), smc(SM_THR + 6 + jj)))(j),
            reads=("st",), writes=("wbm",))
        op("scalar", (lambda jj: lambda e: e.copy(
            smc(SM_NTHR + 3 + jj), wbv(11264 + jj, 1)))(j),
            reads=("wbm",), writes=("st",))

    def emit_proj(layer):
        srcs = {"q": (wq_d, 0), "k": (wk_d, 0), "v": (wv_d, 0)} if layer == 0 \
            else {"q": (wqkv_d, 0), "k": (wqkv_d, 2048), "v": (wqkv_d, 4096)}
        dsts = {"k": kT_in[layer], "v": vT_in[layer], "q": qT_dr[layer]}
        for which in ("k", "v", "q"):
            wsrc, coff = srcs[which]
            for t in range(16):
                slot = 18432 + 2048 * (t % 2)
                dma("gpsimd",
                    aav(slot, 2048).rearrange("p (ft c) -> p ft c", c=128),
                    wsrc[:, coff + 128 * t:coff + 128 * (t + 1)]
                    .rearrange("(ft p) c -> p ft c", p=128),
                    writes=aanames(slot, 2048))
                items = []
                for n2 in range(2):
                    for ft in range(16):
                        items.append(mmf(
                            ps_a[:, 512 * n2:512 * (n2 + 1)],
                            aav(slot + 128 * ft, 128),
                            aav(1024 * ft + 512 * n2, 512),
                            ft == 0, ft == 15))
                mm_group(items, reads=aanames(slot, 2048) + aanames(0, 16384),
                         writes=("ps_a",))
                stg = 22528 + 1024 * (t % 2)
                if layer == 1:
                    boff = {"q": 0, "k": 16, "v": 32}[which] + t
                    op("scalar", (lambda st_, bo: lambda e: e.activation(
                        aav(st_, 1024), ps_a[:], AF.Identity,
                        bias=smc(SM_BQ + bo)))(stg, boff),
                        reads=("ps_a", "st"), writes=aanames(stg, 1024))
                else:
                    op("scalar", (lambda st_: lambda e: e.copy(
                        aav(st_, 1024), ps_a[:]))(stg),
                        reads=("ps_a",), writes=aanames(stg, 1024))
                dma("sync", dsts[which][128 * t:128 * (t + 1), :],
                    aav(stg, 1024), reads=aanames(stg, 1024),
                    writes=(f"{which}T{layer}_in",))
                yield
            if which in ("k", "v"):
                agi = kT_in[layer] if which == "k" else vT_in[layer]
                ago = kT_ag[layer] if which == "k" else vT_ag[layer]
                ag(agi, ago, f"{which}T{layer}_in", f"{which}T{layer}_ag",
                   PAIRS)

    def transpose_tile_out(src_c0, width, stage_c0, reads):
        nt = width // 128
        items = [mmf(ps_big[:, 128 * j:128 * (j + 1)],
                     wbv(src_c0 + 128 * j, 128), IDT[:], True, True)
                 for j in range(nt)]
        mm_group(items, reads=tuple(reads) + ("idt",), writes=("ps_big",))
        op("vector", (lambda w: lambda e: e.tensor_copy(
            wbv(stage_c0, w), ps_big[:, 0:w]))(width),
            reads=("ps_big",), writes=wbnames(stage_c0, width))

    def tern_pass2(dram, gidx, tiles, width, dst, dstname):
        for t in range(tiles):
            dma("sync", FS[:, 0:width], dram[128 * t:128 * (t + 1), :],
                writes=("fs",))
            op("vector", (lambda w, g: lambda e: e.tensor_single_scalar(
                wbv(0, w), FS[:, 0:w], smc(SM_THR + g), ALU.is_gt))(width, gidx),
                reads=("fs", "st"), writes=wbnames(0, width))
            op("vector", (lambda w, g: lambda e: e.tensor_scalar(
                wbv(2048, w), FS[:, 0:w], smc(SM_NTHR + g), -1.0,
                ALU.is_gt, ALU.add))(width, gidx),
                reads=("fs", "st"), writes=wbnames(2048, width))
            op("vector", (lambda w: lambda e: e.tensor_add(
                wbv(4096, w), wbv(0, w), wbv(2048, w)))(width),
                reads=wbnames(0, width) + wbnames(2048, width),
                writes=wbnames(4096, width))
            transpose_tile_out(4096, width, 8192, wbnames(4096, width))
            dma("sync",
                dst[:, 128 * t:128 * (t + 1)]
                .rearrange("(f p) c -> p f c", p=128),
                wbv(8192, width).rearrange("p (f c) -> p f c", c=128),
                reads=wbnames(8192, width), writes=(dstname,))
            yield

    def nf4_pass2(dram, widx, tiles, width, scol, dst, dstname):
        nblk = width // 64
        nsc = tiles * nblk
        nsb = nsc * 128 // 256
        P = nsb
        flat = nsc * 128
        rowc = nblk if width == 5632 else 32
        dma("sync", SM[0:P, SM_SB1:SM_SB1 + 256],
            s_rt[widx].flatten()[0:flat].rearrange("(s c) -> s c", c=256),
            reads=("s_rt",), writes=("smsb1",))
        op("vector", (lambda P_, w_: lambda e: e.tensor_single_scalar(
            SM[0:P_, SM_SB2:SM_SB2 + 256], SM[0:P_, SM_SB1:SM_SB1 + 256],
            SM[0:P_, SM_THR + 6 + w_:SM_THR + 7 + w_], ALU.subtract))(P, widx),
            reads=("smsb1", "st"), writes=("smsb2",))
        op("vector", (lambda P_: lambda e: e.reduce_max(
            SM[0:P_, SM_T + 4:SM_T + 5], SM[0:P_, SM_SB2:SM_SB2 + 256],
            axis=AX.X, apply_absolute_value=True))(P),
            reads=("smsb2",), writes=("st",))
        op("vector", (lambda P_: lambda e: e.reciprocal(
            SM[0:P_, SM_T + 5:SM_T + 6], SM[0:P_, SM_T + 4:SM_T + 5]))(P),
            writes=("st",))
        op("vector", (lambda P_: lambda e: e.tensor_scalar_mul(
            SM[0:P_, SM_T + 5:SM_T + 6], SM[0:P_, SM_T + 5:SM_T + 6],
            128.0))(P), writes=("st",))
        op("vector", (lambda P_: lambda e: e.tensor_scalar(
            SM[0:P_, SM_SB2:SM_SB2 + 256], SM[0:P_, SM_SB2:SM_SB2 + 256],
            SM[0:P_, SM_T + 5:SM_T + 6], MAGIC, ALU.mult, ALU.add))(P),
            reads=("st",), writes=("smsb2",))
        op("vector", (lambda P_: lambda e: e.tensor_scalar(
            SM[0:P_, SM_SB2:SM_SB2 + 256], SM[0:P_, SM_SB2:SM_SB2 + 256],
            -MAGIC, -128.0, ALU.add, ALU.max))(P), writes=("smsb2",))
        op("vector", (lambda P_: lambda e: e.tensor_scalar_min(
            SM[0:P_, SM_SB2:SM_SB2 + 256], SM[0:P_, SM_SB2:SM_SB2 + 256],
            127.0))(P), writes=("smsb2",))
        op("vector", (lambda P_: lambda e: e.tensor_scalar_mul(
            SM[0:P_, SM_T + 4:SM_T + 5], SM[0:P_, SM_T + 4:SM_T + 5],
            1.0 / 128.0))(P), writes=("st",))
        op("vector", (lambda P_: lambda e: e.tensor_single_scalar(
            WB[0:P_, 8192:8448], SM[0:P_, SM_SB2:SM_SB2 + 256],
            SM[0:P_, SM_T + 4:SM_T + 5], ALU.mult))(P),
            reads=("smsb2", "st"), writes=wbnames(8192, 256))
        op("vector", (lambda P_, w_: lambda e: e.tensor_single_scalar(
            WB[0:P_, 8192:8448], WB[0:P_, 8192:8448],
            SM[0:P_, SM_NTHR + 3 + w_:SM_NTHR + 4 + w_], ALU.add))(P, widx),
            reads=("st",), writes=wbnames(8192, 256))
        op("vector", (lambda P_: lambda e: e.tensor_scalar_max(
            SM[0:P_, SM_SB1:SM_SB1 + 256], SM[0:P_, SM_SB1:SM_SB1 + 256],
            1e-30))(P), writes=("smsb1",))
        op("vector", (lambda P_: lambda e: e.reciprocal(
            SM[0:P_, SM_SB2:SM_SB2 + 256], SM[0:P_, SM_SB1:SM_SB1 + 256]))(P),
            reads=("smsb1",), writes=("smsb2",))
        dma("sync", d_rt[widx].flatten()[0:flat].rearrange(
            "(s c) -> s c", c=256)[0:P, :], WB[0:P, 8192:8448],
            reads=wbnames(8192, 256), writes=("d_rt",))
        dma("sync", r_rt[widx].flatten()[0:flat].rearrange(
            "(s c) -> s c", c=256)[0:P, :], SM[0:P, SM_SB2:SM_SB2 + 256],
            reads=("smsb2",), writes=("r_rt",))
        dma("sync", WB[:, 8448:8448 + nsc].rearrange(
            "p (t c) -> p t c", c=rowc),
            d_rt[widx].flatten()[0:flat].rearrange(
                "(t p c) -> p t c", p=128, c=rowc),
            reads=("d_rt",), writes=wbnames(8448, nsc))
        dma("sync", FS[:, 0:nsc].rearrange("p (t c) -> p t c", c=rowc),
            r_rt[widx].flatten()[0:flat].rearrange(
                "(t p c) -> p t c", p=128, c=rowc),
            reads=("r_rt",), writes=("fs",))
        yield
        for t in range(tiles):
            nsub = (width + 2047) // 2048
            for s in range(nsub):
                c0 = 2048 * s
                w = min(2048, width - c0)
                bc = t * nblk + c0 // 64
                dma("sync", wbv(0, w), dram[128 * t:128 * (t + 1), c0:c0 + w],
                    writes=wbnames(0, w))
                op("vector", (lambda w_, b_: lambda e: e.tensor_tensor(
                    wbv(2048, w_).rearrange("p (b i) -> p b i", i=64),
                    wbv(0, w_).rearrange("p (b i) -> p b i", i=64),
                    FS[:, b_:b_ + w_ // 64].unsqueeze(2)
                    .broadcast_to([128, w_ // 64, 64]), ALU.mult))(w, bc),
                    reads=wbnames(0, w) + ("fs",), writes=wbnames(2048, w))
                op("vector", (lambda w_: lambda e: e.tensor_scalar(
                    wbv(4096, w_), wbv(2048, w_), smc(SM_NF4M),
                    float(NF4_DELTA[0]), ALU.is_gt, ALU.mult))(w),
                    reads=wbnames(2048, w) + ("st",), writes=wbnames(4096, w))
                op("vector", (lambda w_: lambda e: e.tensor_scalar_add(
                    wbv(4096, w_), wbv(4096, w_), NF4_BASE))(w),
                    writes=wbnames(4096, w))
                for kk in range(1, 15):
                    op("vector", (lambda w_, k_: lambda e: e.tensor_scalar(
                        wbv(6144, w_), wbv(2048, w_), smc(SM_NF4M + k_),
                        float(NF4_DELTA[k_]), ALU.is_gt, ALU.mult))(w, kk),
                        reads=wbnames(2048, w) + ("st",),
                        writes=wbnames(6144, w))
                    op("vector", (lambda w_: lambda e: e.tensor_add(
                        wbv(4096, w_), wbv(4096, w_), wbv(6144, w_)))(w),
                        reads=wbnames(6144, w), writes=wbnames(4096, w))
                op("vector", (lambda w_, b_: lambda e: e.tensor_tensor(
                    wbv(0, w_).rearrange("p (b i) -> p b i", i=64),
                    wbv(4096, w_).rearrange("p (b i) -> p b i", i=64),
                    WB[:, 8448 + b_:8448 + b_ + w_ // 64].unsqueeze(2)
                    .broadcast_to([128, w_ // 64, 64]), ALU.mult))(w, bc),
                    reads=wbnames(4096, w) + wbnames(8448, nsc),
                    writes=wbnames(0, w))
                transpose_tile_out(0, w, 9472, wbnames(0, w))
                dma("sync",
                    dst[c0:c0 + w, 128 * t:128 * (t + 1)]
                    .rearrange("(f p) c -> p f c", p=128),
                    wbv(9472, w).rearrange("p (f c) -> p f c", c=128),
                    reads=wbnames(9472, w), writes=(dstname,))
                yield

    def side_steps():
        yield from tern_pass2(wg_sh, 0, 8, 2048, wgT_in, "wgT_in")
        ag(wgT_in, wgT_ag, "wgT_in", "wgT_ag", ALL8)
        yield from tern_pass2(wu_sh, 1, 8, 2048, wuT_in, "wuT_in")
        ag(wuT_in, wuT_ag, "wuT_in", "wuT_ag", ALL8)
        yield from tern_pass2(wd_sh, 2, 16, 1024, wdT_in, "wdT_in")
        ag(wdT_in, wdT_ag, "wdT_in", "wdT_ag", ALL8)
        for widx, scol, tiles, width in ((0, 1024, 6, 2048),
                                         (1, 1216, 6, 2048),
                                         (2, 1408, 2, 5632)):
            nblk = width // 64
            nsc = tiles * nblk
            flat = nsc * 128
            rowc = nblk if width == 5632 else 32
            dma("sync",
                s_rt[widx].flatten()[0:flat].rearrange(
                    "(t p c) -> p t c", p=128, c=rowc),
                SM[:, scol:scol + nsc].rearrange("p (t c) -> p t c", c=rowc),
                reads=("smsb",), writes=("s_rt",))
        yield
        yield from nf4_pass2(w1_sh, 0, 6, 2048, 1024, w1T_in, "w1T_in")
        yield from nf4_pass2(w2_sh, 1, 6, 2048, 1216, w2T_in, "w2T_in")
        yield from nf4_pass2(w3_sh, 2, 2, 5632, 1408, w3T_in, "w3T_in")

    _sg = side_steps()
    for _ in emit_proj(0):
        next(_sg, None)
    for _ in _sg:
        pass

    def emit_ffn0():
        for m in range(8):
            op("vector", (lambda mi: lambda e: e.tensor_tensor(
                smc(SM_SCG + mi), smc(SM_ISX + mi), smc(SM_THR + 3),
                ALU.mult))(m), writes=("st",))
            op("vector", (lambda mi: lambda e: e.tensor_tensor(
                smc(SM_SCU + mi), smc(SM_ISX + mi), smc(SM_THR + 4),
                ALU.mult))(m), writes=("st",))
        for th in range(2):
            ms = [4 * th + i for i in range(4)]
            hslot = [8192, 16384, 24576, 32768]
            for i, m in enumerate(ms):
                xt = x_sb[:, 2048 * m:2048 * (m + 1)]
                for cchunk in range(4):
                    c0 = 512 * cchunk
                    op("vector", (lambda xt_, mi, c_: lambda e:
                        e.scalar_tensor_tensor(
                            out=SM[:, SM_QT:SM_QT + 512],
                            in0=xt_[:, c_:c_ + 512],
                            scalar=smc(SM_MEAN0 + mi),
                            in1=smc(SM_C1X + mi).broadcast_to([128, 512]),
                            op0=ALU.subtract, op1=ALU.mult))(xt, m, c0),
                        reads=("x", "st"), writes=("smqt",))
                    op("vector", lambda e: e.tensor_scalar(
                        SM[:, SM_QT:SM_QT + 512], SM[:, SM_QT:SM_QT + 512],
                        MAGIC, -MAGIC, ALU.add, ALU.add), writes=("smqt",))
                    op("vector", (lambda c_: lambda e: e.tensor_copy(
                        aav(SCR + c_, 512), SM[:, SM_QT:SM_QT + 512]))(c0),
                        reads=("smqt",), writes=aanames(SCR, 2048))
                items = [mmf(ps_big[:, 128 * ft:128 * (ft + 1)],
                             aav(SCR + 128 * ft, 128), IDT[:], True, True)
                         for ft in range(16)]
                mm_group(items, reads=aanames(SCR, 2048) + ("idt",),
                         writes=("ps_big",))
                op("vector", (lambda ii: lambda e: e.tensor_copy(
                    AA[:, 0:8192].rearrange("p (t q) -> p t q", q=512)
                    [:, :, 128 * ii:128 * (ii + 1)],
                    ps_big[:].rearrange("p (t q) -> p t q", q=128)))(i),
                    reads=("ps_big",), writes=aanames(0, 8192))
            for m in ms:
                op("vector", (lambda mi: lambda e: e.memset(
                    smc(SM_VAR + mi), 0.0))(m), writes=("st",))
            for i in range(4):
                op("vector", (lambda ii: lambda e: e.memset(
                    smc(SM_T + 12 + ii), 0.0))(i), writes=("st",))
            for n in range(16):
                r, off = (512 * n) // 1024, (512 * n) % 1024
                dma("sync", WB[:, 0:8192].rearrange("p (f c) -> p f c", c=512),
                    wgT_ag[2048 * r:2048 * (r + 1), off:off + 512]
                    .rearrange("(f p) c -> p f c", p=128),
                    writes=wbnames(0, 8192), extra=res.rd("wgT_ag"))
                dma("sync", AA[:, 40960:49152].rearrange(
                    "p (f c) -> p f c", c=512),
                    wuT_ag[2048 * r:2048 * (r + 1), off:off + 512]
                    .rearrange("(f p) c -> p f c", p=128),
                    writes=aanames(40960, 8192), extra=res.rd("wuT_ag"))
                dma("gpsimd", wbv(8960, 512),
                    lnw_d[:, 512 * n:512 * (n + 1)],
                    writes=wbnames(8960, 512))
                for i, m in enumerate(ms):
                    items = [mmf(ps_a[:, 0:512], aav(512 * ft + 128 * i, 128),
                                 wbv(512 * ft, 512), ft == 0, ft == 15)
                             for ft in range(16)]
                    mm_group(items, reads=aanames(0, 8192) + wbnames(0, 8192),
                             writes=("ps_a",))
                    items = [mmf(ps_b[:, 0:512], aav(512 * ft + 128 * i, 128),
                                 aav(40960 + 512 * ft, 512), ft == 0, ft == 15)
                             for ft in range(16)]
                    mm_group(items,
                             reads=aanames(0, 8192) + aanames(40960, 8192),
                             writes=("ps_b",))
                    op("scalar", (lambda mi: lambda e: e.activation(
                        wbv(8192, 512), ps_a[:, 0:512], AF.Silu,
                        scale=smc(SM_SCG + mi)))(m),
                        reads=("ps_a", "st"), writes=wbnames(8192, 512))
                    hc = hslot[i] + 512 * n
                    op("vector", (lambda mi, hc_: lambda e:
                        e.scalar_tensor_tensor(
                            out=aav(hc_, 512), in0=ps_b[:, 0:512],
                            scalar=smc(SM_SCU + mi), in1=wbv(8192, 512),
                            op0=ALU.mult, op1=ALU.mult))(m, hc),
                        reads=("ps_b", "st") + wbnames(8192, 512),
                        writes=aanames(hc, 512))
                    op("scalar", (lambda hc_: lambda e: e.activation(
                        wbv(9984, 512), aav(hc_, 512), AF.Square,
                        accum_out=smc(SM_T + 2)))(hc),
                        reads=aanames(hc, 512),
                        writes=wbnames(9984, 512) + ("st",))
                    op("vector", (lambda mi: lambda e: e.tensor_add(
                        smc(SM_VAR + mi), smc(SM_VAR + mi),
                        smc(SM_T + 2)))(m), writes=("st",))
                    op("vector", (lambda hc_: lambda e: e.tensor_tensor(
                        SM[:, SM_QT:SM_QT + 512], aav(hc_, 512),
                        wbv(8960, 512), ALU.mult))(hc),
                        reads=aanames(hc, 512) + wbnames(8960, 512),
                        writes=("smqt",))
                    op("vector", lambda e: e.reduce_max(
                        smc(SM_T + 2), SM[:, SM_QT:SM_QT + 512], axis=AX.X,
                        apply_absolute_value=True),
                        reads=("smqt",), writes=("st",))
                    op("vector", (lambda ii: lambda e: e.tensor_max(
                        smc(SM_T + 12 + ii), smc(SM_T + 12 + ii),
                        smc(SM_T + 2)))(i), writes=("st",))
            for i, m in enumerate(ms):
                op("vector", (lambda mi: lambda e: e.tensor_scalar(
                    smc(SM_T + 2), smc(SM_VAR + mi), 1.0 / 8192.0, 1e-6,
                    ALU.mult, ALU.add))(m), writes=("st",))
                op("scalar", lambda e: e.activation(
                    smc(SM_T + 2), smc(SM_T + 2), AF.Ln),
                    reads=("st",), writes=("st",))
                op("scalar", lambda e: e.activation(
                    smc(SM_T + 3), smc(SM_T + 2), AF.Exp, scale=-0.5),
                    reads=("st",), writes=("st",))
                op("vector", (lambda ii: lambda e: e.tensor_tensor(
                    smc(SM_T + 2), smc(SM_T + 3), smc(SM_T + 12 + ii),
                    ALU.mult))(i), writes=("st",))
                op("vector", lambda e: e.tensor_scalar_max(
                    smc(SM_T + 2), smc(SM_T + 2), 1e-5), writes=("st",))
                op("vector", (lambda mi: lambda e: e.tensor_scalar_mul(
                    smc(SM_C3 + mi), smc(SM_T + 2), 1.0 / 127.0))(m),
                    writes=("st",))
                op("vector", lambda e: e.reciprocal(
                    smc(SM_T + 2), smc(SM_T + 2)), writes=("st",))
                op("vector", lambda e: e.scalar_tensor_tensor(
                    out=smc(SM_T + 2), in0=smc(SM_T + 2), scalar=127.0,
                    in1=smc(SM_T + 3), op0=ALU.mult, op1=ALU.mult),
                    writes=("st",))
                op("vector", (lambda mi: lambda e: e.tensor_tensor(
                    smc(SM_C3 + mi), smc(SM_C3 + mi), smc(SM_THR + 5),
                    ALU.mult))(m), writes=("st",))
                for n in range(16):
                    hc = hslot[i] + 512 * n
                    dma("gpsimd", wbv(8960, 512),
                        lnw_d[:, 512 * n:512 * (n + 1)],
                        writes=wbnames(8960, 512))
                    op("vector", (lambda hc_: lambda e: e.tensor_tensor(
                        SM[:, SM_QT:SM_QT + 512], aav(hc_, 512),
                        wbv(8960, 512), ALU.mult))(hc),
                        reads=aanames(hc, 512) + wbnames(8960, 512),
                        writes=("smqt",))
                    op("vector", lambda e: e.tensor_scalar(
                        SM[:, SM_QT:SM_QT + 512], SM[:, SM_QT:SM_QT + 512],
                        smc(SM_T + 2), MAGIC, ALU.mult, ALU.add),
                        reads=("st",), writes=("smqt",))
                    op("vector", (lambda hc_: lambda e: e.tensor_scalar_add(
                        aav(hc_, 512), SM[:, SM_QT:SM_QT + 512], -MAGIC))(hc),
                        reads=("smqt",), writes=aanames(hc, 512))
                for batch in range(4):
                    c0 = hslot[i] + 2048 * batch
                    items = [mmf(ps_big[:, 128 * j:128 * (j + 1)],
                                 aav(c0 + 128 * j, 128), IDT[:], True, True)
                             for j in range(16)]
                    mm_group(items, reads=aanames(c0, 2048) + ("idt",),
                             writes=("ps_big",))
                    op("vector", (lambda c_: lambda e: e.tensor_copy(
                        aav(c_, 2048), ps_big[:]))(c0),
                        reads=("ps_big",), writes=aanames(c0, 2048))
            for dg in range(8):
                wdc = []
                for jj in range(8):
                    d = 8 * dg + jj
                    dst_c = (2048 * jj) if jj < 4 else (40960 + 2048 * (jj - 4))
                    wdc.append(dst_c)
                    dma("sync", aav(dst_c, 2048),
                        wdT_ag[128 * d:128 * (d + 1), :],
                        writes=aanames(dst_c, 2048),
                        extra=res.rd("wdT_ag"))
                for i, m in enumerate(ms):
                    items = []
                    for jj in range(8):
                        d = 8 * dg + jj
                        for n4 in range(4):
                            items.append(mmf(
                                ps_big[:, 512 * n4:512 * (n4 + 1)],
                                aav(hslot[i] + 128 * d, 128),
                                aav(wdc[jj] + 512 * n4, 512),
                                jj == 0, jj == 7))
                    rdn = aanames(hslot[i], 8192) + aanames(0, 8192) \
                        + aanames(40960, 8192)
                    mm_group(items, reads=rdn, writes=("ps_big",))
                    op("vector", (lambda mi: lambda e: e.scalar_tensor_tensor(
                        out=x_sb[:, 2048 * mi:2048 * (mi + 1)],
                        in0=ps_big[:], scalar=smc(SM_C3 + mi),
                        in1=x_sb[:, 2048 * mi:2048 * (mi + 1)],
                        op0=ALU.mult, op1=ALU.add))(m),
                        reads=("ps_big", "st", "x"), writes=("x",))

    emit_ffn0()
    ag(w1T_in, w1T_ag, "w1T_in", "w1T_ag", ALL8)
    ag(w2T_in, w2T_ag, "w2T_in", "w2T_ag", ALL8)
    ag(w3T_in, w3T_ag, "w3T_in", "w3T_ag", ALL8)

    def emit_attn(layer):
        use_mask = layer == 0
        wo_src = wo_d if layer == 0 else wout_d
        kag, vag, qdr = kT_ag[layer], vT_ag[layer], qT_dr[layer]
        kname, vname = f"kT{layer}_ag", f"vT{layer}_ag"
        qname = f"qT{layer}_in"
        tls = TLS if layer == 0 else [list(range(16))] * 8
        offs = [0]
        for i in range(8):
            offs.append(offs[-1] + 128 * len(tls[i]))
        if layer == 1:
            dma("sync", wbv(0, 2048), bout_d, writes=wbnames(0, 2048))
            for m in range(8):
                op("vector", (lambda mi: lambda e: e.tensor_add(
                    x_sb[:, 2048 * mi:2048 * (mi + 1)],
                    x_sb[:, 2048 * mi:2048 * (mi + 1)], wbv(0, 2048)))(m),
                    reads=wbnames(0, 2048) + ("x",), writes=("x",))
        if use_mask:
            for qt in range(8):
                Wm = 128 * len(tls[qt])
                dma("sync", aav(40960 + offs[qt], Wm), mask_d[qt][:, 0:Wm],
                    writes=aanames(40960 + offs[qt], Wm))

        def psb_names(qt):
            if layer == 1 or len(tls[qt]) > 8:
                return ("ps_big", "psbg0", "psbg1")
            return (f"psbg{qt % 2}",)

        def softmax_stage(qt):
            tl = tls[qt]
            nt = len(tl)
            W = 128 * nt
            par = qt % 2
            dense = nt > 8
            st0 = SM_AT + 4 * par
            if layer == 1:
                pc = 6144 + 2048 * par
                po = 0
            else:
                pc = 6144 if dense else 6144 + 1024 * par
                po = 0 if dense else 1024 * par
            psn = psb_names(qt)
            san = (f"sa{par}",)
            items = []
            for s0, t0, L in _segs(tl):
                items.append(mmf(ps_big[:, po + 128 * s0:po + 128 * (s0 + L)],
                                 wbv(10240 + 128 * qt, 128),
                                 wbv(128 * t0, 128 * L), True, True))
            mm_group(items, reads=wbnames(10240, 1024) + wbnames(0, 2048),
                     writes=psn)
            if use_mask:
                op("vector", (lambda W_, mo, po_: lambda e: e.tensor_add(
                    ps_big[:, po_:po_ + W_], ps_big[:, po_:po_ + W_],
                    aav(mo, W_)))(W, 40960 + offs[qt], po),
                    reads=aanames(40960 + offs[qt], W), writes=psn)
            op("vector", (lambda W_, c, po_: lambda e: e.reduce_max(
                smc(c), ps_big[:, po_:po_ + W_], axis=AX.X))(W, st0, po),
                reads=psn, writes=san)
            op("vector", (lambda c: lambda e: e.tensor_scalar_mul(
                smc(c + 1), smc(c), -C_ATTN))(st0), writes=san)
            op("scalar", (lambda W_, c, pc_, po_: lambda e: e.activation(
                wbv(pc_, W_), ps_big[:, po_:po_ + W_], AF.Exp,
                bias=smc(c + 1), scale=C_ATTN,
                accum_out=smc(c + 2)))(W, st0, pc, po),
                reads=psn + san, writes=wbnames(pc, W) + san)
            op("vector", (lambda c: lambda e: e.reciprocal(
                smc(c + 3), smc(c + 2)))(st0), writes=san)
            op("scalar", (lambda W_, c, pc_: lambda e: e.activation(
                wbv(pc_, W_), wbv(pc_, W_), AF.Identity,
                scale=smc(c + 3)))(W, st0, pc),
                reads=san, writes=wbnames(pc, W))

        def transpose_stage(qt):
            tl = tls[qt]
            nt = len(tl)
            par = qt % 2
            dense = nt > 8
            if layer == 1:
                pc = 6144 + 2048 * par
            else:
                pc = 6144 if dense else 6144 + 1024 * par
            if layer == 0:
                for w0 in range(0, nt, 8):
                    nw = min(8, nt - w0)
                    items = [mmf(ps_b[:, 128 * j:128 * (j + 1)],
                                 wbv(pc + 128 * (w0 + j), 128), IDT[:],
                                 True, True) for j in range(nw)]
                    mm_group(items, reads=wbnames(pc, 128 * nt) + ("idt",),
                             writes=("ps_b",))
                    dst = offs[qt] + 128 * w0
                    op("scalar", (lambda d_, n_: lambda e: e.copy(
                        AA[:, d_:d_ + 128 * n_],
                        ps_b[:, 0:128 * n_]))(dst, nw),
                        reads=("ps_b",), writes=aanames(dst, 128 * nw))
            else:
                items = [mmf(ps_a[:, 128 * j:128 * (j + 1)],
                             wbv(pc + 128 * j, 128), IDT[:], True, True)
                         for j in range(8)]
                mm_group(items, reads=wbnames(pc, 2048) + ("idt",),
                         writes=("ps_a",))
                items = [mmf(ps_b[:, 128 * j:128 * (j + 1)],
                             wbv(pc + 1024 + 128 * j, 128), IDT[:],
                             True, True) for j in range(8)]
                mm_group(items, reads=wbnames(pc, 2048) + ("idt",),
                         writes=("ps_b",))
                op("scalar", (lambda q_: lambda e: e.copy(
                    AA[:, 0:16384].rearrange("p (j q) -> p j q", q=1024)
                    [:, 0:8, 128 * q_:128 * (q_ + 1)],
                    ps_a[:].rearrange("p (j q) -> p j q", q=128)))(qt),
                    reads=("ps_a",), writes=aanames(0, 8192))
                op("scalar", (lambda q_: lambda e: e.copy(
                    AA[:, 0:16384].rearrange("p (j q) -> p j q", q=1024)
                    [:, 8:16, 128 * q_:128 * (q_ + 1)],
                    ps_b[:].rearrange("p (j q) -> p j q", q=128)))(qt),
                    reads=("ps_b",), writes=aanames(8192, 8192))

        for g in range(4):
            for hh in range(4):
                dma("gpsimd", aav(20480 + 2048 * hh, 2048),
                    wo_src[512 * g + 128 * hh:512 * g + 128 * (hh + 1), :],
                    writes=aanames(20480 + 2048 * hh, 2048))
            for hh in range(4):
                h = 4 * g + hh
                for half in range(2):
                    dma("sync", wbv(1024 * half, 1024),
                        kag[2048 * half + 128 * h:2048 * half + 128 * (h + 1), :],
                        writes=wbnames(1024 * half, 1024),
                        extra=res.rd(kname))
                    dma("sync", wbv(2048 + 1024 * half, 1024),
                        vag[2048 * half + 128 * h:2048 * half + 128 * (h + 1), :],
                        writes=wbnames(2048 + 1024 * half, 1024),
                        extra=res.rd(vname))
                dma("sync", wbv(10240, 1024), qdr[128 * h:128 * (h + 1), :],
                    writes=wbnames(10240, 1024), extra=res.rd(qname))
                items = [mmf(ps_a[:, 128 * j:128 * (j + 1)],
                             wbv(2048 + 128 * j, 128), IDT[:], True, True)
                         for j in range(8)]
                mm_group(items, reads=wbnames(2048, 2048) + ("idt",),
                         writes=("ps_a",))
                items = [mmf(ps_b[:, 128 * j:128 * (j + 1)],
                             wbv(2048 + 1024 + 128 * j, 128), IDT[:],
                             True, True) for j in range(8)]
                mm_group(items, reads=wbnames(2048, 2048) + ("idt",),
                         writes=("ps_b",))
                op("vector", lambda e: e.tensor_copy(wbv(4096, 1024),
                                                     ps_a[:]),
                   reads=("ps_a",), writes=wbnames(4096, 1024))
                op("vector", lambda e: e.tensor_copy(wbv(4096 + 1024, 1024),
                                                     ps_b[:]),
                   reads=("ps_b",), writes=wbnames(4096 + 1024, 1024))
                first_def = 1 if layer == 0 else 0
                for qt in range(8):
                    softmax_stage(qt)
                    if qt == 0 and first_def == 1:
                        transpose_stage(0)
                    elif qt > first_def:
                        transpose_stage(qt - 1)
                transpose_stage(7)
                if layer == 0:
                    for qt in range(8):
                        tl = tls[qt]
                        items = []
                        for s, t in enumerate(tl):
                            items.append(mmf(
                                ps_a[:, 128 * qt:128 * (qt + 1)],
                                wbv(4096 + 128 * t, 128),
                                AA[:, offs[qt] + 128 * s:
                                   offs[qt] + 128 * (s + 1)],
                                s == 0, s == len(tl) - 1))
                        mm_group(items,
                                 reads=aanames(offs[qt], 128 * len(tl))
                                 + wbnames(4096, 2048), writes=("ps_a",))
                else:
                    items = []
                    for j in range(16):
                        for n2 in range(2):
                            items.append(mmf(
                                ps_a[:, 512 * n2:512 * (n2 + 1)],
                                wbv(4096 + 128 * j, 128),
                                aav(1024 * j + 512 * n2, 512),
                                j == 0, j == 15))
                    mm_group(items,
                             reads=aanames(0, 16384) + wbnames(4096, 2048),
                             writes=("ps_a",))
                op("scalar", (lambda hh_: lambda e: e.copy(
                    aav(16384 + 1024 * hh_, 1024), ps_a[:]))(hh),
                    reads=("ps_a",), writes=aanames(16384 + 1024 * hh, 1024))
            for m in range(8):
                items = []
                for hh in range(4):
                    for n4 in range(4):
                        items.append(mmf(
                            ps_big[:, 512 * n4:512 * (n4 + 1)],
                            aav(16384 + 1024 * hh + 128 * m, 128),
                            aav(20480 + 2048 * hh + 512 * n4, 512),
                            hh == 0, hh == 3))
                mm_group(items, reads=aanames(16384, 4096) + aanames(20480, 8192),
                         writes=("ps_big", "psbg0", "psbg1"))
                op("vector", (lambda mi: lambda e: e.tensor_add(
                    x_sb[:, 2048 * mi:2048 * (mi + 1)],
                    x_sb[:, 2048 * mi:2048 * (mi + 1)], ps_big[:]))(m),
                    reads=("ps_big", "x"), writes=("x",))

    emit_attn(0)

    emit_ln(1)
    for _ in emit_proj(1):
        pass

    def emit_ffn1():
        for qd in range(4):
            dls = list(range(11 * qd, 11 * qd + 11))
            for idx, dl in enumerate(dls):
                r, off = (128 * dl) // 768, (128 * dl) % 768
                dma("sync", wbv(0, 2048).rearrange("p (f c) -> p f c", c=128),
                    w1T_ag[2048 * r:2048 * (r + 1), off:off + 128]
                    .rearrange("(f p) c -> p f c", p=128),
                    writes=wbnames(0, 2048), extra=res.rd("w1T_ag"))
                dma("sync", wbv(2048, 2048).rearrange("p (f c) -> p f c", c=128),
                    w2T_ag[2048 * r:2048 * (r + 1), off:off + 128]
                    .rearrange("(f p) c -> p f c", p=128),
                    writes=wbnames(2048, 2048), extra=res.rd("w2T_ag"))
                items = []
                for n2 in range(2):
                    for ft in range(16):
                        items.append(mmf(
                            ps_a[:, 512 * n2:512 * (n2 + 1)],
                            wbv(128 * ft, 128),
                            aav(1024 * ft + 512 * n2, 512),
                            ft == 0, ft == 15))
                mm_group(items, reads=wbnames(0, 2048) + aanames(0, 16384),
                         writes=("ps_a",))
                items = []
                for n2 in range(2):
                    for ft in range(16):
                        items.append(mmf(
                            ps_b[:, 512 * n2:512 * (n2 + 1)],
                            wbv(2048 + 128 * ft, 128),
                            aav(1024 * ft + 512 * n2, 512),
                            ft == 0, ft == 15))
                mm_group(items, reads=wbnames(2048, 2048) + aanames(0, 16384),
                         writes=("ps_b",))
                op("scalar", lambda e: e.activation(
                    wbv(4096, 1024), ps_a[:], AF.Silu),
                    reads=("ps_a",), writes=wbnames(4096, 1024))
                hc = 16384 + 1024 * idx
                op("vector", (lambda hc_: lambda e: e.tensor_tensor(
                    aav(hc_, 1024), wbv(4096, 1024), ps_b[:], ALU.mult))(hc),
                    reads=wbnames(4096, 1024) + ("ps_b",),
                    writes=aanames(hc, 1024))
            for idx, dl in enumerate(dls):
                wc = 27648 + 2048 * idx
                dma("sync", aav(wc, 2048).rearrange("p (r c) -> p r c", c=256),
                    w3T_ag.rearrange("(r d) c -> d r c", d=5632)
                    [128 * dl:128 * (dl + 1)],
                    writes=aanames(wc, 2048), extra=res.rd("w3T_ag"))
            for m in range(8):
                items = []
                for idx in range(11):
                    for n4 in range(4):
                        items.append(mmf(
                            ps_big[:, 512 * n4:512 * (n4 + 1)],
                            aav(16384 + 1024 * idx + 128 * m, 128),
                            aav(27648 + 2048 * idx + 512 * n4, 512),
                            idx == 0, idx == 10))
                mm_group(items,
                         reads=aanames(16384, 11264) + aanames(27648, 22528),
                         writes=("ps_big",))
                op("vector", (lambda mi: lambda e: e.tensor_add(
                    x_sb[:, 2048 * mi:2048 * (mi + 1)],
                    x_sb[:, 2048 * mi:2048 * (mi + 1)], ps_big[:]))(m),
                    reads=("ps_big", "x"), writes=("x",))

    emit_ffn1()
    emit_attn(1)

    fin = dma("sync", out_d.rearrange("(m p) f -> p m f", p=128),
              x_sb[:].rearrange("p (m f) -> p m f", f=2048), reads=("x",),
              writes=("outd",))
    p.emit("sync", lambda e: None, waits=[("d_outd", fin)])

    with nc.Block() as block:
        def mk(engname):
            recs = p.recs[engname]

            def body(e, recs=recs):
                for fn, waits, inc, n in recs:
                    for s, v in waits:
                        e.wait_ge(sems[s], v)
                    instr = fn(e)
                    if inc is not None:
                        instr.then_inc(sems[inc], n)
            return body

        block.sync(mk("sync"))
        block.gpsimd(mk("gpsimd"))
        block.tensor(mk("tensor"))
        block.vector(mk("vector"))
        block.scalar(mk("scalar"))

    ctx.close()
    return nc


def _blk_idx(r):
    return np.concatenate([np.arange(128 * (2 * i + r), 128 * (2 * i + r) + 128)
                           for i in range(8)])


def _mask_for_core(r):
    out = np.zeros((8, 128, 2048), np.float32)
    for i in range(8):
        tl = TLS[i]
        q = (128 * (2 * i + r) + np.arange(128))[:, None]
        k = np.concatenate([128 * (2 * (t % 8) + t // 8) + np.arange(128)
                            for t in tl])[None, :]
        allow = (np.abs(q - k) < 256) | (q < 16) | (k < 16)
        out[i, :, 0:128 * len(tl)] = np.where(allow, np.float32(0.0),
                                              np.float32(NEG))
    return out.astype(ml_dtypes.bfloat16)


_NC_CACHE = [None]


def kernel(L0_attn_hidden_states, L0_attn_Wq, L0_attn_Wk, L0_attn_Wv,
           L0_attn_Wo, L0_ffn_w_gate, L0_ffn_w_up, L0_ffn_w_down,
           L0_ffn_ln_weight, L1_attn_w_qkv, L1_attn_b_qkv, L1_attn_w_out,
           L1_attn_b_out, L1_ffn_weight1, L1_ffn_weight2, L1_ffn_weight3):
    xs = np.asarray(L0_attn_hidden_states)
    f32 = np.float32
    bq = np.ascontiguousarray(np.asarray(L1_attn_b_qkv, f32)
                              .reshape(48, 128).T)
    bout = np.broadcast_to(np.asarray(L1_attn_b_out).astype(ml_dtypes.bfloat16),
                           (128, 2048)).copy()
    lnw = np.broadcast_to(np.asarray(L0_ffn_ln_weight, f32), (128, 8192)).copy()
    ident = np.eye(128, dtype=ml_dtypes.bfloat16)
    nf4c = np.zeros((32,), f32)
    nf4c[0:15] = NF4_MID
    nf4c[16:31] = NF4_DELTA
    nf4c = np.broadcast_to(nf4c, (128, 32)).copy()
    w1p = np.zeros((6144, 2048), ml_dtypes.bfloat16)
    w1p[0:5632] = np.asarray(L1_ffn_weight1)
    w2p = np.zeros((6144, 2048), ml_dtypes.bfloat16)
    w2p[0:5632] = np.asarray(L1_ffn_weight2)
    w3 = np.asarray(L1_ffn_weight3)
    wg = np.asarray(L0_ffn_w_gate, f32)
    wu = np.asarray(L0_ffn_w_up, f32)
    wd = np.asarray(L0_ffn_w_down, f32)

    bidx = [_blk_idx(0), _blk_idx(1)]
    in_maps = []
    for c in range(8):
        b, hf = c // 2, c % 2
        in_maps.append({
            "x_in": np.ascontiguousarray(xs[b][bidx[hf]], f32),
            "wq": np.asarray(L0_attn_Wq, f32),
            "wk": np.asarray(L0_attn_Wk, f32),
            "wv": np.asarray(L0_attn_Wv, f32),
            "wo": np.asarray(L0_attn_Wo, f32),
            "wqkv": np.asarray(L1_attn_w_qkv, f32),
            "wout": np.asarray(L1_attn_w_out, f32),
            "bqkvT": bq,
            "bout_bc": bout,
            "wg_sh": np.ascontiguousarray(wg[1024 * c:1024 * (c + 1), :]),
            "wu_sh": np.ascontiguousarray(wu[1024 * c:1024 * (c + 1), :]),
            "wd_sh": np.ascontiguousarray(wd[:, 1024 * c:1024 * (c + 1)]),
            "lnw_bc": lnw,
            "w1_sh": np.ascontiguousarray(w1p[768 * c:768 * (c + 1), :]),
            "w2_sh": np.ascontiguousarray(w2p[768 * c:768 * (c + 1), :]),
            "w3_sh": np.ascontiguousarray(w3[256 * c:256 * (c + 1), :]),
            "mask_in": _mask_for_core(hf),
            "ident_in": ident,
            "nf4c_in": nf4c,
        })
    if _NC_CACHE[0] is None:
        _NC_CACHE[0] = build_program()
    nc = _NC_CACHE[0]
    r = run_bass_kernel_spmd(nc, in_maps, core_ids=list(range(8)))
    out = np.empty((4, 2048, 2048), np.float32)
    for c in range(8):
        b, hf = c // 2, c % 2
        out[b][bidx[hf]] = r.results[c]["out"]
    return out


if __name__ == "__main__":
    nc = build_program()
    print("build ok")
